# version 09
# baseline (speedup 1.0000x reference)
import contextlib
import numpy as np
import ml_dtypes
import concourse.bass as bass
import concourse.mybir as mybir
from concourse.bass_utils import run_bass_kernel_spmd

F32 = mybir.dt.float32
BF16 = mybir.dt.bfloat16
AF = mybir.ActivationFunctionType
ALU = mybir.AluOpType
AXL = mybir.AxisListType

B, N, D = 16, 4096, 1024
H, DH, M = 16, 64, 256
NCORES = 8
BPC = B // NCORES
TIL = 512
NT = N // TIL
NCH = TIL // 128
OC = D // 128

NORMZ = float(DH ** -0.25)
LN16 = float(np.log(16.0))
EPSQ = float((M ** -0.5) * 1e-3)
LNEPSQ = float(np.log(EPSQ))
EPS_LN = 1e-5
INV_D = 1.0 / D


class Sem:
    def __init__(self, h):
        self.h = h
        self.n = 0


class Builder:
    def __init__(self, debug=False, phases="full", stop=99):
        self.debug = debug
        self.phases = phases
        self.stop = stop
        self.nc = bass.Bass("TRN2", target_bir_lowering=False)
        self.q = {k: [] for k in "tvsgy"}
        self._can_inc = {k: False for k in "tvsgy"}
        self.ctx = contextlib.ExitStack()
        self.dt = {}
        self.build()

    def _ins(self, eng, fn):
        self.q[eng].append(fn)
        self._can_inc[eng] = True

    def t(self, fn): self._ins("t", fn)
    def v(self, fn): self._ins("v", fn)
    def s(self, fn): self._ins("s", fn)
    def g(self, fn): self._ins("g", fn)
    def y(self, fn): self._ins("y", fn)

    def wait(self, eng, sem, val):
        if val > sem.n:
            raise RuntimeError(f"wait {val} > scheduled {sem.n}")
        if val > 0:
            self.q[eng].append(lambda e, s=sem.h, v=val: e.wait_ge(s, v))
            self._can_inc[eng] = False

    def inc(self, eng, sem, k=1):
        if not self._can_inc[eng]:
            raise RuntimeError("inc must ride on a fresh datapath instruction")
        fn = self.q[eng].pop()
        self.q[eng].append(lambda e, f=fn, s=sem.h, k=k: f(e).then_inc(s, k))
        self._can_inc[eng] = False
        sem.n += k
        return sem.n

    def emit(self):
        with self.nc.Block() as blk:
            for key, attr in (("y", "sync"), ("g", "gpsimd"), ("s", "scalar"),
                              ("v", "vector"), ("t", "tensor")):
                fns = self.q[key]
                if fns:
                    getattr(blk, attr)(lambda e, fns=fns: [f(e) for f in fns])

    def build(self):
        nc, ctx = self.nc, self.ctx
        sb = lambda name, shape, dtype: ctx.enter_context(nc.sbuf_tensor(name, shape, dtype))
        psr = lambda name, shape: ctx.enter_context(nc.psum_tensor(name, shape, F32))
        mks = lambda name: Sem(ctx.enter_context(nc.semaphore(name)))

        def din(name, shape):
            self.dt[name] = nc.dram_tensor(name, shape, F32, kind="ExternalInput")
        din("x", [BPC, N, D])
        for nm in ("wqT", "wkT", "wvT", "woT", "wT"):
            din(nm, [D, D])
        din("vrT", [32, D]); din("cT", [32, D])
        din("bqkv", [128, 3 * OC])
        din("bo_rep", [128, D]); din("b_rep", [128, D]); din("bv_rep", [128, D])
        din("projz", [128, 512]); din("projzq", [128, 512])
        din("identf", [128, 128])
        din("ediag", [128, 128]); din("ones", [128, 128])
        din("identb", [128, 128]); din("cvals", [128, 3])
        self.out_h = nc.dram_tensor("out", [BPC, N, D], F32, kind="ExternalOutput")
        self.ccin = nc.dram_tensor("ccin", [1, 16], F32)
        self.ccout = nc.dram_tensor("ccout", [1, 16], F32)
        if self.debug:
            self.tap_h = {nm: nc.dram_tensor(nm, shape, F32, kind="ExternalOutput")
                          for nm, shape in [
                              ("tap_c1e0", [128, H * 66]), ("tap_c1e1", [128, H * 66]),
                              ("tap_vs0", [1, D]), ("tap_stab", [1, 16]),
                              ("tap_prj", [128, OC * TIL]),
                              ("tap_k2rep", [128, H]), ("tap_v12", [32, D]),
                              ("tap_eps", [128, 1])]}

        self.wb0 = sb("wb0", [128, OC * D], BF16)
        self.wb1 = sb("wb1", [128, OC * D], BF16)
        self.wb2 = sb("wb2", [128, OC * D], BF16)
        self.w_k = self.w_q = self.wb0
        self.w_v = self.w_o = self.wb1
        self.w_p = self.wb2
        self.projz = sb("projzS", [128, 512], BF16)
        self.projzq = sb("projzqS", [128, 512], BF16)
        self.identf = sb("identfS", [128, 128], F32)
        self.ediag = sb("ediagS", [128, 128], BF16)
        self.onesb = sb("onesbS", [128, 128], BF16)
        self.bqkv = sb("bqkvS", [128, 3 * OC], F32)
        self.cvalsb = sb("cvalsbS", [128, 3], F32)
        nc.const_aps.register(F32, EPS_LN, self.cvalsb[:, 0:1])
        nc.const_aps.register(F32, -LN16, self.cvalsb[:, 1:2])
        nc.const_aps.register(F32, LNEPSQ, self.cvalsb[:, 2:3])
        self.bo_rep = sb("bo_repS", [128, D], F32)
        self.b_rep = sb("b_repS", [128, D], F32)
        self.bv_rep = sb("bv_repS", [128, D], F32)
        self.onesf = sb("onesfS", [1, 128], F32)
        self.vr_sb = sb("vr_sb", [32, D], BF16)
        self.c_sb = sb("c_sb", [32, D], BF16)

        self.c1a = sb("c1a", [128, 32 * 65], F32)
        self.c1e = [[sb(f"c1e_{s}_{mc}", [128, H * 66], BF16) for mc in range(2)]
                    for s in range(BPC)]
        self.vs_row = [sb(f"vs_{s}", [1, D], F32) for s in range(BPC)]
        self.vst_fm = [sb(f"vst_{s}", [128, OC], F32) for s in range(BPC)]
        self.usb = [sb(f"usb_{s}", [128, 256], BF16) for s in range(BPC)]
        self.v12t = [sb(f"v12t_{s}", [32, D], BF16) for s in range(BPC)]
        self.k2rep = [sb(f"k2rep_{s}", [128, H], F32) for s in range(BPC)]
        self.stabv = sb("stabvS", [1, 16], F32)
        self.stab_in = sb("stab_inS", [1, 16], F32)
        self.stab_rep = sb("stab_repS", [128, 16], F32)
        self.epsp = sb("epspS", [128, 1], F32)
        self.nep = sb("nepS", [128, 1], F32)
        self.epnk = sb("epnkS", [128, 1], F32)
        self.evst = sb("evstS", [128, OC], F32)
        self.evs256 = sb("evs256S", [128, OC], F32)
        self.k1_fm = sb("k1_fmS", [128, OC], F32)
        self.cs_sb = sb("cs_sbS", [1, 1024], BF16)
        self.csfm2 = [sb(f"csfmS{i}", [128, OC], F32) for i in range(BPC)]
        self.csb_w = 0
        self.cs_marks = {}
        self.k2s_marks = {}
        self.stabsc = sb("stabscS", [1, 2], F32)

        self.x_tm = [sb(f"x_tm{i}", [128, NCH * D], F32) for i in range(2)]
        self.xln_tm = [sb(f"xln_tm{i}", [128, D], F32) for i in range(2)]
        self.xln_fm0 = sb("xln_fm0", [128, OC * TIL], BF16)
        self.prj_fm = sb("prj_fmS", [128, OC * TIL], BF16)
        self.k2b = [sb(f"k2b{i}", [128, TIL], BF16) for i in range(2)]
        self.e1b0 = [sb(f"e1b0_{i}", [128, TIL], BF16) for i in range(4)]
        self.vxb = sb("vxbS", [128, NCH * H * 65], BF16)
        self.edrep0 = sb("edrep0", [128, H * 65], BF16)
        self.ed_tm = sb("ed_tmS", [128, NCH * 16], F32)
        self.ed_rows = sb("ed_rowsS", [16, TIL], F32)
        self.dmaxb = sb("dmaxbS", [16, NT], F32)
        self.dmax1 = sb("dmax1S", [16, 1], F32)
        self.stat = [sb(f"stat{i}", [128, 7 * 16], F32) for i in range(2)]
        self.sqscr = sb("sqscrS", [128, D], BF16)
        self.junk = sb("junkS", [128, 4], F32)
        c1a_addr = nc.lookup_mloc(self.c1a).addr
        self.gq = [nc.alloc_sbuf_tensor_at(f"gq{i}", [128, 2 * TIL], BF16,
                                           offset=c1a_addr + i * 2048)
                   for i in range(4)]
        self.qd3t = sb("qd3S", [128, NCH * 32], F32)
        self.qsden = sb("qsdenS", [128, H * 8], F32)
        self.tden = sb("tdenS", [128, H * 4], F32)
        self.attn_tm = sb("attn_tmS", [128, NCH * D], BF16)
        at_addr = nc.lookup_mloc(self.attn_tm).addr
        self.e1b = [self.e1b0,
                    [nc.alloc_sbuf_tensor_at(f"e1b1_{i}", [128, TIL], BF16,
                                             offset=at_addr + i * 1024)
                     for i in range(4)]]
        self.identb = sb("identbS", [128, 128], BF16)
        self.attn_fm0 = sb("attn_fm0", [128, OC * 128], BF16)
        self.qd_fm = sb("qd_fmS", [32, NCH * 128], BF16)
        self.lnin0 = sb("lnin0", [128, D], F32)
        self.lnout0 = sb("lnout0", [128, D], F32)
        self.xln2_tm = [sb(f"xln2_tm{i}", [128, D], F32) for i in range(2)]
        self.xln2_fm0 = sb("xln2_fm0", [128, OC * 128], BF16)
        self.wtstage = self.xln2_tm

        self.pool = [psr(f"poolP{i}", [128, 512]) for i in range(3)]
        self.gen = [psr(f"genP{i}", [128, 512]) for i in range(3)]
        self.pbf = ctx.enter_context(nc.psum_tensor("pbfP", [128, 1024], BF16))

        self.s_x = [mks("s_x0"), mks("s_x1")]
        self.s_cst = mks("s_cst"); self.s_wk = mks("s_wk"); self.s_wv = mks("s_wv")
        self.s_vrc = mks("s_vrc"); self.s_bo = mks("s_bo"); self.s_br = mks("s_br")
        self.s_wq2 = mks("s_wq2"); self.s_wo2 = mks("s_wo2")
        self.s_wt = [mks("s_wt0"), mks("s_wt1")]
        self.s_o = [mks("s_o0"), mks("s_o1")]
        self.s_pe = mks("s_pe"); self.s_sc = mks("s_sc"); self.s_ve = mks("s_ve")
        self.s_gp = mks("s_gp"); self.s_cc = mks("s_cc"); self.s_tap = mks("s_tap")
        self.s_ccr = mks("s_ccr")
        self.pool_mark = [None, None, None]
        self.pbf_mark = None
        self._rr = 0

        self.vx3 = self.vxb[:, :].rearrange("p (c h j) -> p c h j", c=NCH, j=65)
        self.bv3 = self.bv_rep[:, :].rearrange("p (h j) -> p h j", j=64)
        self.qd3 = self.qd3t[:, :].rearrange("p (c k) -> p c k", k=32)

        self.xpref = {}
        self.x_rel = [None, None]
        self.xlnT_mark = [0, 0]
        self.xfm_war = 0
        self.prj_war = 0
        self.pdiag_war = 0
        self.edr_war = 0
        self.vx_war = 0
        self.k2_war = 0
        self.c1mini_free = [0, 0]
        self.c1aw = [0] * 8
        self.vs_war = [0, 0]
        self.afm_war = 0
        self.attn_war = 0
        self.x2f_war = 0
        self.lnin_war = None
        self.lnout_war = 0
        self.np_war = [0, 0, 0]
        self.last_c1pair = {}
        self.ln_mark = [dict(), dict()]

        self.schedule()

    def pool_next(self):
        b = self._rr % 3
        self._rr += 1
        return b

    def pool_acquire(self, bank):
        m = self.pool_mark[bank]
        if m is not None:
            self.wait("t", m[0], m[1])

    def pool_release(self, bank, sem, val):
        self.pool_mark[bank] = (sem, val)

    def emit_ln_chunk(self, tb, slot, src_ap, dst_ap, waits_v=(), waits_s=(),
                      apply_eng="v"):
        p = self
        st = self.stat[tb]
        base = slot * 7
        cS, cSS, cM, cMSQ, cVAR, cLNV, cRS = range(base, base + 7)
        war = self.ln_mark[tb].get(slot, 0)
        for sem, val in waits_v:
            p.wait("v", sem, val)
        if war:
            p.wait("v", p.s_sc, war)
        p.v(lambda e, a=src_ap: e.reduce_sum(st[:, cS:cS + 1], a, axis=AXL.X))
        red = p.inc("v", p.s_ve, 1)
        for sem, val in waits_s:
            p.wait("s", sem, val)
        p.s(lambda e, a=src_ap: e.activation(out=self.sqscr[:, :], in_=a,
                                             func=AF.Square, accum_out=st[:, cSS:cSS + 1]))
        ssv = p.inc("s", p.s_sc, 1)
        p.wait("s", p.s_ve, red)
        p.s(lambda e: e.activation(out=st[:, cM:cM + 1], in_=st[:, cS:cS + 1],
                                   func=AF.Identity, scale=INV_D))
        mval = p.inc("s", p.s_sc, 1)
        p.wait("s", p.s_sc, mval)
        p.s(lambda e: e.activation(out=st[:, cMSQ:cMSQ + 1], in_=st[:, cM:cM + 1],
                                   func=AF.Square))
        msq = p.inc("s", p.s_sc, 1)
        p.wait("v", p.s_sc, msq)
        p.v(lambda e: e.scalar_tensor_tensor(
            out=st[:, cVAR:cVAR + 1], in0=st[:, cSS:cSS + 1], scalar=INV_D,
            in1=st[:, cMSQ:cMSQ + 1], op0=ALU.mult, op1=ALU.subtract))
        var = p.inc("v", p.s_ve, 1)
        p.wait("s", p.s_ve, var)
        p.s(lambda e: e.activation(out=st[:, cLNV:cLNV + 1], in_=st[:, cVAR:cVAR + 1],
                                   func=AF.Ln, bias=EPS_LN))
        lnv = p.inc("s", p.s_sc, 1)
        p.wait("s", p.s_sc, lnv)
        p.s(lambda e: e.activation(out=st[:, cRS:cRS + 1], in_=st[:, cLNV:cLNV + 1],
                                   func=AF.Exp, scale=-0.5))
        rstd = p.inc("s", p.s_sc, 1)
        self.ln_mark[tb][slot] = rstd
        sem = {"v": p.s_ve, "g": p.s_gp}[apply_eng]
        p.wait(apply_eng, p.s_sc, rstd)
        for w in waits_v:
            if apply_eng != "v":
                p.wait(apply_eng, *w)
        p.q[apply_eng].append(lambda e, a=src_ap, o=dst_ap: e.tensor_scalar(
            out=o, in0=a, scalar1=st[:, cM:cM + 1], scalar2=st[:, cRS:cRS + 1],
            op0=ALU.subtract, op1=ALU.mult))
        p._can_inc[apply_eng] = True
        return (sem, p.inc(apply_eng, sem, 1))

    def emit_transposes(self, src_t, src_off, dst_fm, dst_base, dst_stride,
                        extra_t_waits=()):
        p = self
        last_sc = last_pe = None
        d3 = dst_fm[:, :].rearrange("p (o k) -> p o k", k=dst_stride)
        b0 = dst_base // dst_stride
        co = dst_base % dst_stride
        for g2 in range(2):
            b = p.pool_next()
            p.pool_acquire(b)
            for sem, val in extra_t_waits:
                p.wait("t", sem, val)
            for j in range(4):
                oc = g2 * 4 + j
                p.t(lambda e, b=b, j=j, off=src_off + oc * 128, s=src_t:
                    e.transpose(p.pool[b][:, j * 128:(j + 1) * 128],
                                s[:, off:off + 128], p.identf[:, :]))
            last_pe = p.inc("t", p.s_pe, 1)
            p.wait("s", p.s_pe, last_pe)
            oap = d3[:, b0 + g2 * 4: b0 + (g2 + 1) * 4, co:co + 128]
            iap = p.pool[b][:, :].rearrange("p (o k) -> p o k", k=128)
            p.s(lambda e, oap=oap, iap=iap: e.copy(oap, iap))
            last_sc = p.inc("s", p.s_sc, 1)
            p.pool_release(b, p.s_sc, last_sc)
        return last_sc, last_pe

    def emit_transposes_bf16(self, src_t, src_off, dst_fm, extra_t_waits=(),
                             extra_s_waits=()):
        p = self
        if self.pbf_mark is not None:
            p.wait("t", *self.pbf_mark)
        for sem, val in extra_t_waits:
            p.wait("t", sem, val)
        for oc in range(OC):
            p.t(lambda e, off=src_off + oc * 128, oc=oc, s=src_t:
                e.transpose(p.pbf[:, oc * 128:(oc + 1) * 128],
                            s[:, off:off + 128], p.identb[:, :]))
        pe = p.inc("t", p.s_pe, 1)
        p.wait("s", p.s_pe, pe)
        for sem, val in extra_s_waits:
            p.wait("s", sem, val)
        p.s(lambda e, d=dst_fm: e.copy(d[:, 0:OC * 128], p.pbf[:, :]))
        sc = p.inc("s", p.s_sc, 1)
        self.pbf_mark = (p.s_sc, sc)
        return sc, pe

    def sched_loads(self):
        p = self

        def gload(dst, name, sem, rearr=False):
            src = self.dt[name].ap()
            if rearr:
                src = src.rearrange("(c p) o -> p c o", p=128)
                dstap = dst[:, :].rearrange("p (c o) -> p c o", o=D)
            else:
                dstap = dst[:, :]
            p.g(lambda e, d=dstap, s=src: e.dma_start(out=d, in_=s))
            return (sem, p.inc("g", sem, 16))

        for dst, nm in ((self.identf, "identf"), (self.projz, "projz"),
                        (self.projzq, "projzq"),
                        (self.ediag, "ediag"), (self.onesb, "ones"),
                        (self.bqkv, "bqkv"), (self.bv_rep, "bv_rep"),
                        (self.identb, "identb"), (self.cvalsb, "cvals")):
            gload(dst, nm, p.s_cst)
        p.g(lambda e: e.dma_start(out=self.onesf[0:1, :], in_=self.dt["ones"].ap()[0:1, :]))
        p.inc("g", p.s_cst, 16)
        self.consts_done = (p.s_cst, p.s_cst.n)
        self.wk_done = gload(self.w_k, "wkT", p.s_wk, True)
        self.wv_done = gload(self.w_v, "wvT", p.s_wv, True)
        gload(self.vr_sb, "vrT", p.s_vrc)
        self.vrc_done = gload(self.c_sb, "cT", p.s_vrc)
        self.bo_done = gload(self.bo_rep, "bo_rep", p.s_bo)
        self.brep_done = gload(self.b_rep, "b_rep", p.s_br)

    def load_x_tile(self, s, t):
        p = self
        key = (s, t)
        if key in p.xpref:
            return p.xpref.pop(key)
        tb = t % 2
        if p.x_rel[tb] is not None:
            for sem, val in p.x_rel[tb]:
                p.wait("y", sem, val)
        src = p.dt["x"].ap()[s, t * TIL:(t + 1) * TIL, :].rearrange("(c p) f -> p c f", p=128)
        dst = p.x_tm[tb][:, :].rearrange("p (c f) -> p c f", f=D)
        p.y(lambda e, d=dst, s2=src: e.dma_start(out=d, in_=s2))
        mark = (p.s_x[tb], p.inc("y", p.s_x[tb], 16))
        return mark

    def prefetch_x(self, s, t):
        if (s, t) not in self.xpref:
            self.xpref[(s, t)] = self.load_x_tile(s, t)

    def prologue_tile(self, s, t, first=False):
        p = self
        tb = t % 2
        x_done = self.load_x_tile(s, t)
        xc = p.x_tm[tb][:, :].rearrange("p (c f) -> p c f", f=D)
        for c in range(NCH):
            waits_v = [x_done]
            if p.xlnT_mark[c % 2] > 0:
                waits_v.append((p.s_pe, p.xlnT_mark[c % 2]))
            ap_done = self.emit_ln_chunk(tb, c, xc[:, c, :], p.xln_tm[c % 2][:, :],
                                         waits_v=waits_v, waits_s=[x_done])
            tw = [ap_done]
            if first and c == 0:
                tw.append(self.consts_done)
            if self.xfm_war > 0:
                tw.append((p.s_pe, self.xfm_war))
            _, tpe = self.emit_transposes(p.xln_tm[c % 2], 0, p.xln_fm0,
                                          c * 128, TIL, extra_t_waits=tw)
            p.xlnT_mark[c % 2] = tpe
        p.x_rel[tb] = [(p.s_ve, p.s_ve.n), (p.s_sc, p.s_sc.n)]
        return p.s_sc.n

    def phase_a_tile(self, s, t, xfm_done, nxt):
        p = self
        tb = t % 2
        first = (s == 0 and t == 0)
        if self.stop <= 1:
            return None
        diag_mm = {}
        for oc in range(OC):
            b = p.pool_next()
            p.pool_acquire(b)
            p.wait("t", p.s_sc, xfm_done)
            if first and oc == 0:
                p.wait("t", *self.wk_done)
            for ic in range(OC):
                p.t(lambda e, b=b, ic=ic, oc=oc: e.matmul(
                    p.pool[b][:, :], p.w_k[:, ic * D + oc * 128: ic * D + (oc + 1) * 128],
                    p.xln_fm0[:, ic * TIL:(ic + 1) * TIL],
                    start=(ic == 0), stop=(ic == OC - 1)))
            rv = p.inc("t", p.s_pe, 1)
            p.wait("s", p.s_pe, rv)
            if oc == 0 and self.prj_war > 0:
                p.wait("s", p.s_pe, self.prj_war)
            p.s(lambda e, b=b, oc=oc: e.activation(
                out=p.prj_fm[:, oc * TIL:(oc + 1) * TIL], in_=p.pool[b][:, :],
                func=AF.Identity, bias=p.bqkv[:, OC + oc:OC + oc + 1]))
            p.inc("s", p.s_sc, 1)
            if oc >= 2:
                p.wait("s", p.s_pe, diag_mm[oc - 2])
            elif self.k2_war > 0:
                p.wait("s", p.s_pe, self.k2_war)
            p.s(lambda e, b=b, oc=oc: e.activation(
                out=p.k2b[oc % 2][:, :], in_=p.pool[b][:, :],
                func=AF.Square, bias=p.bqkv[:, OC + oc:OC + oc + 1]))
            k2d = p.inc("s", p.s_sc, 1)
            p.pool_release(b, p.s_sc, k2d)
            p.wait("t", p.s_sc, k2d)
            if oc == 0:
                if self.pdiag_war > 0:
                    p.wait("t", p.s_ve, self.pdiag_war)
                if self.edr_war > 0:
                    p.wait("t", p.s_sc, self.edr_war)
            p.t(lambda e, oc=oc: e.matmul(
                p.gen[2][0:16, :], p.ediag[:, oc * 16:(oc + 1) * 16], p.k2b[oc % 2][:, :],
                start=(oc == 0), stop=(oc == OC - 1)))
            diag_mm[oc] = p.inc("t", p.s_pe, 1)
        self.k2_war = diag_mm[6]
        kevac_done = p.s_sc.n
        if self.stop <= 2:
            return None
        for c in range(NCH):
            for half in range(2):
                b = p.pool_next()
                p.pool_acquire(b)
                if first and c == 0 and half == 0:
                    p.wait("t", *self.wv_done)
                for ic in range(OC):
                    p.t(lambda e, b=b, ic=ic, c=c, half=half: e.matmul(
                        p.pool[b][:, :],
                        p.xln_fm0[:, ic * TIL + c * 128: ic * TIL + (c + 1) * 128],
                        p.w_v[:, ic * D + half * 512: ic * D + (half + 1) * 512],
                        start=(ic == 0), stop=(ic == OC - 1)))
                rv = p.inc("t", p.s_pe, 1)
                p.wait("v", p.s_pe, rv)
                if c == 0 and half == 0:
                    if first:
                        p.wait("v", *self.consts_done)
                    if self.vx_war > 0:
                        p.wait("v", p.s_pe, self.vx_war)
                iap = p.pool[b][:, :].rearrange("p (h j) -> p h j", j=64)
                p.v(lambda e, iap=iap, c=c, half=half: e.tensor_tensor(
                    out=p.vx3[:, c, half * 8:(half + 1) * 8, 0:64], in0=iap,
                    in1=p.bv3[:, half * 8:(half + 1) * 8, :], op=ALU.add))
                vv = p.inc("v", p.s_ve, 1)
                p.pool_release(b, p.s_ve, vv)
        self.xfm_war = p.s_pe.n
        p.v(lambda e: e.memset(p.vx3[:, :, :, 64:65], 1.0))
        vx_done = p.inc("v", p.s_ve, 1)
        if self.stop <= 3:
            return None
        p.wait("s", p.s_pe, diag_mm[OC - 1])
        p.s(lambda e: e.activation(out=p.ed_rows[:, :], in_=p.gen[2][0:16, :], func=AF.Exp))
        edr_done = p.inc("s", p.s_sc, 1)
        self.edr_war = edr_done
        p.wait("v", p.s_pe, diag_mm[OC - 1])
        p.v(lambda e, t=t: e.reduce_max(p.dmaxb[0:16, t:t + 1], p.gen[2][0:16, :],
                                        axis=AXL.X))
        self.pdiag_war = p.inc("v", p.s_ve, 1)
        b = p.pool_next()
        p.pool_acquire(b)
        p.wait("t", p.s_sc, edr_done)
        for c in range(NCH):
            p.t(lambda e, b=b, c=c: e.transpose(
                p.pool[b][:, c * 16:(c + 1) * 16],
                p.ed_rows[0:16, c * 128:(c + 1) * 128], p.identf[0:16, 0:16]))
        rv = p.inc("t", p.s_pe, 1)
        p.wait("v", p.s_pe, rv)
        p.v(lambda e, b=b: e.tensor_copy(p.ed_tm[:, :], p.pool[b][:, 0:NCH * 16]))
        edt = p.inc("v", p.s_ve, 1)
        p.pool_release(b, p.s_ve, edt)
        for half in range(2):
            b = p.pool_next()
            p.pool_acquire(b)
            p.wait("t", p.s_ve, vx_done)
            for c in range(NCH):
                p.t(lambda e, b=b, c=c, half=half: e.matmul(
                    p.pool[b][0:1, :], p.onesb[:, 0:1],
                    p.vx3[:, c, half * 8:(half + 1) * 8, 0:64],
                    start=(c == 0), stop=(c == NCH - 1)))
            rv = p.inc("t", p.s_pe, 1)
            p.wait("v", p.s_pe, rv)
            if self.vs_war[half] > 0:
                p.wait("v", p.s_ve, self.vs_war[half])
            if t == 0:
                p.v(lambda e, b=b, half=half, s=s: e.tensor_copy(
                    p.vs_row[s][:, half * 512:(half + 1) * 512], p.pool[b][0:1, :]))
            else:
                p.v(lambda e, b=b, half=half, s=s: e.tensor_tensor(
                    out=p.vs_row[s][:, half * 512:(half + 1) * 512],
                    in0=p.vs_row[s][:, half * 512:(half + 1) * 512],
                    in1=p.pool[b][0:1, :], op=ALU.add))
            self.vs_war[half] = p.inc("v", p.s_ve, 1)
            p.pool_release(b, p.s_ve, self.vs_war[half])
        vs_mm_done = p.s_pe.n
        if self.stop <= 4:
            return None
        p.wait("v", p.s_pe, vs_mm_done)
        p.wait("v", p.s_ve, edt)
        mul_prev = 0
        for c in range(NCH):
            er = p.edrep0
            src = p.ed_tm[:, c * 16:(c + 1) * 16].broadcast_to([128, 16, 65])
            if mul_prev > 0:
                p.wait("v", p.s_ve, mul_prev)
            p.v(lambda e, er=er, src=src: e.tensor_copy(
                er[:, :].rearrange("p (h j) -> p h j", j=65), src))
            erv = p.inc("v", p.s_ve, 1)
            p.wait("v", p.s_ve, erv)
            p.v(lambda e, er=er, c=c: e.tensor_tensor(
                out=p.vx3[:, c, :, :], in0=p.vx3[:, c, :, :],
                in1=er[:, :].rearrange("p (h j) -> p h j", j=65), op=ALU.mult))
            mul_prev = p.inc("v", p.s_ve, 1)
        vp_done = p.s_ve.n
        if self.stop <= 5:
            return None
        nxt_mark = None
        if nxt is not None:
            nxt_mark = self.prologue_tile(nxt[0], nxt[1])
        c1grp = dict(self.last_c1pair)
        exp_done = {}

        def emit_c1(pr):
            mb = pr % 2
            p.wait("t", p.s_sc, exp_done[pr])
            if pr == 0:
                p.wait("t", p.s_ve, vp_done)
            if self.c1mini_free[mb] > 0:
                p.wait("t", p.s_ve, self.c1mini_free[mb])
            for hh in range(2):
                for mc in range(2):
                    h = 2 * pr + hh
                    soff = (hh * 2 + mc) * 65
                    for c in range(NCH):
                        p.t(lambda e, mb=mb, soff=soff, c=c, hh=hh, h=h, pr=pr:
                            e.matmul(p.gen[mb][:, soff:soff + 65],
                                     p.e1b[pr % 2][c][:, hh * 256 + (soff // 65 % 2) * 128: hh * 256 + (soff // 65 % 2) * 128 + 128],
                                     p.vx3[:, c, h, :],
                                     start=(c == 0), stop=(c == NCH - 1)))
            c1grp[pr] = p.inc("t", p.s_pe, 1)
            p.wait("v", p.s_pe, c1grp[pr])
            if self.c1aw[pr] > 0:
                p.wait("v", p.s_ve, self.c1aw[pr])
            if t == 0:
                p.v(lambda e, mb=mb, pr=pr: e.tensor_copy(
                    p.c1a[:, pr * 260:(pr + 1) * 260], p.gen[mb][:, 0:260]))
            else:
                p.v(lambda e, mb=mb, pr=pr: e.tensor_tensor(
                    out=p.c1a[:, pr * 260:(pr + 1) * 260],
                    in0=p.c1a[:, pr * 260:(pr + 1) * 260],
                    in1=p.gen[mb][:, 0:260], op=ALU.add))
            self.c1aw[pr] = p.inc("v", p.s_ve, 1)
            self.c1mini_free[mb] = self.c1aw[pr]

        for pr in range(8):
            for c in range(NCH):
                b = p.pool_next()
                p.pool_acquire(b)
                col = pr * TIL + c * 128
                p.wait("t", p.s_sc, kevac_done)
                p.t(lambda e, b=b, col=col: e.matmul(
                    p.pool[b][:, :], p.prj_fm[:, col:col + 128], p.projz[:, :]))
                rv = p.inc("t", p.s_pe, 1)
                eb = p.e1b[pr % 2][c]
                p.wait("s", p.s_pe, rv)
                war = c1grp.get(pr - 2)
                if war:
                    p.wait("s", p.s_pe, war)
                p.s(lambda e, b=b, eb=eb: e.activation(
                    out=eb[:, :], in_=p.pool[b][:, :], func=AF.Exp,
                    scale=NORMZ, bias=-LN16))
                ev = p.inc("s", p.s_sc, 1)
                p.pool_release(b, p.s_sc, ev)
            exp_done[pr] = p.s_sc.n
            if pr >= 1:
                emit_c1(pr - 1)
        emit_c1(7)
        self.vx_war = c1grp[7]
        self.last_c1pair = {-2: c1grp[6], -1: c1grp[7]}
        self.prj_war = c1grp[7]
        self.last_c1evac = max(self.c1aw)
        return nxt_mark

    def phase_a_seq(self, s, xfm_first=None):
        p = self
        xfm = xfm_first
        for t in range(NT):
            if xfm is None:
                xfm = self.prologue_tile(s, t, first=(s == 0 and t == 0))
            if t + 1 < NT:
                nxt = (s, t + 1, (s, t + 2) if t + 2 < NT else
                       ((s + 1, 0) if s + 1 < BPC else None))
            elif s + 1 < BPC:
                nxt = (s + 1, 0, (s + 1, 1))
            else:
                nxt = None
            xfm = self.phase_a_tile(s, t, xfm, nxt)
        p.wait("v", p.s_ve, self.last_c1evac)
        for mc in range(2):
            iap = p.c1a[:, :].rearrange("p (h j) -> p h j", j=130)[:, :, mc * 65:(mc + 1) * 65]
            oap = p.c1e[s][mc][:, :].rearrange("p (h j) -> p h j", j=66)[:, :, 0:65]
            p.v(lambda e, iap=iap, oap=oap: e.tensor_copy(oap, iap))
            p.inc("v", p.s_ve, 1)
            ap1 = p.c1e[s][mc][:, :].rearrange("p (h j) -> p h j", j=66)[:, :, 65:66]
            p.v(lambda e, ap1=ap1: e.memset(ap1, 1.0))
            p.inc("v", p.s_ve, 1)
        c1e_done = p.s_ve.n
        self.c1a_free = c1e_done
        self.c1aw = [0] * 8
        b = p.pool_next()
        p.pool_acquire(b)
        p.wait("t", p.s_ve, c1e_done)
        for oc in range(OC):
            p.t(lambda e, b=b, oc=oc, s=s: e.transpose(
                p.pool[b][:, oc:oc + 1], p.vs_row[s][0:1, oc * 128:(oc + 1) * 128],
                p.identf[0:1, 0:1]))
        rv = p.inc("t", p.s_pe, 1)
        p.wait("v", p.s_pe, rv)
        p.v(lambda e, b=b, s=s: e.tensor_copy(p.vst_fm[s][:, :], p.pool[b][:, 0:OC]))
        vst = p.inc("v", p.s_ve, 1)
        p.pool_release(b, p.s_ve, vst)
        self.vs_war = [0, 0]
        self.vst_done = vst
        p.v(lambda e: e.reduce_max(p.dmax1[0:16, 0:1], p.dmaxb[0:16, 0:NT], axis=AXL.X))
        dm = p.inc("v", p.s_ve, 1)
        b = p.pool_next()
        p.pool_acquire(b)
        p.wait("t", p.s_ve, dm)
        p.t(lambda e, b=b: e.transpose(p.pool[b][0:1, 0:16], p.dmax1[0:16, 0:1],
                                       p.identf[0:16, 0:16]))
        rv = p.inc("t", p.s_pe, 1)
        p.wait("v", p.s_pe, rv)
        p.v(lambda e, b=b, s=s: e.reduce_max(p.stabsc[0:1, s:s + 1],
                                             p.pool[b][0:1, 0:16], axis=AXL.X))
        sv = p.inc("v", p.s_ve, 1)
        p.pool_release(b, p.s_ve, sv)
        return xfm

    def sched_collective(self):
        p = self
        p.wait("v", p.s_ve, p.s_ve.n)
        p.v(lambda e: e.reduce_max(p.stabv[0:1, 0:1], p.stabsc[0:1, 0:2], axis=AXL.X))
        smx = p.inc("v", p.s_ve, 1)
        p.wait("v", p.s_ve, smx)
        src = p.stabv[0:1, 0:1].broadcast_to([1, 1, 15])
        dst = p.stabv[0:1, 1:16].rearrange("p (a j) -> p a j", a=1)
        p.v(lambda e, src=src, dst=dst: e.tensor_copy(dst, src))
        st_done = p.inc("v", p.s_ve, 1)
        p.wait("g", p.s_ve, st_done)
        p.g(lambda e: e.dma_start(out=p.ccin.ap(), in_=p.stabv[0:1, :]))
        p.inc("g", p.s_cc, 16)
        p.wait("g", p.s_cc, 16)
        p.g(lambda e: e.collective_compute(
            "AllReduce", ALU.max, replica_groups=[list(range(NCORES))],
            ins=[p.ccin.ap().opt()], outs=[p.ccout.ap().opt()]))
        p.inc("g", p.s_ccr, 1)
        p.wait("g", p.s_ccr, 1)
        p.g(lambda e: e.dma_start(out=p.stab_in[0:1, :], in_=p.ccout.ap()))
        p.inc("g", p.s_cc, 16)
        p.wait("g", p.s_cc, 32)
        b = p.pool_next()
        p.pool_acquire(b)
        p.wait("t", p.s_cc, 32)
        p.t(lambda e, b=b: e.matmul(p.pool[b][:, 0:16], p.onesf[0:1, :],
                                    p.stab_in[0:1, :]))
        rv = p.inc("t", p.s_pe, 1)
        p.wait("v", p.s_pe, rv)
        p.v(lambda e, b=b: e.tensor_copy(p.stab_rep[:, :], p.pool[b][:, 0:16]))
        bc = p.inc("v", p.s_ve, 1)
        p.pool_release(b, p.s_ve, bc)
        p.wait("s", p.s_ve, bc)
        p.s(lambda e: e.activation(out=p.epsp[:, :], in_=p.stab_rep[:, 0:1],
                                   func=AF.Exp, bias=LNEPSQ))
        ep = p.inc("s", p.s_sc, 1)
        p.wait("v", p.s_sc, ep)
        p.v(lambda e: e.tensor_scalar_mul(p.nep[:, :], p.epsp[:, :], float(N)))
        p.inc("v", p.s_ve, 1)
        p.v(lambda e: e.tensor_scalar_mul(p.epnk[:, :], p.epsp[:, :],
                                          float(256.0 * EPSQ * N)))
        self.eps_done = p.inc("v", p.s_ve, 1)

    def sched_wprime(self):
        p = self
        wst_free = {}
        for ic in range(OC):
            src = p.dt["wT"].ap()[ic * 128:(ic + 1) * 128, :]
            if ic >= 2:
                p.wait("y", p.s_ve, wst_free[ic - 2])
            p.y(lambda e, s2=src, ic=ic: e.dma_start(out=p.wtstage[ic % 2][:, :], in_=s2))
            wst = (p.s_wt[ic % 2], p.inc("y", p.s_wt[ic % 2], 16))
            for half in range(2):
                b = p.pool_next()
                p.pool_acquire(b)
                if ic == 0 and half == 0:
                    p.wait("t", *self.vrc_done)
                p.t(lambda e, b=b, ic=ic, half=half: e.matmul(
                    p.pool[b][:, :], p.vr_sb[0:32, ic * 128:(ic + 1) * 128],
                    p.c_sb[0:32, half * 512:(half + 1) * 512]))
                rv = p.inc("t", p.s_pe, 1)
                p.wait("v", p.s_pe, rv)
                p.wait("v", *wst)
                p.v(lambda e, b=b, ic=ic, half=half: e.tensor_tensor(
                    out=p.w_p[:, ic * D + half * 512: ic * D + (half + 1) * 512],
                    in0=p.pool[b][:, :],
                    in1=p.wtstage[ic % 2][:, half * 512:(half + 1) * 512],
                    op=ALU.add))
                wst_free[ic] = p.inc("v", p.s_ve, 1)
                p.pool_release(b, p.s_ve, wst_free[ic])
        self.wp_done = p.s_ve.n

    def sched_bsetup1(self, s):
        p = self
        pse = p.gen[2]
        if s == 0:
            if self.pdiag_war > 0:
                p.wait("t", p.s_ve, self.pdiag_war)
            if self.edr_war > 0:
                p.wait("t", p.s_sc, self.edr_war)
        csb = self.cs_sb
        for grp in range(2):
            for hh in range(8):
                h = grp * 8 + hh
                for mc in range(2):
                    p.t(lambda e, h=h, hh=hh, mc=mc, s=s: e.matmul(
                        pse[0:1, hh * 64:(hh + 1) * 64], p.onesb[:, 0:1],
                        p.c1e[s][mc][:, h * 66:h * 66 + 64],
                        start=(mc == 0), stop=(mc == 1)))
            gv = p.inc("t", p.s_pe, 1)
            p.wait("v", p.s_pe, gv)
            if grp == 1:
                p.wait("v", p.s_ve, self.csb_w)
            p.v(lambda e, grp=grp: e.tensor_copy(csb[0:1, grp * 512:(grp + 1) * 512],
                                                 pse[0:1, 0:512]))
            self.csb_w = p.inc("v", p.s_ve, 1)
            if grp == 0:
                p.wait("t", p.s_ve, self.csb_w)
        p.wait("t", p.s_ve, self.csb_w)
        if self.pbf_mark is not None:
            p.wait("t", *self.pbf_mark)
        for h in range(H):
            p.t(lambda e, h=h: e.transpose(
                p.pbf[(h % 2) * 64:(h % 2) * 64 + 64, 2 * (h // 2):2 * (h // 2) + 1],
                csb[0:1, h * 64:(h + 1) * 64], p.identb[0:1, 0:1]))
        cst = p.inc("t", p.s_pe, 1)
        p.wait("v", p.s_pe, cst)
        csrc = p.pbf[:, 0:2 * OC].rearrange("p (c k) -> p c k", k=2)[:, :, 0]
        p.v(lambda e, s=s, csrc=csrc: e.tensor_copy(p.csfm2[s][:, :], csrc))
        csf = p.inc("v", p.s_ve, 1)
        self.pbf_mark = (p.s_ve, csf)
        self.cs_marks[s] = csf
        for mc in range(2):
            gsap = p.c1e[s][mc][:, :].rearrange("p (h j) -> p h j", j=66)[:, :, 64]
            p.t(lambda e, gsap=gsap, mc=mc, s=s: e.matmul(
                pse[:, 16 + s * 16:32 + s * 16], p.onesb[:, :], gsap,
                start=(mc == 0), stop=(mc == 1)))
        self.k2s_marks[s] = p.inc("t", p.s_pe, 1)

    def sched_bsetup2(self, s):
        p = self
        pse = p.gen[2]
        p.wait("v", p.s_ve, self.eps_done)
        p.v(lambda e, s=s: e.tensor_scalar_mul(p.evst[:, :], p.vst_fm[s][:, :],
                                               p.epsp[:, 0:1]))
        p.inc("v", p.s_ve, 1)
        p.v(lambda e, s=s: e.tensor_scalar(
            out=p.evs256[:, :], in0=p.vst_fm[s][:, :], scalar1=p.epsp[:, 0:1],
            scalar2=float(256.0 * EPSQ), op0=ALU.mult, op1=ALU.mult))
        ev2 = p.inc("v", p.s_ve, 1)
        p.wait("v", p.s_ve, ev2)
        p.v(lambda e, s=s: e.scalar_tensor_tensor(
            out=p.k1_fm[:, :], in0=p.csfm2[s][:, :], scalar=EPSQ, in1=p.evs256[:, :],
            op0=ALU.mult, op1=ALU.add))
        k1f = p.inc("v", p.s_ve, 1)
        p.wait("v", p.s_pe, self.k2s_marks[s])
        p.v(lambda e, s=s: e.tensor_scalar(
            out=p.k2rep[s][:, :], in0=pse[:, 16 + s * 16:32 + s * 16], scalar1=EPSQ,
            scalar2=p.epnk[:, 0:1], op0=ALU.mult, op1=ALU.add))
        self.psetup_read = p.inc("v", p.s_ve, 1)
        p.v(lambda e, s=s: e.memset(p.usb[s][:, :], 0.0))
        um = p.inc("v", p.s_ve, 1)
        p.wait("v", p.s_ve, um)
        for c in range(OC):
            for colofs, srcT in ((2 * c, p.evst), (16 + 2 * c, p.k1_fm)):
                p.v(lambda e, c=c, colofs=colofs, srcT=srcT, s=s: e.tensor_copy(
                    p.usb[s][0:64, c * 32 + colofs:c * 32 + colofs + 1],
                    srcT[0:64, c:c + 1]))
                p.inc("v", p.s_ve, 1)
                p.v(lambda e, c=c, colofs=colofs, srcT=srcT, s=s: e.tensor_copy(
                    p.usb[s][64:128, c * 32 + colofs + 1:c * 32 + colofs + 2],
                    srcT[64:128, c:c + 1]))
                p.inc("v", p.s_ve, 1)
        u_done = p.s_ve.n
        for half in range(2):
            b = p.pool_next()
            p.pool_acquire(b)
            p.wait("t", p.s_ve, u_done)
            if s == 0 and half == 0:
                p.wait("t", *self.wo_done)
            for c in range(OC):
                p.t(lambda e, b=b, c=c, half=half, s=s: e.matmul(
                    p.pool[b][0:32, :], p.usb[s][:, c * 32:(c + 1) * 32],
                    p.w_o[:, c * D + half * 512: c * D + (half + 1) * 512],
                    start=(c == 0), stop=(c == OC - 1)))
            rv = p.inc("t", p.s_pe, 1)
            p.wait("v", p.s_pe, rv)
            p.v(lambda e, b=b, half=half, s=s: e.tensor_copy(
                p.v12t[s][0:32, half * 512:(half + 1) * 512], p.pool[b][0:32, :]))
            vv = p.inc("v", p.s_ve, 1)
            p.pool_release(b, p.s_ve, vv)
        self.bset_done = p.s_ve.n

    def _emit_head_tail(self, h, recv):
        p = self
        p.wait("v", p.s_ve, recv)
        qodd = p.qsden[:, h * 8:(h + 1) * 8].rearrange("p (n j) -> p n j", j=2)[:, :, 1]
        p.v(lambda e, h=h, qodd=qodd: e.tensor_tensor(
            out=p.qd3[:, :, h], in0=qodd, in1=p.qd3[:, :, 16 + h], op=ALU.mult))
        p.inc("v", p.s_ve, 1)
        npb = p.gen[h % 3]
        for ns in range(NCH):
            p.v(lambda e, h=h, ns=ns, npb=npb: e.tensor_scalar_mul(
                p.attn_tm[:, ns * D + h * 64: ns * D + (h + 1) * 64],
                npb[:, ns * 66:ns * 66 + 64],
                p.qd3t[:, ns * 32 + 16 + h: ns * 32 + 16 + h + 1]))
            p.inc("v", p.s_ve, 1)

    def pro_b(self, s, t, firstb=False):
        p = self
        xfm_done = self.prologue_tile(s, t)
        for oc in range(OC):
            b = p.pool_next()
            p.pool_acquire(b)
            p.wait("t", p.s_sc, xfm_done)
            if firstb and oc == 0:
                p.wait("t", *self.wq_done)
            for ic in range(OC):
                p.t(lambda e, b=b, ic=ic, oc=oc: e.matmul(
                    p.pool[b][:, :], p.w_q[:, ic * D + oc * 128: ic * D + (oc + 1) * 128],
                    p.xln_fm0[:, ic * TIL:(ic + 1) * TIL],
                    start=(ic == 0), stop=(ic == OC - 1)))
            rv = p.inc("t", p.s_pe, 1)
            p.wait("s", p.s_pe, rv)
            if oc == 0 and self.prj_war > 0:
                p.wait("s", p.s_pe, self.prj_war)
            p.s(lambda e, b=b, oc=oc: e.activation(
                out=p.prj_fm[:, oc * TIL:(oc + 1) * TIL], in_=p.pool[b][:, :],
                func=AF.Identity, bias=p.bqkv[:, oc:oc + 1]))
            qe = p.inc("s", p.s_sc, 1)
            p.pool_release(b, p.s_sc, qe)
        self.xfm_war = p.s_pe.n
        return p.s_sc.n

    def phase_b_tile(self, s, t, qevac_done, nxt):
        p = self
        tb = t % 2
        firstb = (s == 0 and t == 0)
        xc = p.x_tm[tb][:, :].rearrange("p (c f) -> p c f", f=D)
        h_done = {}
        np_free = {}
        rec_val = {}
        prev = None
        p.wait("v", p.s_sc, qevac_done)
        if self.attn_war > 0:
            p.wait("v", p.s_pe, self.attn_war)
        for pr in range(8):
            for hh in range(2):
                h = 2 * pr + hh
                r = h % 4
                gq3 = p.gq[r][:, :].rearrange("p (m n) -> p m n", n=TIL)
                for nh in range(2):
                    b = p.pool_next()
                    p.pool_acquire(b)
                    p.wait("t", p.s_sc, qevac_done)
                    for mc in range(2):
                        p.t(lambda e, b=b, mc=mc, hh=hh, pr=pr, nh=nh: e.matmul(
                            p.pool[b][:, mc * 256:(mc + 1) * 256],
                            p.projzq[:, hh * 256 + mc * 128: hh * 256 + (mc + 1) * 128],
                            p.prj_fm[:, pr * TIL + nh * 256: pr * TIL + (nh + 1) * 256],
                            start=True, stop=True))
                    rv = p.inc("t", p.s_pe, 1)
                    p.wait("s", p.s_pe, rv)
                    if h >= 4 and nh == 0:
                        p.wait("s", p.s_pe, h_done[h - 4])
                    elif firstb and h < 4 and nh == 0:
                        p.wait("s", p.s_ve, self.c1a_free)
                    oap = gq3[:, :, nh * 256:(nh + 1) * 256]
                    iap = p.pool[b][:, :].rearrange("p (m n) -> p m n", n=256)
                    p.s(lambda e, oap=oap, iap=iap: e.activation(
                        out=oap, in_=iap, func=AF.Exp, scale=NORMZ, bias=-LN16))
                    ev = p.inc("s", p.s_sc, 1)
                    p.pool_release(b, p.s_sc, ev)
                exp_done = p.s_sc.n
                npb = p.gen[h % 3]
                p.wait("t", p.s_sc, exp_done)
                if firstb and h == 0:
                    p.wait("t", p.s_ve, self.bset_done)
                if h >= 3:
                    p.wait("t", p.s_ve, np_free[h - 3])
                elif self.np_war[h % 3] > 0:
                    p.wait("t", p.s_ve, self.np_war[h % 3])
                for ns in range(NCH):
                    for mc in range(2):
                        p.t(lambda e, npb=npb, ns=ns, mc=mc, gq3=gq3, h=h, s=s: e.matmul(
                            npb[:, ns * 66:(ns + 1) * 66],
                            gq3[:, mc, ns * 128:(ns + 1) * 128],
                            p.c1e[s][mc][:, h * 66:(h + 1) * 66],
                            start=(mc == 0), stop=(mc == 1)))
                h_done[h] = p.inc("t", p.s_pe, 1)
                p.wait("v", p.s_pe, h_done[h])
                srcA = npb[:, 0:264].rearrange("p (n j) -> p n j", j=66)[:, :, 64:66]
                dstA = p.qsden[:, h * 8:(h + 1) * 8].rearrange("p (n j) -> p n j", j=2)
                p.v(lambda e, srcA=srcA, dstA=dstA: e.tensor_copy(dstA, srcA))
                gat = p.inc("v", p.s_ve, 1)
                p.wait("s", p.s_ve, gat)
                qodd = p.qsden[:, h * 8:(h + 1) * 8].rearrange("p (n j) -> p n j", j=2)[:, :, 1]
                p.s(lambda e, h=h, qodd=qodd, s=s: e.activation(
                    out=p.tden[:, h * 4:(h + 1) * 4], in_=qodd, func=AF.Identity,
                    scale=p.nep[:, 0:1], bias=p.k2rep[s][:, h:h + 1]))
                td = p.inc("s", p.s_sc, 1)
                p.wait("v", p.s_sc, td)
                qev = p.qsden[:, h * 8:(h + 1) * 8].rearrange("p (n j) -> p n j", j=2)[:, :, 0]
                p.v(lambda e, h=h, qev=qev: e.tensor_tensor(
                    out=p.tden[:, h * 4:(h + 1) * 4], in0=p.tden[:, h * 4:(h + 1) * 4],
                    in1=qev, op=ALU.add))
                tav = p.inc("v", p.s_ve, 1)
                p.wait("v", p.s_ve, tav)
                p.v(lambda e, h=h: e.reciprocal(p.qd3[:, :, 16 + h],
                                                p.tden[:, h * 4:(h + 1) * 4]))
                rec_val[h] = p.inc("v", p.s_ve, 1)
                if prev is not None:
                    self._emit_head_tail(prev, rec_val[prev])
                    np_free[prev] = p.s_ve.n
                prev = h
        self._emit_head_tail(prev, rec_val[prev])
        np_free[prev] = p.s_ve.n
        self.np_war = [np_free[15], np_free[13], np_free[14]]
        self.prj_war = h_done[15]
        comb_done = p.s_ve.n
        pro_next = None
        if nxt is not None:
            pro_next = self.pro_b(nxt[0], nxt[1])
        for ns in range(NCH):
            bqd = p.pool_next()
            p.pool_acquire(bqd)
            p.wait("t", p.s_ve, comb_done)
            p.t(lambda e, b=bqd, ns=ns: e.transpose(
                p.pool[b][0:32, 0:128], p.qd3t[:, ns * 32:(ns + 1) * 32],
                p.identf[:, :]))
            rv = p.inc("t", p.s_pe, 1)
            p.wait("v", p.s_pe, rv)
            p.v(lambda e, b=bqd, ns=ns: e.tensor_copy(
                p.qd_fm[0:32, ns * 128:(ns + 1) * 128], p.pool[b][0:32, 0:128]))
            qdf = p.inc("v", p.s_ve, 1)
            p.pool_release(bqd, p.s_ve, qdf)
            af = p.attn_fm0
            tw = [(p.s_ve, comb_done)]
            if self.afm_war > 0:
                tw.append((p.s_pe, self.afm_war))
            _, tpe = self.emit_transposes_bf16(p.attn_tm, ns * D, af,
                                               extra_t_waits=tw)
            self.attn_war = tpe
            afm_done = p.s_sc.n
            li = p.lnin0
            lo = p.lnout0
            for half in range(2):
                b = p.pool_next()
                p.pool_acquire(b)
                p.wait("t", p.s_sc, afm_done)
                if half == 0:
                    p.wait("t", p.s_ve, qdf)
                for ic in range(OC):
                    p.t(lambda e, b=b, ic=ic, half=half, af=af: e.matmul(
                        p.pool[b][:, :], af[:, ic * 128:(ic + 1) * 128],
                        p.w_o[:, ic * D + half * 512: ic * D + (half + 1) * 512],
                        start=(ic == 0), stop=False))
                p.t(lambda e, b=b, half=half, ns=ns, s=s: e.matmul(
                    p.pool[b][:, :], p.qd_fm[0:32, ns * 128:(ns + 1) * 128],
                    p.v12t[s][0:32, half * 512:(half + 1) * 512],
                    start=False, stop=True))
                rv = p.inc("t", p.s_pe, 1)
                p.wait("v", p.s_pe, rv)
                if half == 0 and self.lnin_war is not None:
                    p.wait("v", *self.lnin_war)
                if firstb and ns == 0 and half == 0:
                    p.wait("v", *self.bo_done)
                p.v(lambda e, b=b, half=half, li=li: e.tensor_tensor(
                    out=li[:, half * 512:(half + 1) * 512], in0=p.pool[b][:, :],
                    in1=p.bo_rep[:, half * 512:(half + 1) * 512], op=ALU.add))
                wv = p.inc("v", p.s_ve, 1)
                p.pool_release(b, p.s_ve, wv)
            wo_done = p.s_ve.n
            self.afm_war = p.s_pe.n
            ap1 = self.emit_ln_chunk(tb, 4 + ns, li[:, :], lo[:, :],
                                     waits_v=[(p.s_ve, wo_done),
                                              (p.s_gp, self.lnout_war)],
                                     waits_s=[(p.s_ve, wo_done)])
            self.lnin_war = ap1
            p.wait("g", *ap1)
            p.g(lambda e, ns=ns, lo=lo, tb=tb: e.tensor_tensor(
                out=p.x_tm[tb][:, ns * D:(ns + 1) * D],
                in0=p.x_tm[tb][:, ns * D:(ns + 1) * D], in1=lo[:, :], op=ALU.add))
            x1_done = p.inc("g", p.s_gp, 1)
            x2t = p.xln2_tm[ns % 2]
            ap2 = self.emit_ln_chunk(tb, 8 + ns, p.x_tm[tb][:, ns * D:(ns + 1) * D],
                                     x2t[:, :],
                                     waits_v=[(p.s_gp, x1_done)],
                                     waits_s=[(p.s_gp, x1_done)])
            x2f = p.xln2_fm0
            tw = [ap2]
            if self.x2f_war > 0:
                tw.append((p.s_pe, self.x2f_war))
            self.emit_transposes(x2t, 0, x2f, 0, 128, extra_t_waits=tw)
            x2f_done = p.s_sc.n
            for half in range(2):
                b = p.pool_next()
                p.pool_acquire(b)
                p.wait("t", p.s_sc, x2f_done)
                if firstb and ns == 0 and half == 0:
                    p.wait("t", p.s_ve, self.wp_done)
                for ic in range(OC):
                    p.t(lambda e, b=b, ic=ic, half=half, x2f=x2f: e.matmul(
                        p.pool[b][:, :], x2f[:, ic * 128:(ic + 1) * 128],
                        p.w_p[:, ic * D + half * 512: ic * D + (half + 1) * 512],
                        start=(ic == 0), stop=(ic == OC - 1)))
                rv = p.inc("t", p.s_pe, 1)
                p.wait("v", p.s_pe, rv)
                if half == 0 and self.lnin_war is not None:
                    p.wait("v", *self.lnin_war)
                if firstb and ns == 0 and half == 0:
                    p.wait("v", *self.brep_done)
                p.v(lambda e, b=b, half=half, li=li: e.tensor_tensor(
                    out=li[:, half * 512:(half + 1) * 512], in0=p.pool[b][:, :],
                    in1=p.b_rep[:, half * 512:(half + 1) * 512], op=ALU.add))
                fv = p.inc("v", p.s_ve, 1)
                p.pool_release(b, p.s_ve, fv)
            ffn_done = p.s_ve.n
            self.x2f_war = p.s_pe.n
            ap3 = self.emit_ln_chunk(tb, 12 + ns, li[:, :], lo[:, :],
                                     waits_v=[(p.s_ve, ffn_done),
                                              (p.s_gp, x1_done)],
                                     waits_s=[(p.s_ve, ffn_done)])
            self.lnin_war = ap3
            p.wait("g", *ap3)
            p.wait("g", p.s_gp, x1_done)
            p.g(lambda e, ns=ns, lo=lo, tb=tb: e.tensor_tensor(
                out=p.x_tm[tb][:, ns * D:(ns + 1) * D],
                in0=p.x_tm[tb][:, ns * D:(ns + 1) * D], in1=lo[:, :], op=ALU.add))
            self.lnout_war = p.inc("g", p.s_gp, 1)
        out_ready = p.s_gp.n
        p.wait("s", p.s_gp, out_ready)
        src = p.x_tm[tb][:, :].rearrange("p (c f) -> p c f", f=D)
        dst = p.out_h.ap()[s, t * TIL:(t + 1) * TIL, :].rearrange("(c p) f -> p c f", p=128)
        p.s(lambda e, d=dst, s2=src: e.dma_start(out=d, in_=s2))
        stv = p.inc("s", p.s_o[tb], 16)
        p.x_rel[tb] = [(p.s_o[tb], stv)]
        return pro_next

    def schedule(self):
        p = self
        self.sched_loads()
        p.wait("s", *self.consts_done)
        if self.phases == "a0":
            xfm = self.prologue_tile(0, 0, first=True)
            self.phase_a_tile(0, 0, xfm, None)
            self._finish()
            return
        xfm = None
        for s in range(BPC):
            xfm = self.phase_a_seq(s, xfm)
        if self.phases == "a":
            self._finish()
            return
        self.sched_collective()
        self.sched_wprime()
        a_end = p.s_pe.n
        self.prj_war = a_end
        self.xfm_war = a_end
        if self.phases == "ac":
            self._finish()
            return
        p.wait("g", p.s_pe, a_end)
        for dst, nm, sem in ((self.wb0, "wqT", p.s_wq2), (self.wb1, "woT", p.s_wo2)):
            srcw = self.dt[nm].ap().rearrange("(c p) o -> p c o", p=128)
            dstw = dst[:, :].rearrange("p (c o) -> p c o", o=D)
            p.g(lambda e, d=dstw, s2=srcw: e.dma_start(out=d, in_=s2))
            p.inc("g", sem, 16)
        self.wq_done = (p.s_wq2, p.s_wq2.n)
        self.wo_done = (p.s_wo2, p.s_wo2.n)
        for s in range(BPC):
            self.sched_bsetup1(s)
        pro = self.pro_b(0, 0, firstb=True)
        for s in range(BPC):
            self.sched_bsetup2(s)
        self.np_war = [0, 0, self.psetup_read]
        if self.phases == "w":
            self._finish()
            return
        tiles = [(s, t) for s in range(BPC) for t in range(NT)]
        for i, (s, t) in enumerate(tiles):
            if i + 1 < len(tiles):
                nxt = (*tiles[i + 1], tiles[i + 2] if i + 2 < len(tiles) else None)
            else:
                nxt = None
            pro = self.phase_b_tile(s, t, pro, nxt)
        self._finish()

    def _finish(self):
        p = self
        if self.debug:
            p.wait("g", p.s_ve, p.s_ve.n)
            p.wait("g", p.s_sc, p.s_sc.n)
            p.wait("g", p.s_pe, p.s_pe.n)
            p.wait("g", p.s_gp, p.s_gp.n)
            order = ["a0", "a", "ac", "w", "b0", "b1", "full"]
            lvl = order.index(self.phases)
            taps = [("tap_prj", self.prj_fm), ("tap_vs0", self.vs_row[0])]
            if lvl >= 1:
                taps += [("tap_c1e0", self.c1e[0][0]), ("tap_c1e1", self.c1e[0][1])]
            if lvl >= 2:
                taps += [("tap_stab", self.stab_in), ("tap_eps", self.epsp)]
            if lvl >= 3:
                taps += [("tap_k2rep", self.k2rep[0]), ("tap_v12", self.v12t[0])]
            for nm, src in taps:
                p.g(lambda e, nm=nm, src=src: e.dma_start(
                    out=self.tap_h[nm].ap(), in_=src[:, :]))
                p.inc("g", p.s_tap, 16)
        p.wait("g", p.s_o[0], p.s_o[0].n)
        p.wait("g", p.s_o[1], p.s_o[1].n)
        p.wait("g", p.s_tap, p.s_tap.n)


def _projz(proj):
    z = np.zeros((128, 512), dtype=np.float32)
    z[0:64, 0:256] = proj.T
    z[64:128, 256:512] = proj.T
    return z


def _projzq(proj):
    z = np.zeros((128, 512), dtype=np.float32)
    z[0:64, 0:256] = proj.T
    z[64:128, 256:512] = proj.T
    return z


def make_inputs(inputs):
    f32 = np.float32
    x = np.ascontiguousarray(np.asarray(inputs["L0_attn_hidden_states"], dtype=f32))
    wq = np.asarray(inputs["L0_attn_Wq"], dtype=f32)
    wk = np.asarray(inputs["L0_attn_Wk"], dtype=f32)
    wv = np.asarray(inputs["L0_attn_Wv"], dtype=f32)
    wo = np.asarray(inputs["L0_attn_Wo"], dtype=f32)
    bq = np.asarray(inputs["L0_attn_bq"], dtype=f32)
    bk = np.asarray(inputs["L0_attn_bk"], dtype=f32)
    bv = np.asarray(inputs["L0_attn_bv"], dtype=f32)
    bo = np.asarray(inputs["L0_attn_bo"], dtype=f32)
    proj = np.asarray(inputs["L0_attn_proj"], dtype=f32)
    W = np.asarray(inputs["L0_ffn_W"], dtype=f32)
    bffn = np.asarray(inputs["L0_ffn_b"], dtype=f32)
    Vr = np.asarray(inputs["L0_ffn_V_r"], dtype=f32)
    C = np.asarray(inputs["L0_ffn_C"], dtype=f32)

    common = {
        "wqT": np.ascontiguousarray(wq.T), "wkT": np.ascontiguousarray(wk.T),
        "wvT": np.ascontiguousarray(wv.T), "woT": np.ascontiguousarray(wo.T),
        "wT": np.ascontiguousarray(W.T),
        "vrT": np.ascontiguousarray(Vr.T), "cT": np.ascontiguousarray(C.T),
        "projz": _projz(proj), "projzq": _projzq(proj),
        "identf": np.eye(128, dtype=f32),
        "identb": np.eye(128, dtype=f32),
        "cvals": np.ascontiguousarray(np.broadcast_to(
            np.array([EPS_LN, -LN16, LNEPSQ], dtype=f32), (128, 3))),
        "ones": np.ones((128, 128), dtype=f32),
        "bo_rep": np.ascontiguousarray(np.broadcast_to(bo, (128, D))),
        "b_rep": np.ascontiguousarray(np.broadcast_to(bffn, (128, D))),
        "bv_rep": np.ascontiguousarray(np.broadcast_to(bv, (128, D))),
    }
    bqkv = np.zeros((128, 3 * OC), dtype=f32)
    for i, bb in enumerate((bq, bk, bv)):
        bqkv[:, i * OC:(i + 1) * OC] = bb.reshape(OC, 128).T
    common["bqkv"] = bqkv
    ediag = np.zeros((128, 128), dtype=f32)
    for c in range(OC):
        ediag[0:64, c * 16 + 2 * c] = -0.5
        ediag[64:128, c * 16 + 2 * c + 1] = -0.5
    common["ediag"] = ediag

    in_maps = []
    for core in range(NCORES):
        m = dict(common)
        m["x"] = np.ascontiguousarray(x[core * BPC:(core + 1) * BPC])
        in_maps.append(m)
    return in_maps


_CACHE = {}


def get_nc(debug=False):
    key = ("nc", debug)
    if key not in _CACHE:
        b = Builder(debug=debug)
        b.emit()
        _CACHE[key] = b.nc
    return _CACHE[key]


def kernel(**inputs):
    in_maps = make_inputs(inputs)
    nc = get_nc()
    res = run_bass_kernel_spmd(nc, in_maps, core_ids=list(range(NCORES)))
    return np.concatenate([np.asarray(res.results[c]["out"]) for c in range(NCORES)],
                          axis=0)


# version 10
# speedup vs baseline: 1.3371x; 1.3371x over previous
import contextlib
import numpy as np
import ml_dtypes
import concourse.bass as bass
import concourse.mybir as mybir
from concourse.bass_utils import run_bass_kernel_spmd

F32 = mybir.dt.float32
BF16 = mybir.dt.bfloat16
AF = mybir.ActivationFunctionType
ALU = mybir.AluOpType
AXL = mybir.AxisListType

B, N, D = 16, 4096, 1024
H, DH, M = 16, 64, 256
NCORES = 8
BPC = B // NCORES
TIL = 512
NT = N // TIL
NCH = TIL // 128
OC = D // 128

NORMZ = float(DH ** -0.25)
LN16 = float(np.log(16.0))
EPSQ = float((M ** -0.5) * 1e-3)
LNEPSQ = float(np.log(EPSQ))
EPS_LN = 1e-5
INV_D = 1.0 / D


class Sem:
    def __init__(self, h):
        self.h = h
        self.n = 0


class Builder:
    def __init__(self, debug=False, phases="full", stop=99):
        self.debug = debug
        self.phases = phases
        self.stop = stop
        self.nc = bass.Bass("TRN2", target_bir_lowering=False)
        self.q = {k: [] for k in "tvsgy"}
        self._can_inc = {k: False for k in "tvsgy"}
        self.ctx = contextlib.ExitStack()
        self.dt = {}
        self.build()

    def _ins(self, eng, fn):
        self.q[eng].append(fn)
        self._can_inc[eng] = True

    def t(self, fn): self._ins("t", fn)
    def v(self, fn): self._ins("v", fn)
    def s(self, fn): self._ins("s", fn)
    def g(self, fn): self._ins("g", fn)
    def y(self, fn): self._ins("y", fn)

    def wait(self, eng, sem, val):
        if val > sem.n:
            raise RuntimeError(f"wait {val} > scheduled {sem.n}")
        if val > 0:
            self.q[eng].append(lambda e, s=sem.h, v=val: e.wait_ge(s, v))
            self._can_inc[eng] = False

    def inc(self, eng, sem, k=1):
        if not self._can_inc[eng]:
            raise RuntimeError("inc must ride on a fresh datapath instruction")
        fn = self.q[eng].pop()
        self.q[eng].append(lambda e, f=fn, s=sem.h, k=k: f(e).then_inc(s, k))
        self._can_inc[eng] = False
        sem.n += k
        return sem.n

    def emit(self):
        with self.nc.Block() as blk:
            for key, attr in (("y", "sync"), ("g", "gpsimd"), ("s", "scalar"),
                              ("v", "vector"), ("t", "tensor")):
                fns = self.q[key]
                if fns:
                    getattr(blk, attr)(lambda e, fns=fns: [f(e) for f in fns])

    def build(self):
        nc, ctx = self.nc, self.ctx
        sb = lambda name, shape, dtype: ctx.enter_context(nc.sbuf_tensor(name, shape, dtype))
        psr = lambda name, shape: ctx.enter_context(nc.psum_tensor(name, shape, F32))
        mks = lambda name: Sem(ctx.enter_context(nc.semaphore(name)))

        def din(name, shape):
            self.dt[name] = nc.dram_tensor(name, shape, F32, kind="ExternalInput")
        din("x", [BPC, N, D])
        for nm in ("wqT", "wkT", "wvT", "woT", "wT"):
            din(nm, [D, D])
        din("vrT", [32, D]); din("cT", [32, D])
        din("bqkv", [128, 3 * OC])
        din("bo_rep", [128, D]); din("b_rep", [128, D]); din("bv_rep", [128, D])
        din("projz", [128, 512]); din("projzq", [128, 512])
        din("identf", [128, 128])
        din("ediag", [128, 128]); din("ones", [128, 128])
        din("identb", [128, 128]); din("cvals", [128, 3])
        self.out_h = nc.dram_tensor("out", [BPC, N, D], F32, kind="ExternalOutput")
        self.ccin = nc.dram_tensor("ccin", [1, 16], F32)
        self.ccout = nc.dram_tensor("ccout", [1, 16], F32)
        if self.debug:
            self.tap_h = {nm: nc.dram_tensor(nm, shape, F32, kind="ExternalOutput")
                          for nm, shape in [
                              ("tap_c1e0", [128, H * 66]), ("tap_c1e1", [128, H * 66]),
                              ("tap_vs0", [1, D]), ("tap_stab", [1, 16]),
                              ("tap_prj", [128, OC * TIL]),
                              ("tap_k2rep", [128, H]), ("tap_v12", [32, D]),
                              ("tap_eps", [128, 1])]}

        self.wb0 = sb("wb0", [128, OC * D], BF16)
        self.wb1 = sb("wb1", [128, OC * D], BF16)
        self.wb2 = sb("wb2", [128, OC * D], BF16)
        self.w_k = self.w_q = self.wb0
        self.w_v = self.w_o = self.wb1
        self.w_p = self.wb2
        self.projz = sb("projzS", [128, 512], BF16)
        self.projzq = sb("projzqS", [128, 512], BF16)
        self.identf = sb("identfS", [128, 128], F32)
        self.ediag = sb("ediagS", [128, 128], BF16)
        self.onesb = sb("onesbS", [128, 128], BF16)
        self.bqkv = sb("bqkvS", [128, 3 * OC], F32)
        self.cvalsb = sb("cvalsbS", [128, 3], F32)
        nc.const_aps.register(F32, EPS_LN, self.cvalsb[:, 0:1])
        nc.const_aps.register(F32, -LN16, self.cvalsb[:, 1:2])
        nc.const_aps.register(F32, LNEPSQ, self.cvalsb[:, 2:3])
        self.bo_rep = sb("bo_repS", [128, D], F32)
        self.b_rep = sb("b_repS", [128, D], F32)
        self.bv_rep = sb("bv_repS", [128, D], F32)
        self.onesf = sb("onesfS", [1, 128], F32)
        self.vr_sb = sb("vr_sb", [32, D], BF16)
        self.c_sb = sb("c_sb", [32, D], BF16)

        self.c1a = sb("c1a", [128, 32 * 65], F32)
        self.c1e = [[sb(f"c1e_{s}_{mc}", [128, H * 66], BF16) for mc in range(2)]
                    for s in range(BPC)]
        self.vs_row = [sb(f"vs_{s}", [1, D], F32) for s in range(BPC)]
        self.vst_fm = [sb(f"vst_{s}", [128, OC], F32) for s in range(BPC)]
        self.usb = [sb(f"usb_{s}", [128, 256], BF16) for s in range(BPC)]
        self.v12t = [sb(f"v12t_{s}", [32, D], BF16) for s in range(BPC)]
        self.k2rep = [sb(f"k2rep_{s}", [128, H], F32) for s in range(BPC)]
        self.stabv = sb("stabvS", [1, 16], F32)
        self.stab_in = sb("stab_inS", [1, 16], F32)
        self.stab_rep = sb("stab_repS", [128, 16], F32)
        self.epsp = sb("epspS", [128, 1], F32)
        self.nep = sb("nepS", [128, 1], F32)
        self.epnk = sb("epnkS", [128, 1], F32)
        self.evst = sb("evstS", [128, OC], F32)
        self.evs256 = sb("evs256S", [128, OC], F32)
        self.k1_fm = sb("k1_fmS", [128, OC], F32)
        self.cs_sb = sb("cs_sbS", [1, 1024], BF16)
        self.csfm2 = [sb(f"csfmS{i}", [128, OC], F32) for i in range(BPC)]
        self.csb_w = 0
        self.cs_marks = {}
        self.k2s_marks = {}
        self.stabsc = sb("stabscS", [1, 2], F32)

        self.x_tm = [sb(f"x_tm{i}", [128, NCH * D], F32) for i in range(2)]
        self.xln_tm = [sb(f"xln_tm{i}", [128, D], F32) for i in range(2)]
        self.xln_fm0 = sb("xln_fm0", [128, OC * TIL], BF16)
        self.prj_fm = sb("prj_fmS", [128, OC * TIL], BF16)
        self.k2b = [sb(f"k2b{i}", [128, TIL], BF16) for i in range(2)]
        self.e1b0 = [sb(f"e1b0_{i}", [128, TIL], BF16) for i in range(4)]
        self.vxb = sb("vxbS", [128, NCH * H * 65], BF16)
        self.edrep0 = sb("edrep0", [128, H * 65], BF16)
        self.ed_tm = sb("ed_tmS", [128, NCH * 16], F32)
        self.ed_rows = sb("ed_rowsS", [16, TIL], F32)
        self.dmaxb = sb("dmaxbS", [16, NT], F32)
        self.dmax1 = sb("dmax1S", [16, 1], F32)
        self.stat = [sb(f"stat{i}", [128, 4 * 16], F32) for i in range(2)]
        self.bst = sb("bst", [128, 12], F32)
        self.bst_war = 0
        self.sqscr = sb("sqscrS", [128, D], BF16)
        self.junk = sb("junkS", [128, 4], F32)
        c1a_addr = nc.lookup_mloc(self.c1a).addr
        self.gq = [nc.alloc_sbuf_tensor_at(f"gq{i}", [128, 2 * TIL], BF16,
                                           offset=c1a_addr + i * 2048)
                   for i in range(4)]
        self.qd3t = sb("qd3S", [128, NCH * 32], F32)
        self.qsden = sb("qsdenS", [128, H * 8], F32)
        self.tden = sb("tdenS", [128, H * 4], F32)
        self.attn_tm = sb("attn_tmS", [128, NCH * D], BF16)
        at_addr = nc.lookup_mloc(self.attn_tm).addr
        self.e1b = [self.e1b0,
                    [nc.alloc_sbuf_tensor_at(f"e1b1_{i}", [128, TIL], BF16,
                                             offset=at_addr + i * 1024)
                     for i in range(4)]]
        self.identb = sb("identbS", [128, 128], BF16)
        self.attn_fm0 = sb("attn_fm0", [128, OC * 128], BF16)
        self.qd_fm = sb("qd_fmS", [32, NCH * 128], BF16)
        self.lnin0 = sb("lnin0", [128, D], F32)
        self.lnout0 = sb("lnout0", [128, D], F32)
        self.xln2_tm = [sb(f"xln2_tm{i}", [128, D], F32) for i in range(2)]
        self.xln2_fm0 = sb("xln2_fm0", [128, OC * 128], BF16)
        self.wtstage = self.xln2_tm

        self.pool = [psr(f"poolP{i}", [128, 512]) for i in range(3)]
        self.gen = [psr(f"genP{i}", [128, 512]) for i in range(3)]
        self.pbf = ctx.enter_context(nc.psum_tensor("pbfP", [128, 1024], BF16))

        self.s_x = [mks("s_x0"), mks("s_x1")]
        self.s_cst = mks("s_cst"); self.s_wk = mks("s_wk"); self.s_wv = mks("s_wv")
        self.s_vrc = mks("s_vrc"); self.s_bo = mks("s_bo"); self.s_br = mks("s_br")
        self.s_wq2 = mks("s_wq2"); self.s_wo2 = mks("s_wo2")
        self.s_wt = [mks("s_wt0"), mks("s_wt1")]
        self.s_o = [mks("s_o0"), mks("s_o1")]
        self.s_pe = mks("s_pe"); self.s_sc = mks("s_sc"); self.s_ve = mks("s_ve")
        self.s_gp = mks("s_gp"); self.s_cc = mks("s_cc"); self.s_tap = mks("s_tap")
        self.s_ccr = mks("s_ccr")
        self.pool_mark = [None, None, None]
        self.pbf_mark = None
        self._rr = 0

        self.vx3 = self.vxb[:, :].rearrange("p (c h j) -> p c h j", c=NCH, j=65)
        self.bv3 = self.bv_rep[:, :].rearrange("p (h j) -> p h j", j=64)
        self.qd3 = self.qd3t[:, :].rearrange("p (c k) -> p c k", k=32)

        self.xpref = {}
        self.x_rel = [None, None]
        self.xlnT_mark = [0, 0]
        self.xfm_war = 0
        self.prj_war = 0
        self.pdiag_war = 0
        self.edr_war = 0
        self.vx_war = 0
        self.k2_war = 0
        self.c1mini_free = [0, 0]
        self.c1aw = [0] * 8
        self.vs_war = [0, 0]
        self.afm_war = 0
        self.attn_war = 0
        self.x2f_war = 0
        self.lnin_war = None
        self.lnout_war = 0
        self.np_war = [0, 0, 0]
        self.last_c1pair = {}
        self.ln_mark = [dict(), dict()]

        self.schedule()

    def pool_next(self):
        b = self._rr % 3
        self._rr += 1
        return b

    def pool_acquire(self, bank):
        m = self.pool_mark[bank]
        if m is not None:
            self.wait("t", m[0], m[1])

    def pool_release(self, bank, sem, val):
        self.pool_mark[bank] = (sem, val)

    def emit_ln_chunk(self, tb, slot, src_ap, dst_ap, waits_v=(), waits_s=(),
                      apply_eng="v"):
        p = self
        st = self.stat[tb]
        base = slot * 4
        cMV, cLNV, cRS = base, base + 2, base + 3
        bst = self.bst
        war = self.ln_mark[tb].get(slot, 0)
        for sem, val in waits_v:
            p.wait("v", sem, val)
        if war:
            p.wait("v", p.s_sc, war)
        if self.bst_war:
            p.wait("v", p.s_ve, self.bst_war)
        p.v(lambda e, a=src_ap: e.bn_stats(bst[:, 0:6], a[:, 0:512]))
        p.v(lambda e, a=src_ap: e.bn_stats(bst[:, 6:12], a[:, 512:1024]))
        bsv = p.inc("v", p.s_ve, 1)
        p.wait("v", p.s_ve, bsv)
        p.v(lambda e: e.bn_aggr(st[:, cMV:cMV + 2], bst[:, 0:12]))
        agv = p.inc("v", p.s_ve, 1)
        self.bst_war = agv
        for sem, val in waits_s:
            p.wait("s", sem, val)
        p.wait("s", p.s_ve, agv)
        p.s(lambda e: e.activation(out=st[:, cLNV:cLNV + 1], in_=st[:, cMV + 1:cMV + 2],
                                   func=AF.Ln, bias=EPS_LN))
        lnv = p.inc("s", p.s_sc, 1)
        p.wait("s", p.s_sc, lnv)
        p.s(lambda e: e.activation(out=st[:, cRS:cRS + 1], in_=st[:, cLNV:cLNV + 1],
                                   func=AF.Exp, scale=-0.5))
        rstd = p.inc("s", p.s_sc, 1)
        cM = cMV
        self.ln_mark[tb][slot] = rstd
        sem = {"v": p.s_ve, "g": p.s_gp}[apply_eng]
        p.wait(apply_eng, p.s_sc, rstd)
        for w in waits_v:
            if apply_eng != "v":
                p.wait(apply_eng, *w)
        p.q[apply_eng].append(lambda e, a=src_ap, o=dst_ap: e.tensor_scalar(
            out=o, in0=a, scalar1=st[:, cM:cM + 1], scalar2=st[:, cRS:cRS + 1],
            op0=ALU.subtract, op1=ALU.mult))
        p._can_inc[apply_eng] = True
        return (sem, p.inc(apply_eng, sem, 1))

    def emit_transposes(self, src_t, src_off, dst_fm, dst_base, dst_stride,
                        extra_t_waits=()):
        p = self
        last_sc = last_pe = None
        d3 = dst_fm[:, :].rearrange("p (o k) -> p o k", k=dst_stride)
        b0 = dst_base // dst_stride
        co = dst_base % dst_stride
        for g2 in range(2):
            b = p.pool_next()
            p.pool_acquire(b)
            for sem, val in extra_t_waits:
                p.wait("t", sem, val)
            for j in range(4):
                oc = g2 * 4 + j
                p.t(lambda e, b=b, j=j, off=src_off + oc * 128, s=src_t:
                    e.transpose(p.pool[b][:, j * 128:(j + 1) * 128],
                                s[:, off:off + 128], p.identf[:, :]))
            last_pe = p.inc("t", p.s_pe, 1)
            p.wait("s", p.s_pe, last_pe)
            oap = d3[:, b0 + g2 * 4: b0 + (g2 + 1) * 4, co:co + 128]
            iap = p.pool[b][:, :].rearrange("p (o k) -> p o k", k=128)
            p.s(lambda e, oap=oap, iap=iap: e.copy(oap, iap))
            last_sc = p.inc("s", p.s_sc, 1)
            p.pool_release(b, p.s_sc, last_sc)
        return last_sc, last_pe

    def emit_transposes_bf16(self, src_t, src_off, dst_fm, extra_t_waits=(),
                             extra_s_waits=()):
        p = self
        if self.pbf_mark is not None:
            p.wait("t", *self.pbf_mark)
        for sem, val in extra_t_waits:
            p.wait("t", sem, val)
        for oc in range(OC):
            p.t(lambda e, off=src_off + oc * 128, oc=oc, s=src_t:
                e.transpose(p.pbf[:, oc * 128:(oc + 1) * 128],
                            s[:, off:off + 128], p.identb[:, :]))
        pe = p.inc("t", p.s_pe, 1)
        p.wait("s", p.s_pe, pe)
        for sem, val in extra_s_waits:
            p.wait("s", sem, val)
        p.s(lambda e, d=dst_fm: e.copy(d[:, 0:OC * 128], p.pbf[:, :]))
        sc = p.inc("s", p.s_sc, 1)
        self.pbf_mark = (p.s_sc, sc)
        return sc, pe

    def sched_loads(self):
        p = self

        def gload(dst, name, sem, rearr=False):
            src = self.dt[name].ap()
            if rearr:
                src = src.rearrange("(c p) o -> p c o", p=128)
                dstap = dst[:, :].rearrange("p (c o) -> p c o", o=D)
            else:
                dstap = dst[:, :]
            p.g(lambda e, d=dstap, s=src: e.dma_start(out=d, in_=s))
            return (sem, p.inc("g", sem, 16))

        for dst, nm in ((self.identf, "identf"), (self.projz, "projz"),
                        (self.projzq, "projzq"),
                        (self.ediag, "ediag"), (self.onesb, "ones"),
                        (self.bqkv, "bqkv"), (self.bv_rep, "bv_rep"),
                        (self.identb, "identb"), (self.cvalsb, "cvals")):
            gload(dst, nm, p.s_cst)
        p.g(lambda e: e.dma_start(out=self.onesf[0:1, :], in_=self.dt["ones"].ap()[0:1, :]))
        p.inc("g", p.s_cst, 16)
        self.consts_done = (p.s_cst, p.s_cst.n)
        self.wk_done = gload(self.w_k, "wkT", p.s_wk, True)
        self.wv_done = gload(self.w_v, "wvT", p.s_wv, True)
        gload(self.vr_sb, "vrT", p.s_vrc)
        self.vrc_done = gload(self.c_sb, "cT", p.s_vrc)
        self.bo_done = gload(self.bo_rep, "bo_rep", p.s_bo)
        self.brep_done = gload(self.b_rep, "b_rep", p.s_br)

    def load_x_tile(self, s, t):
        p = self
        key = (s, t)
        if key in p.xpref:
            return p.xpref.pop(key)
        tb = t % 2
        if p.x_rel[tb] is not None:
            for sem, val in p.x_rel[tb]:
                p.wait("y", sem, val)
        src = p.dt["x"].ap()[s, t * TIL:(t + 1) * TIL, :].rearrange("(c p) f -> p c f", p=128)
        dst = p.x_tm[tb][:, :].rearrange("p (c f) -> p c f", f=D)
        p.y(lambda e, d=dst, s2=src: e.dma_start(out=d, in_=s2))
        mark = (p.s_x[tb], p.inc("y", p.s_x[tb], 16))
        return mark

    def prefetch_x(self, s, t):
        if (s, t) not in self.xpref:
            self.xpref[(s, t)] = self.load_x_tile(s, t)

    def prologue_tile(self, s, t, first=False):
        p = self
        tb = t % 2
        x_done = self.load_x_tile(s, t)
        xc = p.x_tm[tb][:, :].rearrange("p (c f) -> p c f", f=D)
        for c in range(NCH):
            waits_v = [x_done]
            if p.xlnT_mark[c % 2] > 0:
                waits_v.append((p.s_pe, p.xlnT_mark[c % 2]))
            ap_done = self.emit_ln_chunk(tb, c, xc[:, c, :], p.xln_tm[c % 2][:, :],
                                         waits_v=waits_v, waits_s=[x_done])
            tw = [ap_done]
            if first and c == 0:
                tw.append(self.consts_done)
            if self.xfm_war > 0:
                tw.append((p.s_pe, self.xfm_war))
            _, tpe = self.emit_transposes(p.xln_tm[c % 2], 0, p.xln_fm0,
                                          c * 128, TIL, extra_t_waits=tw)
            p.xlnT_mark[c % 2] = tpe
        p.x_rel[tb] = [(p.s_ve, p.s_ve.n), (p.s_sc, p.s_sc.n)]
        return p.s_sc.n

    def phase_a_tile(self, s, t, xfm_done, nxt):
        p = self
        tb = t % 2
        first = (s == 0 and t == 0)
        if self.stop <= 1:
            return None
        diag_mm = {}
        k2dv = {}
        for oc in range(OC):
            b = p.pool_next()
            p.pool_acquire(b)
            p.wait("t", p.s_sc, xfm_done)
            if first and oc == 0:
                p.wait("t", *self.wk_done)
            for ic in range(OC):
                p.t(lambda e, b=b, ic=ic, oc=oc: e.matmul(
                    p.pool[b][:, :], p.w_k[:, ic * D + oc * 128: ic * D + (oc + 1) * 128],
                    p.xln_fm0[:, ic * TIL:(ic + 1) * TIL],
                    start=(ic == 0), stop=(ic == OC - 1)))
            rv = p.inc("t", p.s_pe, 1)
            p.wait("s", p.s_pe, rv)
            if oc == 0 and self.prj_war > 0:
                p.wait("s", p.s_pe, self.prj_war)
            p.s(lambda e, b=b, oc=oc: e.activation(
                out=p.prj_fm[:, oc * TIL:(oc + 1) * TIL], in_=p.pool[b][:, :],
                func=AF.Identity, bias=p.bqkv[:, OC + oc:OC + oc + 1]))
            p.inc("s", p.s_sc, 1)
            if oc >= 2:
                p.wait("s", p.s_pe, diag_mm[oc - 2])
            elif self.k2_war > 0:
                p.wait("s", p.s_pe, self.k2_war)
            p.s(lambda e, b=b, oc=oc: e.activation(
                out=p.k2b[oc % 2][:, :], in_=p.pool[b][:, :],
                func=AF.Square, bias=p.bqkv[:, OC + oc:OC + oc + 1]))
            k2d = p.inc("s", p.s_sc, 1)
            p.pool_release(b, p.s_sc, k2d)
            k2dv[oc] = k2d

            def emit_diag(oc2):
                p.wait("t", p.s_sc, k2dv[oc2])
                if oc2 == 0:
                    if self.pdiag_war > 0:
                        p.wait("t", p.s_ve, self.pdiag_war)
                    if self.edr_war > 0:
                        p.wait("t", p.s_sc, self.edr_war)
                p.t(lambda e, oc2=oc2: e.matmul(
                    p.gen[2][0:16, :], p.ediag[:, oc2 * 16:(oc2 + 1) * 16],
                    p.k2b[oc2 % 2][:, :],
                    start=(oc2 == 0), stop=(oc2 == OC - 1)))
                diag_mm[oc2] = p.inc("t", p.s_pe, 1)

            if oc >= 1:
                emit_diag(oc - 1)
            if oc == OC - 1:
                emit_diag(oc)
        self.k2_war = diag_mm[6]
        kevac_done = p.s_sc.n
        if self.stop <= 2:
            return None
        for c in range(NCH):
            for half in range(2):
                b = p.pool_next()
                p.pool_acquire(b)
                if first and c == 0 and half == 0:
                    p.wait("t", *self.wv_done)
                for ic in range(OC):
                    p.t(lambda e, b=b, ic=ic, c=c, half=half: e.matmul(
                        p.pool[b][:, :],
                        p.xln_fm0[:, ic * TIL + c * 128: ic * TIL + (c + 1) * 128],
                        p.w_v[:, ic * D + half * 512: ic * D + (half + 1) * 512],
                        start=(ic == 0), stop=(ic == OC - 1)))
                rv = p.inc("t", p.s_pe, 1)
                p.wait("v", p.s_pe, rv)
                if c == 0 and half == 0:
                    if first:
                        p.wait("v", *self.consts_done)
                    if self.vx_war > 0:
                        p.wait("v", p.s_pe, self.vx_war)
                iap = p.pool[b][:, :].rearrange("p (h j) -> p h j", j=64)
                p.v(lambda e, iap=iap, c=c, half=half: e.tensor_tensor(
                    out=p.vx3[:, c, half * 8:(half + 1) * 8, 0:64], in0=iap,
                    in1=p.bv3[:, half * 8:(half + 1) * 8, :], op=ALU.add))
                vv = p.inc("v", p.s_ve, 1)
                p.pool_release(b, p.s_ve, vv)
        self.xfm_war = p.s_pe.n
        p.v(lambda e: e.memset(p.vx3[:, :, :, 64:65], 1.0))
        vx_done = p.inc("v", p.s_ve, 1)
        if self.stop <= 3:
            return None
        p.wait("s", p.s_pe, diag_mm[OC - 1])
        p.s(lambda e: e.activation(out=p.ed_rows[:, :], in_=p.gen[2][0:16, :], func=AF.Exp))
        edr_done = p.inc("s", p.s_sc, 1)
        self.edr_war = edr_done
        p.wait("v", p.s_pe, diag_mm[OC - 1])
        p.v(lambda e, t=t: e.reduce_max(p.dmaxb[0:16, t:t + 1], p.gen[2][0:16, :],
                                        axis=AXL.X))
        self.pdiag_war = p.inc("v", p.s_ve, 1)
        b = p.pool_next()
        p.pool_acquire(b)
        p.wait("t", p.s_sc, edr_done)
        for c in range(NCH):
            p.t(lambda e, b=b, c=c: e.transpose(
                p.pool[b][:, c * 16:(c + 1) * 16],
                p.ed_rows[0:16, c * 128:(c + 1) * 128], p.identf[0:16, 0:16]))
        rv = p.inc("t", p.s_pe, 1)
        p.wait("v", p.s_pe, rv)
        p.v(lambda e, b=b: e.tensor_copy(p.ed_tm[:, :], p.pool[b][:, 0:NCH * 16]))
        edt = p.inc("v", p.s_ve, 1)
        p.pool_release(b, p.s_ve, edt)
        for half in range(2):
            b = p.pool_next()
            p.pool_acquire(b)
            p.wait("t", p.s_ve, vx_done)
            for c in range(NCH):
                p.t(lambda e, b=b, c=c, half=half: e.matmul(
                    p.pool[b][0:1, :], p.onesb[:, 0:1],
                    p.vx3[:, c, half * 8:(half + 1) * 8, 0:64],
                    start=(c == 0), stop=(c == NCH - 1)))
            rv = p.inc("t", p.s_pe, 1)
            p.wait("v", p.s_pe, rv)
            if self.vs_war[half] > 0:
                p.wait("v", p.s_ve, self.vs_war[half])
            if t == 0:
                p.v(lambda e, b=b, half=half, s=s: e.tensor_copy(
                    p.vs_row[s][:, half * 512:(half + 1) * 512], p.pool[b][0:1, :]))
            else:
                p.v(lambda e, b=b, half=half, s=s: e.tensor_tensor(
                    out=p.vs_row[s][:, half * 512:(half + 1) * 512],
                    in0=p.vs_row[s][:, half * 512:(half + 1) * 512],
                    in1=p.pool[b][0:1, :], op=ALU.add))
            self.vs_war[half] = p.inc("v", p.s_ve, 1)
            p.pool_release(b, p.s_ve, self.vs_war[half])
        vs_mm_done = p.s_pe.n
        if self.stop <= 4:
            return None
        p.wait("v", p.s_pe, vs_mm_done)
        p.wait("v", p.s_ve, edt)
        mul_prev = 0
        for c in range(NCH):
            er = p.edrep0
            src = p.ed_tm[:, c * 16:(c + 1) * 16].broadcast_to([128, 16, 65])
            if mul_prev > 0:
                p.wait("v", p.s_ve, mul_prev)
            p.v(lambda e, er=er, src=src: e.tensor_copy(
                er[:, :].rearrange("p (h j) -> p h j", j=65), src))
            erv = p.inc("v", p.s_ve, 1)
            p.wait("v", p.s_ve, erv)
            p.v(lambda e, er=er, c=c: e.tensor_tensor(
                out=p.vx3[:, c, :, :], in0=p.vx3[:, c, :, :],
                in1=er[:, :].rearrange("p (h j) -> p h j", j=65), op=ALU.mult))
            mul_prev = p.inc("v", p.s_ve, 1)
        vp_done = p.s_ve.n
        if self.stop <= 5:
            return None
        nxt_mark = None
        if nxt is not None:
            nxt_mark = self.prologue_tile(nxt[0], nxt[1])
        c1grp = dict(self.last_c1pair)
        exp_done = {}

        def emit_c1(pr):
            mb = pr % 2
            p.wait("t", p.s_sc, exp_done[pr])
            if pr == 0:
                p.wait("t", p.s_ve, vp_done)
            if self.c1mini_free[mb] > 0:
                p.wait("t", p.s_ve, self.c1mini_free[mb])
            for hh in range(2):
                for mc in range(2):
                    h = 2 * pr + hh
                    soff = (hh * 2 + mc) * 65
                    for c in range(NCH):
                        p.t(lambda e, mb=mb, soff=soff, c=c, hh=hh, h=h, pr=pr:
                            e.matmul(p.gen[mb][:, soff:soff + 65],
                                     p.e1b[pr % 2][c][:, hh * 256 + (soff // 65 % 2) * 128: hh * 256 + (soff // 65 % 2) * 128 + 128],
                                     p.vx3[:, c, h, :],
                                     start=(c == 0), stop=(c == NCH - 1)))
            c1grp[pr] = p.inc("t", p.s_pe, 1)
            p.wait("v", p.s_pe, c1grp[pr])
            if self.c1aw[pr] > 0:
                p.wait("v", p.s_ve, self.c1aw[pr])
            if t == 0:
                p.v(lambda e, mb=mb, pr=pr: e.tensor_copy(
                    p.c1a[:, pr * 260:(pr + 1) * 260], p.gen[mb][:, 0:260]))
            else:
                p.v(lambda e, mb=mb, pr=pr: e.tensor_tensor(
                    out=p.c1a[:, pr * 260:(pr + 1) * 260],
                    in0=p.c1a[:, pr * 260:(pr + 1) * 260],
                    in1=p.gen[mb][:, 0:260], op=ALU.add))
            self.c1aw[pr] = p.inc("v", p.s_ve, 1)
            self.c1mini_free[mb] = self.c1aw[pr]

        for pr in range(8):
            for c in range(NCH):
                b = p.pool_next()
                p.pool_acquire(b)
                col = pr * TIL + c * 128
                p.wait("t", p.s_sc, kevac_done)
                p.t(lambda e, b=b, col=col: e.matmul(
                    p.pool[b][:, :], p.prj_fm[:, col:col + 128], p.projz[:, :]))
                rv = p.inc("t", p.s_pe, 1)
                eb = p.e1b[pr % 2][c]
                p.wait("s", p.s_pe, rv)
                war = c1grp.get(pr - 2)
                if war:
                    p.wait("s", p.s_pe, war)
                p.s(lambda e, b=b, eb=eb: e.activation(
                    out=eb[:, :], in_=p.pool[b][:, :], func=AF.Exp,
                    scale=NORMZ, bias=-LN16))
                ev = p.inc("s", p.s_sc, 1)
                p.pool_release(b, p.s_sc, ev)
            exp_done[pr] = p.s_sc.n
            if pr >= 1:
                emit_c1(pr - 1)
        emit_c1(7)
        self.vx_war = c1grp[7]
        self.last_c1pair = {-2: c1grp[6], -1: c1grp[7]}
        self.prj_war = c1grp[7]
        self.last_c1evac = max(self.c1aw)
        return nxt_mark

    def phase_a_seq(self, s, xfm_first=None):
        p = self
        xfm = xfm_first
        for t in range(NT):
            if xfm is None:
                xfm = self.prologue_tile(s, t, first=(s == 0 and t == 0))
            if t + 1 < NT:
                nxt = (s, t + 1, (s, t + 2) if t + 2 < NT else
                       ((s + 1, 0) if s + 1 < BPC else None))
            elif s + 1 < BPC:
                nxt = (s + 1, 0, (s + 1, 1))
            else:
                nxt = None
            xfm = self.phase_a_tile(s, t, xfm, nxt)
        p.wait("v", p.s_ve, self.last_c1evac)
        for mc in range(2):
            iap = p.c1a[:, :].rearrange("p (h j) -> p h j", j=130)[:, :, mc * 65:(mc + 1) * 65]
            oap = p.c1e[s][mc][:, :].rearrange("p (h j) -> p h j", j=66)[:, :, 0:65]
            p.v(lambda e, iap=iap, oap=oap: e.tensor_copy(oap, iap))
            p.inc("v", p.s_ve, 1)
            ap1 = p.c1e[s][mc][:, :].rearrange("p (h j) -> p h j", j=66)[:, :, 65:66]
            p.v(lambda e, ap1=ap1: e.memset(ap1, 1.0))
            p.inc("v", p.s_ve, 1)
        c1e_done = p.s_ve.n
        self.c1a_free = c1e_done
        self.c1aw = [0] * 8
        b = p.pool_next()
        p.pool_acquire(b)
        p.wait("t", p.s_ve, c1e_done)
        for oc in range(OC):
            p.t(lambda e, b=b, oc=oc, s=s: e.transpose(
                p.pool[b][:, oc:oc + 1], p.vs_row[s][0:1, oc * 128:(oc + 1) * 128],
                p.identf[0:1, 0:1]))
        rv = p.inc("t", p.s_pe, 1)
        p.wait("v", p.s_pe, rv)
        p.v(lambda e, b=b, s=s: e.tensor_copy(p.vst_fm[s][:, :], p.pool[b][:, 0:OC]))
        vst = p.inc("v", p.s_ve, 1)
        p.pool_release(b, p.s_ve, vst)
        self.vs_war = [0, 0]
        self.vst_done = vst
        p.v(lambda e: e.reduce_max(p.dmax1[0:16, 0:1], p.dmaxb[0:16, 0:NT], axis=AXL.X))
        dm = p.inc("v", p.s_ve, 1)
        b = p.pool_next()
        p.pool_acquire(b)
        p.wait("t", p.s_ve, dm)
        p.t(lambda e, b=b: e.transpose(p.pool[b][0:1, 0:16], p.dmax1[0:16, 0:1],
                                       p.identf[0:16, 0:16]))
        rv = p.inc("t", p.s_pe, 1)
        p.wait("v", p.s_pe, rv)
        p.v(lambda e, b=b, s=s: e.reduce_max(p.stabsc[0:1, s:s + 1],
                                             p.pool[b][0:1, 0:16], axis=AXL.X))
        sv = p.inc("v", p.s_ve, 1)
        p.pool_release(b, p.s_ve, sv)
        return xfm

    def sched_collective(self):
        p = self
        p.wait("v", p.s_ve, p.s_ve.n)
        p.v(lambda e: e.reduce_max(p.stabv[0:1, 0:1], p.stabsc[0:1, 0:2], axis=AXL.X))
        smx = p.inc("v", p.s_ve, 1)
        p.wait("v", p.s_ve, smx)
        src = p.stabv[0:1, 0:1].broadcast_to([1, 1, 15])
        dst = p.stabv[0:1, 1:16].rearrange("p (a j) -> p a j", a=1)
        p.v(lambda e, src=src, dst=dst: e.tensor_copy(dst, src))
        st_done = p.inc("v", p.s_ve, 1)
        p.wait("g", p.s_ve, st_done)
        p.g(lambda e: e.dma_start(out=p.ccin.ap(), in_=p.stabv[0:1, :]))
        p.inc("g", p.s_cc, 16)
        p.wait("g", p.s_cc, 16)
        p.g(lambda e: e.collective_compute(
            "AllReduce", ALU.max, replica_groups=[list(range(NCORES))],
            ins=[p.ccin.ap().opt()], outs=[p.ccout.ap().opt()]))
        p.inc("g", p.s_ccr, 1)
        p.wait("g", p.s_ccr, 1)
        p.g(lambda e: e.dma_start(out=p.stab_in[0:1, :], in_=p.ccout.ap()))
        p.inc("g", p.s_cc, 16)
        p.wait("g", p.s_cc, 32)
        b = p.pool_next()
        p.pool_acquire(b)
        p.wait("t", p.s_cc, 32)
        p.t(lambda e, b=b: e.matmul(p.pool[b][:, 0:16], p.onesf[0:1, :],
                                    p.stab_in[0:1, :]))
        rv = p.inc("t", p.s_pe, 1)
        p.wait("v", p.s_pe, rv)
        p.v(lambda e, b=b: e.tensor_copy(p.stab_rep[:, :], p.pool[b][:, 0:16]))
        bc = p.inc("v", p.s_ve, 1)
        p.pool_release(b, p.s_ve, bc)
        p.wait("s", p.s_ve, bc)
        p.s(lambda e: e.activation(out=p.epsp[:, :], in_=p.stab_rep[:, 0:1],
                                   func=AF.Exp, bias=LNEPSQ))
        ep = p.inc("s", p.s_sc, 1)
        p.wait("v", p.s_sc, ep)
        p.v(lambda e: e.tensor_scalar_mul(p.nep[:, :], p.epsp[:, :], float(N)))
        p.inc("v", p.s_ve, 1)
        p.v(lambda e: e.tensor_scalar_mul(p.epnk[:, :], p.epsp[:, :],
                                          float(256.0 * EPSQ * N)))
        self.eps_done = p.inc("v", p.s_ve, 1)

    def sched_wprime(self):
        p = self
        wst_free = {}
        for ic in range(OC):
            src = p.dt["wT"].ap()[ic * 128:(ic + 1) * 128, :]
            if ic >= 2:
                p.wait("y", p.s_ve, wst_free[ic - 2])
            p.y(lambda e, s2=src, ic=ic: e.dma_start(out=p.wtstage[ic % 2][:, :], in_=s2))
            wst = (p.s_wt[ic % 2], p.inc("y", p.s_wt[ic % 2], 16))
            for half in range(2):
                b = p.pool_next()
                p.pool_acquire(b)
                if ic == 0 and half == 0:
                    p.wait("t", *self.vrc_done)
                p.t(lambda e, b=b, ic=ic, half=half: e.matmul(
                    p.pool[b][:, :], p.vr_sb[0:32, ic * 128:(ic + 1) * 128],
                    p.c_sb[0:32, half * 512:(half + 1) * 512]))
                rv = p.inc("t", p.s_pe, 1)
                p.wait("v", p.s_pe, rv)
                p.wait("v", *wst)
                p.v(lambda e, b=b, ic=ic, half=half: e.tensor_tensor(
                    out=p.w_p[:, ic * D + half * 512: ic * D + (half + 1) * 512],
                    in0=p.pool[b][:, :],
                    in1=p.wtstage[ic % 2][:, half * 512:(half + 1) * 512],
                    op=ALU.add))
                wst_free[ic] = p.inc("v", p.s_ve, 1)
                p.pool_release(b, p.s_ve, wst_free[ic])
        self.wp_done = p.s_ve.n

    def sched_bsetup1(self, s):
        p = self
        pse = p.gen[2]
        if s == 0:
            if self.pdiag_war > 0:
                p.wait("t", p.s_ve, self.pdiag_war)
            if self.edr_war > 0:
                p.wait("t", p.s_sc, self.edr_war)
        csb = self.cs_sb
        for grp in range(2):
            for hh in range(8):
                h = grp * 8 + hh
                for mc in range(2):
                    p.t(lambda e, h=h, hh=hh, mc=mc, s=s: e.matmul(
                        pse[0:1, hh * 64:(hh + 1) * 64], p.onesb[:, 0:1],
                        p.c1e[s][mc][:, h * 66:h * 66 + 64],
                        start=(mc == 0), stop=(mc == 1)))
            gv = p.inc("t", p.s_pe, 1)
            p.wait("v", p.s_pe, gv)
            if grp == 1:
                p.wait("v", p.s_ve, self.csb_w)
            p.v(lambda e, grp=grp: e.tensor_copy(csb[0:1, grp * 512:(grp + 1) * 512],
                                                 pse[0:1, 0:512]))
            self.csb_w = p.inc("v", p.s_ve, 1)
            if grp == 0:
                p.wait("t", p.s_ve, self.csb_w)
        p.wait("t", p.s_ve, self.csb_w)
        if self.pbf_mark is not None:
            p.wait("t", *self.pbf_mark)
        for h in range(H):
            p.t(lambda e, h=h: e.transpose(
                p.pbf[(h % 2) * 64:(h % 2) * 64 + 64, 2 * (h // 2):2 * (h // 2) + 1],
                csb[0:1, h * 64:(h + 1) * 64], p.identb[0:1, 0:1]))
        cst = p.inc("t", p.s_pe, 1)
        p.wait("v", p.s_pe, cst)
        csrc = p.pbf[:, 0:2 * OC].rearrange("p (c k) -> p c k", k=2)[:, :, 0]
        p.v(lambda e, s=s, csrc=csrc: e.tensor_copy(p.csfm2[s][:, :], csrc))
        csf = p.inc("v", p.s_ve, 1)
        self.pbf_mark = (p.s_ve, csf)
        self.cs_marks[s] = csf
        for mc in range(2):
            gsap = p.c1e[s][mc][:, :].rearrange("p (h j) -> p h j", j=66)[:, :, 64]
            p.t(lambda e, gsap=gsap, mc=mc, s=s: e.matmul(
                pse[:, 16 + s * 16:32 + s * 16], p.onesb[:, :], gsap,
                start=(mc == 0), stop=(mc == 1)))
        self.k2s_marks[s] = p.inc("t", p.s_pe, 1)

    def sched_bsetup2(self, s):
        p = self
        pse = p.gen[2]
        p.wait("v", p.s_ve, self.eps_done)
        p.v(lambda e, s=s: e.tensor_scalar_mul(p.evst[:, :], p.vst_fm[s][:, :],
                                               p.epsp[:, 0:1]))
        p.inc("v", p.s_ve, 1)
        p.v(lambda e, s=s: e.tensor_scalar(
            out=p.evs256[:, :], in0=p.vst_fm[s][:, :], scalar1=p.epsp[:, 0:1],
            scalar2=float(256.0 * EPSQ), op0=ALU.mult, op1=ALU.mult))
        ev2 = p.inc("v", p.s_ve, 1)
        p.wait("v", p.s_ve, ev2)
        p.v(lambda e, s=s: e.scalar_tensor_tensor(
            out=p.k1_fm[:, :], in0=p.csfm2[s][:, :], scalar=EPSQ, in1=p.evs256[:, :],
            op0=ALU.mult, op1=ALU.add))
        k1f = p.inc("v", p.s_ve, 1)
        p.wait("v", p.s_pe, self.k2s_marks[s])
        p.v(lambda e, s=s: e.tensor_scalar(
            out=p.k2rep[s][:, :], in0=pse[:, 16 + s * 16:32 + s * 16], scalar1=EPSQ,
            scalar2=p.epnk[:, 0:1], op0=ALU.mult, op1=ALU.add))
        self.psetup_read = p.inc("v", p.s_ve, 1)
        p.v(lambda e, s=s: e.memset(p.usb[s][:, :], 0.0))
        um = p.inc("v", p.s_ve, 1)
        p.wait("v", p.s_ve, um)
        for c in range(OC):
            for colofs, srcT in ((2 * c, p.evst), (16 + 2 * c, p.k1_fm)):
                p.v(lambda e, c=c, colofs=colofs, srcT=srcT, s=s: e.tensor_copy(
                    p.usb[s][0:64, c * 32 + colofs:c * 32 + colofs + 1],
                    srcT[0:64, c:c + 1]))
                p.inc("v", p.s_ve, 1)
                p.v(lambda e, c=c, colofs=colofs, srcT=srcT, s=s: e.tensor_copy(
                    p.usb[s][64:128, c * 32 + colofs + 1:c * 32 + colofs + 2],
                    srcT[64:128, c:c + 1]))
                p.inc("v", p.s_ve, 1)
        u_done = p.s_ve.n
        for half in range(2):
            b = p.pool_next()
            p.pool_acquire(b)
            p.wait("t", p.s_ve, u_done)
            if s == 0 and half == 0:
                p.wait("t", *self.wo_done)
            for c in range(OC):
                p.t(lambda e, b=b, c=c, half=half, s=s: e.matmul(
                    p.pool[b][0:32, :], p.usb[s][:, c * 32:(c + 1) * 32],
                    p.w_o[:, c * D + half * 512: c * D + (half + 1) * 512],
                    start=(c == 0), stop=(c == OC - 1)))
            rv = p.inc("t", p.s_pe, 1)
            p.wait("v", p.s_pe, rv)
            p.v(lambda e, b=b, half=half, s=s: e.tensor_copy(
                p.v12t[s][0:32, half * 512:(half + 1) * 512], p.pool[b][0:32, :]))
            vv = p.inc("v", p.s_ve, 1)
            p.pool_release(b, p.s_ve, vv)
        self.bset_done = p.s_ve.n

    def _emit_head_tail(self, h, recv):
        p = self
        p.wait("v", p.s_ve, recv)
        qodd = p.qsden[:, h * 8:(h + 1) * 8].rearrange("p (n j) -> p n j", j=2)[:, :, 1]
        p.v(lambda e, h=h, qodd=qodd: e.tensor_tensor(
            out=p.qd3[:, :, h], in0=qodd, in1=p.qd3[:, :, 16 + h], op=ALU.mult))
        p.inc("v", p.s_ve, 1)
        npb = p.gen[h % 3]
        for ns in range(NCH):
            p.v(lambda e, h=h, ns=ns, npb=npb: e.tensor_scalar_mul(
                p.attn_tm[:, ns * D + h * 64: ns * D + (h + 1) * 64],
                npb[:, ns * 66:ns * 66 + 64],
                p.qd3t[:, ns * 32 + 16 + h: ns * 32 + 16 + h + 1]))
            p.inc("v", p.s_ve, 1)

    def pro_b(self, s, t, firstb=False):
        p = self
        xfm_done = self.prologue_tile(s, t)
        for oc in range(OC):
            b = p.pool_next()
            p.pool_acquire(b)
            p.wait("t", p.s_sc, xfm_done)
            if firstb and oc == 0:
                p.wait("t", *self.wq_done)
            for ic in range(OC):
                p.t(lambda e, b=b, ic=ic, oc=oc: e.matmul(
                    p.pool[b][:, :], p.w_q[:, ic * D + oc * 128: ic * D + (oc + 1) * 128],
                    p.xln_fm0[:, ic * TIL:(ic + 1) * TIL],
                    start=(ic == 0), stop=(ic == OC - 1)))
            rv = p.inc("t", p.s_pe, 1)
            p.wait("s", p.s_pe, rv)
            if oc == 0 and self.prj_war > 0:
                p.wait("s", p.s_pe, self.prj_war)
            p.s(lambda e, b=b, oc=oc: e.activation(
                out=p.prj_fm[:, oc * TIL:(oc + 1) * TIL], in_=p.pool[b][:, :],
                func=AF.Identity, bias=p.bqkv[:, oc:oc + 1]))
            qe = p.inc("s", p.s_sc, 1)
            p.pool_release(b, p.s_sc, qe)
        self.xfm_war = p.s_pe.n
        return p.s_sc.n

    def phase_b_tile(self, s, t, qevac_done, nxt):
        p = self
        tb = t % 2
        firstb = (s == 0 and t == 0)
        xc = p.x_tm[tb][:, :].rearrange("p (c f) -> p c f", f=D)
        h_done = {}
        np_free = {}
        rec_val = {}
        prev = None
        p.wait("v", p.s_sc, qevac_done)
        if self.attn_war > 0:
            p.wait("v", p.s_pe, self.attn_war)
        exp_done = {}

        def emit_dash(h):
            hh = h % 2
            pr = h // 2
            r = h % 4
            gq3 = p.gq[r][:, :].rearrange("p (m n) -> p m n", n=TIL)
            for nh in range(2):
                b = p.pool_next()
                p.pool_acquire(b)
                p.wait("t", p.s_sc, qevac_done)
                for mc in range(2):
                    p.t(lambda e, b=b, mc=mc, hh=hh, pr=pr, nh=nh: e.matmul(
                        p.pool[b][:, mc * 256:(mc + 1) * 256],
                        p.projzq[:, hh * 256 + mc * 128: hh * 256 + (mc + 1) * 128],
                        p.prj_fm[:, pr * TIL + nh * 256: pr * TIL + (nh + 1) * 256],
                        start=True, stop=True))
                rv = p.inc("t", p.s_pe, 1)
                p.wait("s", p.s_pe, rv)
                if h >= 4 and nh == 0:
                    p.wait("s", p.s_pe, h_done[h - 4])
                elif firstb and h < 4 and nh == 0:
                    p.wait("s", p.s_ve, self.c1a_free)
                oap = gq3[:, :, nh * 256:(nh + 1) * 256]
                iap = p.pool[b][:, :].rearrange("p (m n) -> p m n", n=256)
                p.s(lambda e, oap=oap, iap=iap: e.activation(
                    out=oap, in_=iap, func=AF.Exp, scale=NORMZ, bias=-LN16))
                ev = p.inc("s", p.s_sc, 1)
                p.pool_release(b, p.s_sc, ev)
            exp_done[h] = p.s_sc.n

        emit_dash(0)
        for h in range(H):
            if h + 1 < H:
                emit_dash(h + 1)
            gq3 = p.gq[h % 4][:, :].rearrange("p (m n) -> p m n", n=TIL)
            npb = p.gen[h % 3]
            p.wait("t", p.s_sc, exp_done[h])
            if firstb and h == 0:
                p.wait("t", p.s_ve, self.bset_done)
            if h >= 3:
                p.wait("t", p.s_ve, np_free[h - 3])
            elif self.np_war[h % 3] > 0:
                p.wait("t", p.s_ve, self.np_war[h % 3])
            for ns in range(NCH):
                for mc in range(2):
                    p.t(lambda e, npb=npb, ns=ns, mc=mc, gq3=gq3, h=h, s=s: e.matmul(
                        npb[:, ns * 66:(ns + 1) * 66],
                        gq3[:, mc, ns * 128:(ns + 1) * 128],
                        p.c1e[s][mc][:, h * 66:(h + 1) * 66],
                        start=(mc == 0), stop=(mc == 1)))
            h_done[h] = p.inc("t", p.s_pe, 1)
            if True:
                p.wait("v", p.s_pe, h_done[h])
                srcA = npb[:, 0:264].rearrange("p (n j) -> p n j", j=66)[:, :, 64:66]
                dstA = p.qsden[:, h * 8:(h + 1) * 8].rearrange("p (n j) -> p n j", j=2)
                p.v(lambda e, srcA=srcA, dstA=dstA: e.tensor_copy(dstA, srcA))
                gat = p.inc("v", p.s_ve, 1)
                p.wait("s", p.s_ve, gat)
                qodd = p.qsden[:, h * 8:(h + 1) * 8].rearrange("p (n j) -> p n j", j=2)[:, :, 1]
                p.s(lambda e, h=h, qodd=qodd, s=s: e.activation(
                    out=p.tden[:, h * 4:(h + 1) * 4], in_=qodd, func=AF.Identity,
                    scale=p.nep[:, 0:1], bias=p.k2rep[s][:, h:h + 1]))
                td = p.inc("s", p.s_sc, 1)
                p.wait("v", p.s_sc, td)
                qev = p.qsden[:, h * 8:(h + 1) * 8].rearrange("p (n j) -> p n j", j=2)[:, :, 0]
                p.v(lambda e, h=h, qev=qev: e.tensor_tensor(
                    out=p.tden[:, h * 4:(h + 1) * 4], in0=p.tden[:, h * 4:(h + 1) * 4],
                    in1=qev, op=ALU.add))
                tav = p.inc("v", p.s_ve, 1)
                p.wait("v", p.s_ve, tav)
                p.v(lambda e, h=h: e.reciprocal(p.qd3[:, :, 16 + h],
                                                p.tden[:, h * 4:(h + 1) * 4]))
                rec_val[h] = p.inc("v", p.s_ve, 1)
                if prev is not None:
                    self._emit_head_tail(prev, rec_val[prev])
                    np_free[prev] = p.s_ve.n
                prev = h
        self._emit_head_tail(prev, rec_val[prev])
        np_free[prev] = p.s_ve.n
        self.np_war = [np_free[15], np_free[13], np_free[14]]
        self.prj_war = h_done[15]
        comb_done = p.s_ve.n
        pro_next = None
        if nxt is not None:
            pro_next = self.pro_b(nxt[0], nxt[1])
        for ns in range(NCH):
            bqd = p.pool_next()
            p.pool_acquire(bqd)
            p.wait("t", p.s_ve, comb_done)
            p.t(lambda e, b=bqd, ns=ns: e.transpose(
                p.pool[b][0:32, 0:128], p.qd3t[:, ns * 32:(ns + 1) * 32],
                p.identf[:, :]))
            rv = p.inc("t", p.s_pe, 1)
            p.wait("v", p.s_pe, rv)
            p.v(lambda e, b=bqd, ns=ns: e.tensor_copy(
                p.qd_fm[0:32, ns * 128:(ns + 1) * 128], p.pool[b][0:32, 0:128]))
            qdf = p.inc("v", p.s_ve, 1)
            p.pool_release(bqd, p.s_ve, qdf)
            af = p.attn_fm0
            tw = [(p.s_ve, comb_done)]
            if self.afm_war > 0:
                tw.append((p.s_pe, self.afm_war))
            _, tpe = self.emit_transposes_bf16(p.attn_tm, ns * D, af,
                                               extra_t_waits=tw)
            self.attn_war = tpe
            afm_done = p.s_sc.n
            li = p.lnin0
            lo = p.lnout0
            for half in range(2):
                b = p.pool_next()
                p.pool_acquire(b)
                p.wait("t", p.s_sc, afm_done)
                if half == 0:
                    p.wait("t", p.s_ve, qdf)
                for ic in range(OC):
                    p.t(lambda e, b=b, ic=ic, half=half, af=af: e.matmul(
                        p.pool[b][:, :], af[:, ic * 128:(ic + 1) * 128],
                        p.w_o[:, ic * D + half * 512: ic * D + (half + 1) * 512],
                        start=(ic == 0), stop=False))
                p.t(lambda e, b=b, half=half, ns=ns, s=s: e.matmul(
                    p.pool[b][:, :], p.qd_fm[0:32, ns * 128:(ns + 1) * 128],
                    p.v12t[s][0:32, half * 512:(half + 1) * 512],
                    start=False, stop=True))
                rv = p.inc("t", p.s_pe, 1)
                p.wait("v", p.s_pe, rv)
                if half == 0 and self.lnin_war is not None:
                    p.wait("v", *self.lnin_war)
                if firstb and ns == 0 and half == 0:
                    p.wait("v", *self.bo_done)
                p.v(lambda e, b=b, half=half, li=li: e.tensor_tensor(
                    out=li[:, half * 512:(half + 1) * 512], in0=p.pool[b][:, :],
                    in1=p.bo_rep[:, half * 512:(half + 1) * 512], op=ALU.add))
                wv = p.inc("v", p.s_ve, 1)
                p.pool_release(b, p.s_ve, wv)
            wo_done = p.s_ve.n
            self.afm_war = p.s_pe.n
            ap1 = self.emit_ln_chunk(tb, 4 + ns, li[:, :], lo[:, :],
                                     waits_v=[(p.s_ve, wo_done),
                                              (p.s_gp, self.lnout_war)],
                                     waits_s=[(p.s_ve, wo_done)])
            self.lnin_war = ap1
            p.wait("g", *ap1)
            p.g(lambda e, ns=ns, lo=lo, tb=tb: e.tensor_tensor(
                out=p.x_tm[tb][:, ns * D:(ns + 1) * D],
                in0=p.x_tm[tb][:, ns * D:(ns + 1) * D], in1=lo[:, :], op=ALU.add))
            x1_done = p.inc("g", p.s_gp, 1)
            x2t = p.xln2_tm[ns % 2]
            ap2 = self.emit_ln_chunk(tb, 8 + ns, p.x_tm[tb][:, ns * D:(ns + 1) * D],
                                     x2t[:, :],
                                     waits_v=[(p.s_gp, x1_done)],
                                     waits_s=[(p.s_gp, x1_done)])
            x2f = p.xln2_fm0
            tw = [ap2]
            if self.x2f_war > 0:
                tw.append((p.s_pe, self.x2f_war))
            self.emit_transposes(x2t, 0, x2f, 0, 128, extra_t_waits=tw)
            x2f_done = p.s_sc.n
            for half in range(2):
                b = p.pool_next()
                p.pool_acquire(b)
                p.wait("t", p.s_sc, x2f_done)
                if firstb and ns == 0 and half == 0:
                    p.wait("t", p.s_ve, self.wp_done)
                for ic in range(OC):
                    p.t(lambda e, b=b, ic=ic, half=half, x2f=x2f: e.matmul(
                        p.pool[b][:, :], x2f[:, ic * 128:(ic + 1) * 128],
                        p.w_p[:, ic * D + half * 512: ic * D + (half + 1) * 512],
                        start=(ic == 0), stop=(ic == OC - 1)))
                rv = p.inc("t", p.s_pe, 1)
                p.wait("v", p.s_pe, rv)
                if half == 0 and self.lnin_war is not None:
                    p.wait("v", *self.lnin_war)
                if firstb and ns == 0 and half == 0:
                    p.wait("v", *self.brep_done)
                p.v(lambda e, b=b, half=half, li=li: e.tensor_tensor(
                    out=li[:, half * 512:(half + 1) * 512], in0=p.pool[b][:, :],
                    in1=p.b_rep[:, half * 512:(half + 1) * 512], op=ALU.add))
                fv = p.inc("v", p.s_ve, 1)
                p.pool_release(b, p.s_ve, fv)
            ffn_done = p.s_ve.n
            self.x2f_war = p.s_pe.n
            ap3 = self.emit_ln_chunk(tb, 12 + ns, li[:, :], lo[:, :],
                                     waits_v=[(p.s_ve, ffn_done),
                                              (p.s_gp, x1_done)],
                                     waits_s=[(p.s_ve, ffn_done)])
            self.lnin_war = ap3
            p.wait("g", *ap3)
            p.wait("g", p.s_gp, x1_done)
            p.g(lambda e, ns=ns, lo=lo, tb=tb: e.tensor_tensor(
                out=p.x_tm[tb][:, ns * D:(ns + 1) * D],
                in0=p.x_tm[tb][:, ns * D:(ns + 1) * D], in1=lo[:, :], op=ALU.add))
            self.lnout_war = p.inc("g", p.s_gp, 1)
        out_ready = p.s_gp.n
        p.wait("s", p.s_gp, out_ready)
        src = p.x_tm[tb][:, :].rearrange("p (c f) -> p c f", f=D)
        dst = p.out_h.ap()[s, t * TIL:(t + 1) * TIL, :].rearrange("(c p) f -> p c f", p=128)
        p.s(lambda e, d=dst, s2=src: e.dma_start(out=d, in_=s2))
        stv = p.inc("s", p.s_o[tb], 16)
        p.x_rel[tb] = [(p.s_o[tb], stv)]
        return pro_next

    def schedule(self):
        p = self
        self.sched_loads()
        p.wait("s", *self.consts_done)
        if self.phases == "a0":
            xfm = self.prologue_tile(0, 0, first=True)
            self.phase_a_tile(0, 0, xfm, None)
            self._finish()
            return
        xfm = None
        for s in range(BPC):
            xfm = self.phase_a_seq(s, xfm)
        if self.phases == "a":
            self._finish()
            return
        self.sched_collective()
        self.sched_wprime()
        a_end = p.s_pe.n
        self.prj_war = a_end
        self.xfm_war = a_end
        if self.phases == "ac":
            self._finish()
            return
        p.wait("g", p.s_pe, a_end)
        for dst, nm, sem in ((self.wb0, "wqT", p.s_wq2), (self.wb1, "woT", p.s_wo2)):
            srcw = self.dt[nm].ap().rearrange("(c p) o -> p c o", p=128)
            dstw = dst[:, :].rearrange("p (c o) -> p c o", o=D)
            p.g(lambda e, d=dstw, s2=srcw: e.dma_start(out=d, in_=s2))
            p.inc("g", sem, 16)
        self.wq_done = (p.s_wq2, p.s_wq2.n)
        self.wo_done = (p.s_wo2, p.s_wo2.n)
        for s in range(BPC):
            self.sched_bsetup1(s)
        pro = self.pro_b(0, 0, firstb=True)
        for s in range(BPC):
            self.sched_bsetup2(s)
        self.np_war = [0, 0, self.psetup_read]
        if self.phases == "w":
            self._finish()
            return
        tiles = [(s, t) for s in range(BPC) for t in range(NT)]
        for i, (s, t) in enumerate(tiles):
            if i + 1 < len(tiles):
                nxt = (*tiles[i + 1], tiles[i + 2] if i + 2 < len(tiles) else None)
            else:
                nxt = None
            pro = self.phase_b_tile(s, t, pro, nxt)
        self._finish()

    def _finish(self):
        p = self
        if self.debug:
            p.wait("g", p.s_ve, p.s_ve.n)
            p.wait("g", p.s_sc, p.s_sc.n)
            p.wait("g", p.s_pe, p.s_pe.n)
            p.wait("g", p.s_gp, p.s_gp.n)
            order = ["a0", "a", "ac", "w", "b0", "b1", "full"]
            lvl = order.index(self.phases)
            taps = [("tap_prj", self.prj_fm), ("tap_vs0", self.vs_row[0])]
            if lvl >= 1:
                taps += [("tap_c1e0", self.c1e[0][0]), ("tap_c1e1", self.c1e[0][1])]
            if lvl >= 2:
                taps += [("tap_stab", self.stab_in), ("tap_eps", self.epsp)]
            if lvl >= 3:
                taps += [("tap_k2rep", self.k2rep[0]), ("tap_v12", self.v12t[0])]
            for nm, src in taps:
                p.g(lambda e, nm=nm, src=src: e.dma_start(
                    out=self.tap_h[nm].ap(), in_=src[:, :]))
                p.inc("g", p.s_tap, 16)
        p.wait("g", p.s_o[0], p.s_o[0].n)
        p.wait("g", p.s_o[1], p.s_o[1].n)
        p.wait("g", p.s_tap, p.s_tap.n)


def _projz(proj):
    z = np.zeros((128, 512), dtype=np.float32)
    z[0:64, 0:256] = proj.T
    z[64:128, 256:512] = proj.T
    return z


def _projzq(proj):
    z = np.zeros((128, 512), dtype=np.float32)
    z[0:64, 0:256] = proj.T
    z[64:128, 256:512] = proj.T
    return z


def make_inputs(inputs):
    f32 = np.float32
    x = np.ascontiguousarray(np.asarray(inputs["L0_attn_hidden_states"], dtype=f32))
    wq = np.asarray(inputs["L0_attn_Wq"], dtype=f32)
    wk = np.asarray(inputs["L0_attn_Wk"], dtype=f32)
    wv = np.asarray(inputs["L0_attn_Wv"], dtype=f32)
    wo = np.asarray(inputs["L0_attn_Wo"], dtype=f32)
    bq = np.asarray(inputs["L0_attn_bq"], dtype=f32)
    bk = np.asarray(inputs["L0_attn_bk"], dtype=f32)
    bv = np.asarray(inputs["L0_attn_bv"], dtype=f32)
    bo = np.asarray(inputs["L0_attn_bo"], dtype=f32)
    proj = np.asarray(inputs["L0_attn_proj"], dtype=f32)
    W = np.asarray(inputs["L0_ffn_W"], dtype=f32)
    bffn = np.asarray(inputs["L0_ffn_b"], dtype=f32)
    Vr = np.asarray(inputs["L0_ffn_V_r"], dtype=f32)
    C = np.asarray(inputs["L0_ffn_C"], dtype=f32)

    common = {
        "wqT": np.ascontiguousarray(wq.T), "wkT": np.ascontiguousarray(wk.T),
        "wvT": np.ascontiguousarray(wv.T), "woT": np.ascontiguousarray(wo.T),
        "wT": np.ascontiguousarray(W.T),
        "vrT": np.ascontiguousarray(Vr.T), "cT": np.ascontiguousarray(C.T),
        "projz": _projz(proj), "projzq": _projzq(proj),
        "identf": np.eye(128, dtype=f32),
        "identb": np.eye(128, dtype=f32),
        "cvals": np.ascontiguousarray(np.broadcast_to(
            np.array([EPS_LN, -LN16, LNEPSQ], dtype=f32), (128, 3))),
        "ones": np.ones((128, 128), dtype=f32),
        "bo_rep": np.ascontiguousarray(np.broadcast_to(bo, (128, D))),
        "b_rep": np.ascontiguousarray(np.broadcast_to(bffn, (128, D))),
        "bv_rep": np.ascontiguousarray(np.broadcast_to(bv, (128, D))),
    }
    bqkv = np.zeros((128, 3 * OC), dtype=f32)
    for i, bb in enumerate((bq, bk, bv)):
        bqkv[:, i * OC:(i + 1) * OC] = bb.reshape(OC, 128).T
    common["bqkv"] = bqkv
    ediag = np.zeros((128, 128), dtype=f32)
    for c in range(OC):
        ediag[0:64, c * 16 + 2 * c] = -0.5
        ediag[64:128, c * 16 + 2 * c + 1] = -0.5
    common["ediag"] = ediag

    in_maps = []
    for core in range(NCORES):
        m = dict(common)
        m["x"] = np.ascontiguousarray(x[core * BPC:(core + 1) * BPC])
        in_maps.append(m)
    return in_maps


_CACHE = {}


def get_nc(debug=False):
    key = ("nc", debug)
    if key not in _CACHE:
        b = Builder(debug=debug)
        b.emit()
        _CACHE[key] = b.nc
    return _CACHE[key]


def kernel(**inputs):
    in_maps = make_inputs(inputs)
    nc = get_nc()
    res = run_bass_kernel_spmd(nc, in_maps, core_ids=list(range(NCORES)))
    return np.concatenate([np.asarray(res.results[c]["out"]) for c in range(NCORES)],
                          axis=0)


# version 11
# speedup vs baseline: 1.3562x; 1.0143x over previous
import contextlib
import numpy as np
import ml_dtypes
import concourse.bass as bass
import concourse.mybir as mybir
from concourse.bass_utils import run_bass_kernel_spmd

F32 = mybir.dt.float32
BF16 = mybir.dt.bfloat16
AF = mybir.ActivationFunctionType
ALU = mybir.AluOpType
AXL = mybir.AxisListType

B, N, D = 16, 4096, 1024
H, DH, M = 16, 64, 256
NCORES = 8
BPC = B // NCORES
TIL = 512
NT = N // TIL
NCH = TIL // 128
OC = D // 128

NORMZ = float(DH ** -0.25)
LN16 = float(np.log(16.0))
EPSQ = float((M ** -0.5) * 1e-3)
LNEPSQ = float(np.log(EPSQ))
EPS_LN = 1e-5
INV_D = 1.0 / D


class Sem:
    def __init__(self, h):
        self.h = h
        self.n = 0


class Builder:
    def __init__(self, debug=False, phases="full", stop=99):
        self.debug = debug
        self.phases = phases
        self.stop = stop
        self.nc = bass.Bass("TRN2", target_bir_lowering=False)
        self.q = {k: [] for k in "tvsgy"}
        self._can_inc = {k: False for k in "tvsgy"}
        self.ctx = contextlib.ExitStack()
        self.dt = {}
        self.build()

    def _ins(self, eng, fn):
        self.q[eng].append(fn)
        self._can_inc[eng] = True

    def t(self, fn): self._ins("t", fn)
    def v(self, fn): self._ins("v", fn)
    def s(self, fn): self._ins("s", fn)
    def g(self, fn): self._ins("g", fn)
    def y(self, fn): self._ins("y", fn)

    def wait(self, eng, sem, val):
        if val > sem.n:
            raise RuntimeError(f"wait {val} > scheduled {sem.n}")
        if val > 0:
            self.q[eng].append(lambda e, s=sem.h, v=val: e.wait_ge(s, v))
            self._can_inc[eng] = False

    def inc(self, eng, sem, k=1):
        if not self._can_inc[eng]:
            raise RuntimeError("inc must ride on a fresh datapath instruction")
        fn = self.q[eng].pop()
        self.q[eng].append(lambda e, f=fn, s=sem.h, k=k: f(e).then_inc(s, k))
        self._can_inc[eng] = False
        sem.n += k
        return sem.n

    def emit(self):
        with self.nc.Block() as blk:
            for key, attr in (("y", "sync"), ("g", "gpsimd"), ("s", "scalar"),
                              ("v", "vector"), ("t", "tensor")):
                fns = self.q[key]
                if fns:
                    getattr(blk, attr)(lambda e, fns=fns: [f(e) for f in fns])

    def build(self):
        nc, ctx = self.nc, self.ctx
        sb = lambda name, shape, dtype: ctx.enter_context(nc.sbuf_tensor(name, shape, dtype))
        psr = lambda name, shape: ctx.enter_context(nc.psum_tensor(name, shape, F32))
        mks = lambda name: Sem(ctx.enter_context(nc.semaphore(name)))

        def din(name, shape):
            self.dt[name] = nc.dram_tensor(name, shape, F32, kind="ExternalInput")
        din("x", [BPC, N, D])
        for nm in ("wqT", "wkT", "wvT", "woT", "wT"):
            din(nm, [D, D])
        din("vrT", [32, D]); din("cT", [32, D])
        din("bqkv", [128, 3 * OC])
        din("bo_rep", [128, D]); din("b_rep", [128, D]); din("bv_rep", [128, D])
        din("projz", [128, 512]); din("projzq", [128, 512])
        din("identf", [128, 128])
        din("ediag", [128, 128]); din("ones", [128, 128])
        din("identb", [128, 128]); din("cvals", [128, 3])
        self.out_h = nc.dram_tensor("out", [BPC, N, D], F32, kind="ExternalOutput")
        self.ccin = nc.dram_tensor("ccin", [1, 16], F32)
        self.ccout = nc.dram_tensor("ccout", [1, 16], F32)
        if self.debug:
            self.tap_h = {nm: nc.dram_tensor(nm, shape, F32, kind="ExternalOutput")
                          for nm, shape in [
                              ("tap_c1e0", [128, H * 66]), ("tap_c1e1", [128, H * 66]),
                              ("tap_vs0", [1, D]), ("tap_stab", [1, 16]),
                              ("tap_prj", [128, OC * TIL]),
                              ("tap_k2rep", [128, H]), ("tap_v12", [32, D]),
                              ("tap_eps", [128, 1])]}

        self.wb0 = sb("wb0", [128, OC * D], BF16)
        self.wb1 = sb("wb1", [128, OC * D], BF16)
        self.wb2 = sb("wb2", [128, OC * D], BF16)
        self.w_k = self.w_q = self.wb0
        self.w_v = self.w_o = self.wb1
        self.w_p = self.wb2
        self.projz = sb("projzS", [128, 512], BF16)
        self.projzq = sb("projzqS", [128, 512], BF16)
        self.identf = sb("identfS", [128, 128], F32)
        self.ediag = sb("ediagS", [128, 128], BF16)
        self.onesb = sb("onesbS", [128, 128], BF16)
        self.bqkv = sb("bqkvS", [128, 3 * OC], F32)
        self.cvalsb = sb("cvalsbS", [128, 3], F32)
        nc.const_aps.register(F32, EPS_LN, self.cvalsb[:, 0:1])
        nc.const_aps.register(F32, -LN16, self.cvalsb[:, 1:2])
        nc.const_aps.register(F32, LNEPSQ, self.cvalsb[:, 2:3])
        self.bo_rep = sb("bo_repS", [128, D], F32)
        self.b_rep = sb("b_repS", [128, D], F32)
        self.bv_rep = sb("bv_repS", [128, D], F32)
        self.onesf = sb("onesfS", [1, 128], F32)
        self.vr_sb = sb("vr_sb", [32, D], BF16)
        self.c_sb = sb("c_sb", [32, D], BF16)

        self.c1a = sb("c1a", [128, 32 * 65], F32)
        self.c1e = [[sb(f"c1e_{s}_{mc}", [128, H * 66], BF16) for mc in range(2)]
                    for s in range(BPC)]
        self.vs_row = [sb(f"vs_{s}", [1, D], F32) for s in range(BPC)]
        self.vst_fm = [sb(f"vst_{s}", [128, OC], F32) for s in range(BPC)]
        self.usb = [sb(f"usb_{s}", [128, 256], BF16) for s in range(BPC)]
        self.v12t = [sb(f"v12t_{s}", [32, D], BF16) for s in range(BPC)]
        self.k2rep = [sb(f"k2rep_{s}", [128, H], F32) for s in range(BPC)]
        self.stabv = sb("stabvS", [1, 16], F32)
        self.stab_in = sb("stab_inS", [1, 16], F32)
        self.stab_rep = sb("stab_repS", [128, 16], F32)
        self.epsp = sb("epspS", [128, 1], F32)
        self.nep = sb("nepS", [128, 1], F32)
        self.epnk = sb("epnkS", [128, 1], F32)
        self.evst = sb("evstS", [128, OC], F32)
        self.evs256 = sb("evs256S", [128, OC], F32)
        self.k1_fm = sb("k1_fmS", [128, OC], F32)
        self.cs_sb = sb("cs_sbS", [1, 1024], BF16)
        self.csfm2 = [sb(f"csfmS{i}", [128, OC], F32) for i in range(BPC)]
        self.csb_w = 0
        self.cs_marks = {}
        self.k2s_marks = {}
        self.stabsc = sb("stabscS", [1, 2], F32)

        self.x_tm = [sb(f"x_tm{i}", [128, NCH * D], F32) for i in range(2)]
        self.xln_tm = [sb(f"xln_tm{i}", [128, D], F32) for i in range(2)]
        self.xln_fm0 = sb("xln_fm0", [128, OC * TIL], BF16)
        self.prj_fm = sb("prj_fmS", [128, OC * TIL], BF16)
        self.k2b = [sb(f"k2b{i}", [128, TIL], BF16) for i in range(2)]
        self.e1b0 = [sb(f"e1b0_{i}", [128, TIL], BF16) for i in range(4)]
        self.vxb = sb("vxbS", [128, NCH * H * 65], BF16)
        self.edrep0 = sb("edrep0", [128, H * 65], BF16)
        self.ed_tm = sb("ed_tmS", [128, NCH * 16], F32)
        self.ed_rows = sb("ed_rowsS", [16, TIL], F32)
        self.dmaxb = sb("dmaxbS", [16, NT], F32)
        self.dmax1 = sb("dmax1S", [16, 1], F32)
        self.stat = [sb(f"stat{i}", [128, 4 * 16], F32) for i in range(2)]
        self.bst = sb("bst", [128, 12], F32)
        self.bst_war = 0
        self.junk = sb("junkS", [128, 4], F32)
        c1a_addr = nc.lookup_mloc(self.c1a).addr
        self.gq = [nc.alloc_sbuf_tensor_at(f"gq{i}", [128, 2 * TIL], BF16,
                                           offset=c1a_addr + i * 2048)
                   for i in range(4)]
        self.qd3t = sb("qd3S", [128, NCH * 32], F32)
        self.qsden = sb("qsdenS", [128, H * 8], F32)
        self.tden = sb("tdenS", [128, H * 4], F32)
        self.attn_tm = sb("attn_tmS", [128, NCH * D], BF16)
        at_addr = nc.lookup_mloc(self.attn_tm).addr
        self.e1b = [self.e1b0,
                    [nc.alloc_sbuf_tensor_at(f"e1b1_{i}", [128, TIL], BF16,
                                             offset=at_addr + i * 1024)
                     for i in range(4)]]
        self.identb = sb("identbS", [128, 128], BF16)
        self.attn_fm0 = sb("attn_fm0", [128, OC * 128], BF16)
        self.qd_fm = sb("qd_fmS", [32, NCH * 128], BF16)
        self.lnin0 = sb("lnin0", [128, D], F32)
        self.lnout0 = sb("lnout0", [128, D], F32)
        self.xln2_tm = [sb(f"xln2_tm{i}", [128, D], F32) for i in range(2)]
        self.xln2_fm0 = sb("xln2_fm0", [128, OC * 128], BF16)
        self.wtstage = self.xln2_tm

        self.pool = [psr(f"poolP{i}", [128, 512]) for i in range(3)]
        self.gen = [psr(f"genP{i}", [128, 512]) for i in range(3)]
        self.pbf = ctx.enter_context(nc.psum_tensor("pbfP", [128, 1024], BF16))

        self.s_x = [mks("s_x0"), mks("s_x1")]
        self.s_cst = mks("s_cst"); self.s_wk = mks("s_wk"); self.s_wv = mks("s_wv")
        self.s_vrc = mks("s_vrc"); self.s_bo = mks("s_bo"); self.s_br = mks("s_br")
        self.s_wq2 = mks("s_wq2"); self.s_wo2 = mks("s_wo2")
        self.s_wt = [mks("s_wt0"), mks("s_wt1")]
        self.s_o = [mks("s_o0"), mks("s_o1")]
        self.s_pe = mks("s_pe"); self.s_sc = mks("s_sc"); self.s_ve = mks("s_ve")
        self.s_gp = mks("s_gp"); self.s_cc = mks("s_cc"); self.s_tap = mks("s_tap")
        self.s_ccr = mks("s_ccr")
        self.pool_mark = [None, None, None]
        self.pbf_mark = None
        self._rr = 0

        self.vx3 = self.vxb[:, :].rearrange("p (c h j) -> p c h j", c=NCH, j=65)
        self.bv3 = self.bv_rep[:, :].rearrange("p (h j) -> p h j", j=64)
        self.qd3 = self.qd3t[:, :].rearrange("p (c k) -> p c k", k=32)

        self.xpref = {}
        self.x_rel = [None, None]
        self.xlnT_mark = [0, 0]
        self.xfm_war = 0
        self.prj_war = 0
        self.pdiag_war = 0
        self.edr_war = 0
        self.vx_war = 0
        self.k2_war = 0
        self.c1mini_free = [0, 0]
        self.c1aw = [0] * 8
        self.vs_war = [0, 0]
        self.afm_war = 0
        self.attn_war = 0
        self.x2f_war = 0
        self.lnin_war = None
        self.lnout_war = 0
        self.np_war = [0, 0, 0]
        self.last_c1pair = {}
        self.ln_mark = [dict(), dict()]

        self.schedule()

    def pool_next(self):
        b = self._rr % 3
        self._rr += 1
        return b

    def pool_acquire(self, bank):
        m = self.pool_mark[bank]
        if m is not None:
            self.wait("t", m[0], m[1])

    def pool_release(self, bank, sem, val):
        self.pool_mark[bank] = (sem, val)

    def emit_ln_chunk(self, tb, slot, src_ap, dst_ap, waits_v=(), waits_s=(),
                      apply_eng="v"):
        p = self
        st = self.stat[tb]
        base = slot * 4
        cMV, cLNV, cRS = base, base + 2, base + 3
        bst = self.bst
        war = self.ln_mark[tb].get(slot, 0)
        for sem, val in waits_v:
            p.wait("v", sem, val)
        if war:
            p.wait("v", p.s_sc, war)
        if self.bst_war:
            p.wait("v", p.s_ve, self.bst_war)
        p.v(lambda e, a=src_ap: e.bn_stats(bst[:, 0:6], a[:, 0:512]))
        p.v(lambda e, a=src_ap: e.bn_stats(bst[:, 6:12], a[:, 512:1024]))
        bsv = p.inc("v", p.s_ve, 1)
        p.wait("v", p.s_ve, bsv)
        p.v(lambda e: e.bn_aggr(st[:, cMV:cMV + 2], bst[:, 0:12]))
        agv = p.inc("v", p.s_ve, 1)
        self.bst_war = agv
        for sem, val in waits_s:
            p.wait("s", sem, val)
        p.wait("s", p.s_ve, agv)
        p.s(lambda e: e.activation(out=st[:, cLNV:cLNV + 1], in_=st[:, cMV + 1:cMV + 2],
                                   func=AF.Ln, bias=EPS_LN))
        lnv = p.inc("s", p.s_sc, 1)
        p.wait("s", p.s_sc, lnv)
        p.s(lambda e: e.activation(out=st[:, cRS:cRS + 1], in_=st[:, cLNV:cLNV + 1],
                                   func=AF.Exp, scale=-0.5))
        rstd = p.inc("s", p.s_sc, 1)
        cM = cMV
        self.ln_mark[tb][slot] = rstd
        sem = {"v": p.s_ve, "g": p.s_gp}[apply_eng]
        p.wait(apply_eng, p.s_sc, rstd)
        for w in waits_v:
            if apply_eng != "v":
                p.wait(apply_eng, *w)
        p.q[apply_eng].append(lambda e, a=src_ap, o=dst_ap: e.tensor_scalar(
            out=o, in0=a, scalar1=st[:, cM:cM + 1], scalar2=st[:, cRS:cRS + 1],
            op0=ALU.subtract, op1=ALU.mult))
        p._can_inc[apply_eng] = True
        return (sem, p.inc(apply_eng, sem, 1))

    def emit_transposes(self, src_t, src_off, dst_fm, dst_base, dst_stride,
                        extra_t_waits=()):
        p = self
        last_sc = last_pe = None
        d3 = dst_fm[:, :].rearrange("p (o k) -> p o k", k=dst_stride)
        b0 = dst_base // dst_stride
        co = dst_base % dst_stride
        for g2 in range(2):
            b = p.pool_next()
            p.pool_acquire(b)
            for sem, val in extra_t_waits:
                p.wait("t", sem, val)
            for j in range(4):
                oc = g2 * 4 + j
                p.t(lambda e, b=b, j=j, off=src_off + oc * 128, s=src_t:
                    e.transpose(p.pool[b][:, j * 128:(j + 1) * 128],
                                s[:, off:off + 128], p.identf[:, :]))
            last_pe = p.inc("t", p.s_pe, 1)
            p.wait("s", p.s_pe, last_pe)
            oap = d3[:, b0 + g2 * 4: b0 + (g2 + 1) * 4, co:co + 128]
            iap = p.pool[b][:, :].rearrange("p (o k) -> p o k", k=128)
            p.s(lambda e, oap=oap, iap=iap: e.copy(oap, iap))
            last_sc = p.inc("s", p.s_sc, 1)
            p.pool_release(b, p.s_sc, last_sc)
        return last_sc, last_pe

    def emit_transposes_bf16(self, src_t, src_off, dst_fm, extra_t_waits=(),
                             extra_s_waits=()):
        p = self
        if self.pbf_mark is not None:
            p.wait("t", *self.pbf_mark)
        for sem, val in extra_t_waits:
            p.wait("t", sem, val)
        for oc in range(OC):
            p.t(lambda e, off=src_off + oc * 128, oc=oc, s=src_t:
                e.transpose(p.pbf[:, oc * 128:(oc + 1) * 128],
                            s[:, off:off + 128], p.identb[:, :]))
        pe = p.inc("t", p.s_pe, 1)
        p.wait("s", p.s_pe, pe)
        for sem, val in extra_s_waits:
            p.wait("s", sem, val)
        p.s(lambda e, d=dst_fm: e.copy(d[:, 0:OC * 128], p.pbf[:, :]))
        sc = p.inc("s", p.s_sc, 1)
        self.pbf_mark = (p.s_sc, sc)
        return sc, pe

    def sched_loads(self):
        p = self

        def gload(dst, name, sem, rearr=False):
            src = self.dt[name].ap()
            if rearr:
                src = src.rearrange("(c p) o -> p c o", p=128)
                dstap = dst[:, :].rearrange("p (c o) -> p c o", o=D)
            else:
                dstap = dst[:, :]
            p.g(lambda e, d=dstap, s=src: e.dma_start(out=d, in_=s))
            return (sem, p.inc("g", sem, 16))

        for dst, nm in ((self.identf, "identf"), (self.projz, "projz"),
                        (self.projzq, "projzq"),
                        (self.ediag, "ediag"), (self.onesb, "ones"),
                        (self.bqkv, "bqkv"), (self.bv_rep, "bv_rep"),
                        (self.identb, "identb"), (self.cvalsb, "cvals")):
            gload(dst, nm, p.s_cst)
        p.g(lambda e: e.dma_start(out=self.onesf[0:1, :], in_=self.dt["ones"].ap()[0:1, :]))
        p.inc("g", p.s_cst, 16)
        self.consts_done = (p.s_cst, p.s_cst.n)
        self.wk_done = gload(self.w_k, "wkT", p.s_wk, True)
        self.wv_done = gload(self.w_v, "wvT", p.s_wv, True)
        gload(self.vr_sb, "vrT", p.s_vrc)
        self.vrc_done = gload(self.c_sb, "cT", p.s_vrc)
        self.bo_done = gload(self.bo_rep, "bo_rep", p.s_bo)
        self.brep_done = gload(self.b_rep, "b_rep", p.s_br)

    def load_x_tile(self, s, t):
        p = self
        key = (s, t)
        if key in p.xpref:
            return p.xpref.pop(key)
        tb = t % 2
        if p.x_rel[tb] is not None:
            for sem, val in p.x_rel[tb]:
                p.wait("y", sem, val)
        src = p.dt["x"].ap()[s, t * TIL:(t + 1) * TIL, :].rearrange("(c p) f -> p c f", p=128)
        dst = p.x_tm[tb][:, :].rearrange("p (c f) -> p c f", f=D)
        p.y(lambda e, d=dst, s2=src: e.dma_start(out=d, in_=s2))
        mark = (p.s_x[tb], p.inc("y", p.s_x[tb], 16))
        return mark

    def prefetch_x(self, s, t):
        if (s, t) not in self.xpref:
            self.xpref[(s, t)] = self.load_x_tile(s, t)

    def prologue_tile(self, s, t, first=False):
        p = self
        tb = t % 2
        x_done = self.load_x_tile(s, t)
        xc = p.x_tm[tb][:, :].rearrange("p (c f) -> p c f", f=D)
        for c in range(NCH):
            waits_v = [x_done]
            if p.xlnT_mark[c % 2] > 0:
                waits_v.append((p.s_pe, p.xlnT_mark[c % 2]))
            ap_done = self.emit_ln_chunk(tb, c, xc[:, c, :], p.xln_tm[c % 2][:, :],
                                         waits_v=waits_v, waits_s=[x_done])
            tw = [ap_done]
            if first and c == 0:
                tw.append(self.consts_done)
            if self.xfm_war > 0:
                tw.append((p.s_pe, self.xfm_war))
            _, tpe = self.emit_transposes(p.xln_tm[c % 2], 0, p.xln_fm0,
                                          c * 128, TIL, extra_t_waits=tw)
            p.xlnT_mark[c % 2] = tpe
        p.x_rel[tb] = [(p.s_ve, p.s_ve.n), (p.s_sc, p.s_sc.n)]
        return p.s_sc.n

    def phase_a_tile(self, s, t, xfm_done, nxt):
        p = self
        tb = t % 2
        first = (s == 0 and t == 0)
        if self.stop <= 1:
            return None
        diag_mm = {}
        k2dv = {}
        for oc in range(OC):
            b = p.pool_next()
            p.pool_acquire(b)
            p.wait("t", p.s_sc, xfm_done)
            if first and oc == 0:
                p.wait("t", *self.wk_done)
            for ic in range(OC):
                p.t(lambda e, b=b, ic=ic, oc=oc: e.matmul(
                    p.pool[b][:, :], p.w_k[:, ic * D + oc * 128: ic * D + (oc + 1) * 128],
                    p.xln_fm0[:, ic * TIL:(ic + 1) * TIL],
                    start=(ic == 0), stop=(ic == OC - 1)))
            rv = p.inc("t", p.s_pe, 1)
            p.wait("s", p.s_pe, rv)
            if oc == 0 and self.prj_war > 0:
                p.wait("s", p.s_pe, self.prj_war)
            p.s(lambda e, b=b, oc=oc: e.activation(
                out=p.prj_fm[:, oc * TIL:(oc + 1) * TIL], in_=p.pool[b][:, :],
                func=AF.Identity, bias=p.bqkv[:, OC + oc:OC + oc + 1]))
            p.inc("s", p.s_sc, 1)
            if oc >= 2:
                p.wait("s", p.s_pe, diag_mm[oc - 2])
            elif self.k2_war > 0:
                p.wait("s", p.s_pe, self.k2_war)
            p.s(lambda e, b=b, oc=oc: e.activation(
                out=p.k2b[oc % 2][:, :], in_=p.pool[b][:, :],
                func=AF.Square, bias=p.bqkv[:, OC + oc:OC + oc + 1]))
            k2d = p.inc("s", p.s_sc, 1)
            p.pool_release(b, p.s_sc, k2d)
            k2dv[oc] = k2d

            def emit_diag(oc2):
                p.wait("t", p.s_sc, k2dv[oc2])
                if oc2 == 0:
                    if self.pdiag_war > 0:
                        p.wait("t", p.s_ve, self.pdiag_war)
                    if self.edr_war > 0:
                        p.wait("t", p.s_sc, self.edr_war)
                p.t(lambda e, oc2=oc2: e.matmul(
                    p.gen[2][0:16, :], p.ediag[:, oc2 * 16:(oc2 + 1) * 16],
                    p.k2b[oc2 % 2][:, :],
                    start=(oc2 == 0), stop=(oc2 == OC - 1)))
                diag_mm[oc2] = p.inc("t", p.s_pe, 1)

            if oc >= 1:
                emit_diag(oc - 1)
            if oc == OC - 1:
                emit_diag(oc)
        self.k2_war = diag_mm[6]
        kevac_done = p.s_sc.n
        if self.stop <= 2:
            return None
        for c in range(NCH):
            for half in range(2):
                b = p.pool_next()
                p.pool_acquire(b)
                if first and c == 0 and half == 0:
                    p.wait("t", *self.wv_done)
                for ic in range(OC):
                    p.t(lambda e, b=b, ic=ic, c=c, half=half: e.matmul(
                        p.pool[b][:, :],
                        p.xln_fm0[:, ic * TIL + c * 128: ic * TIL + (c + 1) * 128],
                        p.w_v[:, ic * D + half * 512: ic * D + (half + 1) * 512],
                        start=(ic == 0), stop=(ic == OC - 1)))
                rv = p.inc("t", p.s_pe, 1)
                p.wait("v", p.s_pe, rv)
                if c == 0 and half == 0:
                    if first:
                        p.wait("v", *self.consts_done)
                    if self.vx_war > 0:
                        p.wait("v", p.s_pe, self.vx_war)
                iap = p.pool[b][:, :].rearrange("p (h j) -> p h j", j=64)
                p.v(lambda e, iap=iap, c=c, half=half: e.tensor_tensor(
                    out=p.vx3[:, c, half * 8:(half + 1) * 8, 0:64], in0=iap,
                    in1=p.bv3[:, half * 8:(half + 1) * 8, :], op=ALU.add))
                vv = p.inc("v", p.s_ve, 1)
                p.pool_release(b, p.s_ve, vv)
        self.xfm_war = p.s_pe.n
        p.v(lambda e: e.memset(p.vx3[:, :, :, 64:65], 1.0))
        vx_done = p.inc("v", p.s_ve, 1)
        if self.stop <= 3:
            return None
        p.wait("s", p.s_pe, diag_mm[OC - 1])
        p.s(lambda e: e.activation(out=p.ed_rows[:, :], in_=p.gen[2][0:16, :], func=AF.Exp))
        edr_done = p.inc("s", p.s_sc, 1)
        self.edr_war = edr_done
        p.wait("v", p.s_pe, diag_mm[OC - 1])
        p.v(lambda e, t=t: e.reduce_max(p.dmaxb[0:16, t:t + 1], p.gen[2][0:16, :],
                                        axis=AXL.X))
        self.pdiag_war = p.inc("v", p.s_ve, 1)
        b = p.pool_next()
        p.pool_acquire(b)
        p.wait("t", p.s_sc, edr_done)
        for c in range(NCH):
            p.t(lambda e, b=b, c=c: e.transpose(
                p.pool[b][:, c * 16:(c + 1) * 16],
                p.ed_rows[0:16, c * 128:(c + 1) * 128], p.identf[0:16, 0:16]))
        rv = p.inc("t", p.s_pe, 1)
        p.wait("v", p.s_pe, rv)
        p.v(lambda e, b=b: e.tensor_copy(p.ed_tm[:, :], p.pool[b][:, 0:NCH * 16]))
        edt = p.inc("v", p.s_ve, 1)
        p.pool_release(b, p.s_ve, edt)
        for half in range(2):
            b = p.pool_next()
            p.pool_acquire(b)
            p.wait("t", p.s_ve, vx_done)
            for c in range(NCH):
                p.t(lambda e, b=b, c=c, half=half: e.matmul(
                    p.pool[b][0:1, :], p.onesb[:, 0:1],
                    p.vx3[:, c, half * 8:(half + 1) * 8, 0:64],
                    start=(c == 0), stop=(c == NCH - 1)))
            rv = p.inc("t", p.s_pe, 1)
            p.wait("v", p.s_pe, rv)
            if self.vs_war[half] > 0:
                p.wait("v", p.s_ve, self.vs_war[half])
            if t == 0:
                p.v(lambda e, b=b, half=half, s=s: e.tensor_copy(
                    p.vs_row[s][:, half * 512:(half + 1) * 512], p.pool[b][0:1, :]))
            else:
                p.v(lambda e, b=b, half=half, s=s: e.tensor_tensor(
                    out=p.vs_row[s][:, half * 512:(half + 1) * 512],
                    in0=p.vs_row[s][:, half * 512:(half + 1) * 512],
                    in1=p.pool[b][0:1, :], op=ALU.add))
            self.vs_war[half] = p.inc("v", p.s_ve, 1)
            p.pool_release(b, p.s_ve, self.vs_war[half])
        vs_mm_done = p.s_pe.n
        if self.stop <= 4:
            return None
        p.wait("v", p.s_pe, vs_mm_done)
        p.wait("v", p.s_ve, edt)
        mul_prev = 0
        for c in range(NCH):
            er = p.edrep0
            src = p.ed_tm[:, c * 16:(c + 1) * 16].broadcast_to([128, 16, 65])
            if mul_prev > 0:
                p.wait("v", p.s_ve, mul_prev)
            p.v(lambda e, er=er, src=src: e.tensor_copy(
                er[:, :].rearrange("p (h j) -> p h j", j=65), src))
            erv = p.inc("v", p.s_ve, 1)
            p.wait("v", p.s_ve, erv)
            p.v(lambda e, er=er, c=c: e.tensor_tensor(
                out=p.vx3[:, c, :, :], in0=p.vx3[:, c, :, :],
                in1=er[:, :].rearrange("p (h j) -> p h j", j=65), op=ALU.mult))
            mul_prev = p.inc("v", p.s_ve, 1)
        vp_done = p.s_ve.n
        if self.stop <= 5:
            return None
        nxt_mark = None
        if nxt is not None:
            nxt_mark = self.prologue_tile(nxt[0], nxt[1])
        c1grp = dict(self.last_c1pair)
        exp_done = {}

        def emit_c1(pr):
            mb = pr % 2
            p.wait("t", p.s_sc, exp_done[pr])
            if pr == 0:
                p.wait("t", p.s_ve, vp_done)
            if self.c1mini_free[mb] > 0:
                p.wait("t", p.s_ve, self.c1mini_free[mb])
            for hh in range(2):
                for mc in range(2):
                    h = 2 * pr + hh
                    soff = (hh * 2 + mc) * 65
                    for c in range(NCH):
                        p.t(lambda e, mb=mb, soff=soff, c=c, hh=hh, h=h, pr=pr:
                            e.matmul(p.gen[mb][:, soff:soff + 65],
                                     p.e1b[pr % 2][c][:, hh * 256 + (soff // 65 % 2) * 128: hh * 256 + (soff // 65 % 2) * 128 + 128],
                                     p.vx3[:, c, h, :],
                                     start=(c == 0), stop=(c == NCH - 1)))
            c1grp[pr] = p.inc("t", p.s_pe, 1)
            p.wait("v", p.s_pe, c1grp[pr])
            if self.c1aw[pr] > 0:
                p.wait("v", p.s_ve, self.c1aw[pr])
            if t == 0:
                p.v(lambda e, mb=mb, pr=pr: e.tensor_copy(
                    p.c1a[:, pr * 260:(pr + 1) * 260], p.gen[mb][:, 0:260]))
            else:
                p.v(lambda e, mb=mb, pr=pr: e.tensor_tensor(
                    out=p.c1a[:, pr * 260:(pr + 1) * 260],
                    in0=p.c1a[:, pr * 260:(pr + 1) * 260],
                    in1=p.gen[mb][:, 0:260], op=ALU.add))
            self.c1aw[pr] = p.inc("v", p.s_ve, 1)
            self.c1mini_free[mb] = self.c1aw[pr]

        for pr in range(8):
            for c in range(NCH):
                b = p.pool_next()
                p.pool_acquire(b)
                col = pr * TIL + c * 128
                p.wait("t", p.s_sc, kevac_done)
                p.t(lambda e, b=b, col=col: e.matmul(
                    p.pool[b][:, :], p.prj_fm[:, col:col + 128], p.projz[:, :]))
                rv = p.inc("t", p.s_pe, 1)
                eb = p.e1b[pr % 2][c]
                p.wait("s", p.s_pe, rv)
                war = c1grp.get(pr - 2)
                if war:
                    p.wait("s", p.s_pe, war)
                p.s(lambda e, b=b, eb=eb: e.activation(
                    out=eb[:, :], in_=p.pool[b][:, :], func=AF.Exp,
                    scale=NORMZ, bias=-LN16))
                ev = p.inc("s", p.s_sc, 1)
                p.pool_release(b, p.s_sc, ev)
            exp_done[pr] = p.s_sc.n
            if pr >= 1:
                emit_c1(pr - 1)
        emit_c1(7)
        self.vx_war = c1grp[7]
        self.last_c1pair = {-2: c1grp[6], -1: c1grp[7]}
        self.prj_war = c1grp[7]
        self.last_c1evac = max(self.c1aw)
        return nxt_mark

    def phase_a_seq(self, s, xfm_first=None):
        p = self
        xfm = xfm_first
        for t in range(NT):
            if xfm is None:
                xfm = self.prologue_tile(s, t, first=(s == 0 and t == 0))
            if t + 1 < NT:
                nxt = (s, t + 1, (s, t + 2) if t + 2 < NT else
                       ((s + 1, 0) if s + 1 < BPC else None))
            elif s + 1 < BPC:
                nxt = (s + 1, 0, (s + 1, 1))
            else:
                nxt = None
            xfm = self.phase_a_tile(s, t, xfm, nxt)
        p.wait("v", p.s_ve, self.last_c1evac)
        for mc in range(2):
            iap = p.c1a[:, :].rearrange("p (h j) -> p h j", j=130)[:, :, mc * 65:(mc + 1) * 65]
            oap = p.c1e[s][mc][:, :].rearrange("p (h j) -> p h j", j=66)[:, :, 0:65]
            p.v(lambda e, iap=iap, oap=oap: e.tensor_copy(oap, iap))
            p.inc("v", p.s_ve, 1)
            ap1 = p.c1e[s][mc][:, :].rearrange("p (h j) -> p h j", j=66)[:, :, 65:66]
            p.v(lambda e, ap1=ap1: e.memset(ap1, 1.0))
            p.inc("v", p.s_ve, 1)
        c1e_done = p.s_ve.n
        self.c1a_free = c1e_done
        self.c1aw = [0] * 8
        b = p.pool_next()
        p.pool_acquire(b)
        p.wait("t", p.s_ve, c1e_done)
        for oc in range(OC):
            p.t(lambda e, b=b, oc=oc, s=s: e.transpose(
                p.pool[b][:, oc:oc + 1], p.vs_row[s][0:1, oc * 128:(oc + 1) * 128],
                p.identf[0:1, 0:1]))
        rv = p.inc("t", p.s_pe, 1)
        p.wait("v", p.s_pe, rv)
        p.v(lambda e, b=b, s=s: e.tensor_copy(p.vst_fm[s][:, :], p.pool[b][:, 0:OC]))
        vst = p.inc("v", p.s_ve, 1)
        p.pool_release(b, p.s_ve, vst)
        self.vs_war = [0, 0]
        self.vst_done = vst
        p.v(lambda e: e.reduce_max(p.dmax1[0:16, 0:1], p.dmaxb[0:16, 0:NT], axis=AXL.X))
        dm = p.inc("v", p.s_ve, 1)
        b = p.pool_next()
        p.pool_acquire(b)
        p.wait("t", p.s_ve, dm)
        p.t(lambda e, b=b: e.transpose(p.pool[b][0:1, 0:16], p.dmax1[0:16, 0:1],
                                       p.identf[0:16, 0:16]))
        rv = p.inc("t", p.s_pe, 1)
        p.wait("v", p.s_pe, rv)
        p.v(lambda e, b=b, s=s: e.reduce_max(p.stabsc[0:1, s:s + 1],
                                             p.pool[b][0:1, 0:16], axis=AXL.X))
        sv = p.inc("v", p.s_ve, 1)
        p.pool_release(b, p.s_ve, sv)
        return xfm

    def sched_collective(self):
        p = self
        p.wait("v", p.s_ve, p.s_ve.n)
        p.v(lambda e: e.reduce_max(p.stabv[0:1, 0:1], p.stabsc[0:1, 0:2], axis=AXL.X))
        smx = p.inc("v", p.s_ve, 1)
        p.wait("v", p.s_ve, smx)
        src = p.stabv[0:1, 0:1].broadcast_to([1, 1, 15])
        dst = p.stabv[0:1, 1:16].rearrange("p (a j) -> p a j", a=1)
        p.v(lambda e, src=src, dst=dst: e.tensor_copy(dst, src))
        st_done = p.inc("v", p.s_ve, 1)
        p.wait("g", p.s_ve, st_done)
        p.g(lambda e: e.dma_start(out=p.ccin.ap(), in_=p.stabv[0:1, :]))
        p.inc("g", p.s_cc, 16)
        p.wait("g", p.s_cc, 16)
        p.g(lambda e: e.collective_compute(
            "AllReduce", ALU.max, replica_groups=[list(range(NCORES))],
            ins=[p.ccin.ap().opt()], outs=[p.ccout.ap().opt()]))
        p.inc("g", p.s_ccr, 1)
        p.wait("g", p.s_ccr, 1)
        p.g(lambda e: e.dma_start(out=p.stab_in[0:1, :], in_=p.ccout.ap()))
        p.inc("g", p.s_cc, 16)
        p.wait("g", p.s_cc, 32)
        b = p.pool_next()
        p.pool_acquire(b)
        p.wait("t", p.s_cc, 32)
        p.t(lambda e, b=b: e.matmul(p.pool[b][:, 0:16], p.onesf[0:1, :],
                                    p.stab_in[0:1, :]))
        rv = p.inc("t", p.s_pe, 1)
        p.wait("v", p.s_pe, rv)
        p.v(lambda e, b=b: e.tensor_copy(p.stab_rep[:, :], p.pool[b][:, 0:16]))
        bc = p.inc("v", p.s_ve, 1)
        p.pool_release(b, p.s_ve, bc)
        p.wait("s", p.s_ve, bc)
        p.s(lambda e: e.activation(out=p.epsp[:, :], in_=p.stab_rep[:, 0:1],
                                   func=AF.Exp, bias=LNEPSQ))
        ep = p.inc("s", p.s_sc, 1)
        p.wait("v", p.s_sc, ep)
        p.v(lambda e: e.tensor_scalar_mul(p.nep[:, :], p.epsp[:, :], float(N)))
        p.inc("v", p.s_ve, 1)
        p.v(lambda e: e.tensor_scalar_mul(p.epnk[:, :], p.epsp[:, :],
                                          float(256.0 * EPSQ * N)))
        self.eps_done = p.inc("v", p.s_ve, 1)

    def sched_wprime(self):
        p = self
        wst_free = {}
        for ic in range(OC):
            src = p.dt["wT"].ap()[ic * 128:(ic + 1) * 128, :]
            if ic >= 2:
                p.wait("y", p.s_ve, wst_free[ic - 2])
            p.y(lambda e, s2=src, ic=ic: e.dma_start(out=p.wtstage[ic % 2][:, :], in_=s2))
            wst = (p.s_wt[ic % 2], p.inc("y", p.s_wt[ic % 2], 16))
            for half in range(2):
                b = p.pool_next()
                p.pool_acquire(b)
                if ic == 0 and half == 0:
                    p.wait("t", *self.vrc_done)
                p.t(lambda e, b=b, ic=ic, half=half: e.matmul(
                    p.pool[b][:, :], p.vr_sb[0:32, ic * 128:(ic + 1) * 128],
                    p.c_sb[0:32, half * 512:(half + 1) * 512]))
                rv = p.inc("t", p.s_pe, 1)
                p.wait("v", p.s_pe, rv)
                p.wait("v", *wst)
                p.v(lambda e, b=b, ic=ic, half=half: e.tensor_tensor(
                    out=p.w_p[:, ic * D + half * 512: ic * D + (half + 1) * 512],
                    in0=p.pool[b][:, :],
                    in1=p.wtstage[ic % 2][:, half * 512:(half + 1) * 512],
                    op=ALU.add))
                wst_free[ic] = p.inc("v", p.s_ve, 1)
                p.pool_release(b, p.s_ve, wst_free[ic])
        self.wp_done = p.s_ve.n

    def sched_bsetup1(self, s):
        p = self
        pse = p.gen[2]
        if s == 0:
            if self.pdiag_war > 0:
                p.wait("t", p.s_ve, self.pdiag_war)
            if self.edr_war > 0:
                p.wait("t", p.s_sc, self.edr_war)
        csb = self.cs_sb
        for grp in range(2):
            for hh in range(8):
                h = grp * 8 + hh
                for mc in range(2):
                    p.t(lambda e, h=h, hh=hh, mc=mc, s=s: e.matmul(
                        pse[0:1, hh * 64:(hh + 1) * 64], p.onesb[:, 0:1],
                        p.c1e[s][mc][:, h * 66:h * 66 + 64],
                        start=(mc == 0), stop=(mc == 1)))
            gv = p.inc("t", p.s_pe, 1)
            p.wait("v", p.s_pe, gv)
            if grp == 1:
                p.wait("v", p.s_ve, self.csb_w)
            p.v(lambda e, grp=grp: e.tensor_copy(csb[0:1, grp * 512:(grp + 1) * 512],
                                                 pse[0:1, 0:512]))
            self.csb_w = p.inc("v", p.s_ve, 1)
            if grp == 0:
                p.wait("t", p.s_ve, self.csb_w)
        p.wait("t", p.s_ve, self.csb_w)
        if self.pbf_mark is not None:
            p.wait("t", *self.pbf_mark)
        for h in range(H):
            p.t(lambda e, h=h: e.transpose(
                p.pbf[(h % 2) * 64:(h % 2) * 64 + 64, 2 * (h // 2):2 * (h // 2) + 1],
                csb[0:1, h * 64:(h + 1) * 64], p.identb[0:1, 0:1]))
        cst = p.inc("t", p.s_pe, 1)
        p.wait("v", p.s_pe, cst)
        csrc = p.pbf[:, 0:2 * OC].rearrange("p (c k) -> p c k", k=2)[:, :, 0]
        p.v(lambda e, s=s, csrc=csrc: e.tensor_copy(p.csfm2[s][:, :], csrc))
        csf = p.inc("v", p.s_ve, 1)
        self.pbf_mark = (p.s_ve, csf)
        self.cs_marks[s] = csf
        for mc in range(2):
            gsap = p.c1e[s][mc][:, :].rearrange("p (h j) -> p h j", j=66)[:, :, 64]
            p.t(lambda e, gsap=gsap, mc=mc, s=s: e.matmul(
                pse[:, 16 + s * 16:32 + s * 16], p.onesb[:, :], gsap,
                start=(mc == 0), stop=(mc == 1)))
        self.k2s_marks[s] = p.inc("t", p.s_pe, 1)

    def sched_bsetup2(self, s):
        p = self
        pse = p.gen[2]
        p.wait("v", p.s_ve, self.eps_done)
        p.v(lambda e, s=s: e.tensor_scalar_mul(p.evst[:, :], p.vst_fm[s][:, :],
                                               p.epsp[:, 0:1]))
        p.inc("v", p.s_ve, 1)
        p.v(lambda e, s=s: e.tensor_scalar(
            out=p.evs256[:, :], in0=p.vst_fm[s][:, :], scalar1=p.epsp[:, 0:1],
            scalar2=float(256.0 * EPSQ), op0=ALU.mult, op1=ALU.mult))
        ev2 = p.inc("v", p.s_ve, 1)
        p.wait("v", p.s_ve, ev2)
        p.v(lambda e, s=s: e.scalar_tensor_tensor(
            out=p.k1_fm[:, :], in0=p.csfm2[s][:, :], scalar=EPSQ, in1=p.evs256[:, :],
            op0=ALU.mult, op1=ALU.add))
        k1f = p.inc("v", p.s_ve, 1)
        p.wait("v", p.s_pe, self.k2s_marks[s])
        p.v(lambda e, s=s: e.tensor_scalar(
            out=p.k2rep[s][:, :], in0=pse[:, 16 + s * 16:32 + s * 16], scalar1=EPSQ,
            scalar2=p.epnk[:, 0:1], op0=ALU.mult, op1=ALU.add))
        self.psetup_read = p.inc("v", p.s_ve, 1)
        p.v(lambda e, s=s: e.memset(p.usb[s][:, :], 0.0))
        um = p.inc("v", p.s_ve, 1)
        p.wait("v", p.s_ve, um)
        for c in range(OC):
            for colofs, srcT in ((2 * c, p.evst), (16 + 2 * c, p.k1_fm)):
                p.v(lambda e, c=c, colofs=colofs, srcT=srcT, s=s: e.tensor_copy(
                    p.usb[s][0:64, c * 32 + colofs:c * 32 + colofs + 1],
                    srcT[0:64, c:c + 1]))
                p.inc("v", p.s_ve, 1)
                p.v(lambda e, c=c, colofs=colofs, srcT=srcT, s=s: e.tensor_copy(
                    p.usb[s][64:128, c * 32 + colofs + 1:c * 32 + colofs + 2],
                    srcT[64:128, c:c + 1]))
                p.inc("v", p.s_ve, 1)
        u_done = p.s_ve.n
        for half in range(2):
            b = p.pool_next()
            p.pool_acquire(b)
            p.wait("t", p.s_ve, u_done)
            if s == 0 and half == 0:
                p.wait("t", *self.wo_done)
            for c in range(OC):
                p.t(lambda e, b=b, c=c, half=half, s=s: e.matmul(
                    p.pool[b][0:32, :], p.usb[s][:, c * 32:(c + 1) * 32],
                    p.w_o[:, c * D + half * 512: c * D + (half + 1) * 512],
                    start=(c == 0), stop=(c == OC - 1)))
            rv = p.inc("t", p.s_pe, 1)
            p.wait("v", p.s_pe, rv)
            p.v(lambda e, b=b, half=half, s=s: e.tensor_copy(
                p.v12t[s][0:32, half * 512:(half + 1) * 512], p.pool[b][0:32, :]))
            vv = p.inc("v", p.s_ve, 1)
            p.pool_release(b, p.s_ve, vv)
        self.bset_done = p.s_ve.n

    def _emit_head_tail(self, h, recv):
        p = self
        p.wait("v", p.s_ve, recv)
        qodd = p.qsden[:, h * 8:(h + 1) * 8].rearrange("p (n j) -> p n j", j=2)[:, :, 1]
        p.v(lambda e, h=h, qodd=qodd: e.tensor_tensor(
            out=p.qd3[:, :, h], in0=qodd, in1=p.qd3[:, :, 16 + h], op=ALU.mult))
        p.inc("v", p.s_ve, 1)
        npb = p.gen[h % 3]
        for ns in range(NCH):
            p.v(lambda e, h=h, ns=ns, npb=npb: e.tensor_scalar_mul(
                p.attn_tm[:, ns * D + h * 64: ns * D + (h + 1) * 64],
                npb[:, ns * 66:ns * 66 + 64],
                p.qd3t[:, ns * 32 + 16 + h: ns * 32 + 16 + h + 1]))
            p.inc("v", p.s_ve, 1)

    def pro_b(self, s, t, firstb=False):
        p = self
        xfm_done = self.prologue_tile(s, t)
        for oc in range(OC):
            b = p.pool_next()
            p.pool_acquire(b)
            p.wait("t", p.s_sc, xfm_done)
            if firstb and oc == 0:
                p.wait("t", *self.wq_done)
            for ic in range(OC):
                p.t(lambda e, b=b, ic=ic, oc=oc: e.matmul(
                    p.pool[b][:, :], p.w_q[:, ic * D + oc * 128: ic * D + (oc + 1) * 128],
                    p.xln_fm0[:, ic * TIL:(ic + 1) * TIL],
                    start=(ic == 0), stop=(ic == OC - 1)))
            rv = p.inc("t", p.s_pe, 1)
            p.wait("s", p.s_pe, rv)
            if oc == 0 and self.prj_war > 0:
                p.wait("s", p.s_pe, self.prj_war)
            p.s(lambda e, b=b, oc=oc: e.activation(
                out=p.prj_fm[:, oc * TIL:(oc + 1) * TIL], in_=p.pool[b][:, :],
                func=AF.Identity, bias=p.bqkv[:, oc:oc + 1]))
            qe = p.inc("s", p.s_sc, 1)
            p.pool_release(b, p.s_sc, qe)
        self.xfm_war = p.s_pe.n
        return p.s_sc.n

    def phase_b_tile(self, s, t, qevac_done, nxt):
        p = self
        tb = t % 2
        firstb = (s == 0 and t == 0)
        xc = p.x_tm[tb][:, :].rearrange("p (c f) -> p c f", f=D)
        h_done = {}
        np_free = {}
        rec_val = {}
        prev = None
        p.wait("v", p.s_sc, qevac_done)
        if self.attn_war > 0:
            p.wait("v", p.s_pe, self.attn_war)
        exp_done = {}

        def emit_dash(h):
            hh = h % 2
            pr = h // 2
            r = h % 4
            gq3 = p.gq[r][:, :].rearrange("p (m n) -> p m n", n=TIL)
            for nh in range(2):
                b = p.pool_next()
                p.pool_acquire(b)
                p.wait("t", p.s_sc, qevac_done)
                for mc in range(2):
                    p.t(lambda e, b=b, mc=mc, hh=hh, pr=pr, nh=nh: e.matmul(
                        p.pool[b][:, mc * 256:(mc + 1) * 256],
                        p.projzq[:, hh * 256 + mc * 128: hh * 256 + (mc + 1) * 128],
                        p.prj_fm[:, pr * TIL + nh * 256: pr * TIL + (nh + 1) * 256],
                        start=True, stop=True))
                rv = p.inc("t", p.s_pe, 1)
                p.wait("s", p.s_pe, rv)
                if h >= 4 and nh == 0:
                    p.wait("s", p.s_pe, h_done[h - 4])
                elif firstb and h < 4 and nh == 0:
                    p.wait("s", p.s_ve, self.c1a_free)
                oap = gq3[:, :, nh * 256:(nh + 1) * 256]
                iap = p.pool[b][:, :].rearrange("p (m n) -> p m n", n=256)
                p.s(lambda e, oap=oap, iap=iap: e.activation(
                    out=oap, in_=iap, func=AF.Exp, scale=NORMZ, bias=-LN16))
                ev = p.inc("s", p.s_sc, 1)
                p.pool_release(b, p.s_sc, ev)
            exp_done[h] = p.s_sc.n

        emit_dash(0)
        for h in range(H):
            if h + 1 < H:
                emit_dash(h + 1)
            gq3 = p.gq[h % 4][:, :].rearrange("p (m n) -> p m n", n=TIL)
            npb = p.gen[h % 3]
            p.wait("t", p.s_sc, exp_done[h])
            if firstb and h == 0:
                p.wait("t", p.s_ve, self.bset_done)
            if h >= 3:
                p.wait("t", p.s_ve, np_free[h - 3])
            elif self.np_war[h % 3] > 0:
                p.wait("t", p.s_ve, self.np_war[h % 3])
            for ns in range(NCH):
                for mc in range(2):
                    p.t(lambda e, npb=npb, ns=ns, mc=mc, gq3=gq3, h=h, s=s: e.matmul(
                        npb[:, ns * 66:(ns + 1) * 66],
                        gq3[:, mc, ns * 128:(ns + 1) * 128],
                        p.c1e[s][mc][:, h * 66:(h + 1) * 66],
                        start=(mc == 0), stop=(mc == 1)))
            h_done[h] = p.inc("t", p.s_pe, 1)
            if True:
                p.wait("v", p.s_pe, h_done[h])
                srcA = npb[:, 0:264].rearrange("p (n j) -> p n j", j=66)[:, :, 64:66]
                dstA = p.qsden[:, h * 8:(h + 1) * 8].rearrange("p (n j) -> p n j", j=2)
                p.v(lambda e, srcA=srcA, dstA=dstA: e.tensor_copy(dstA, srcA))
                gat = p.inc("v", p.s_ve, 1)
                p.wait("s", p.s_ve, gat)
                qodd = p.qsden[:, h * 8:(h + 1) * 8].rearrange("p (n j) -> p n j", j=2)[:, :, 1]
                p.s(lambda e, h=h, qodd=qodd, s=s: e.activation(
                    out=p.tden[:, h * 4:(h + 1) * 4], in_=qodd, func=AF.Identity,
                    scale=p.nep[:, 0:1], bias=p.k2rep[s][:, h:h + 1]))
                td = p.inc("s", p.s_sc, 1)
                p.wait("v", p.s_sc, td)
                qev = p.qsden[:, h * 8:(h + 1) * 8].rearrange("p (n j) -> p n j", j=2)[:, :, 0]
                p.v(lambda e, h=h, qev=qev: e.tensor_tensor(
                    out=p.tden[:, h * 4:(h + 1) * 4], in0=p.tden[:, h * 4:(h + 1) * 4],
                    in1=qev, op=ALU.add))
                tav = p.inc("v", p.s_ve, 1)
                p.wait("v", p.s_ve, tav)
                p.v(lambda e, h=h: e.reciprocal(p.qd3[:, :, 16 + h],
                                                p.tden[:, h * 4:(h + 1) * 4]))
                rec_val[h] = p.inc("v", p.s_ve, 1)
                if prev is not None:
                    self._emit_head_tail(prev, rec_val[prev])
                    np_free[prev] = p.s_ve.n
                prev = h
        self._emit_head_tail(prev, rec_val[prev])
        np_free[prev] = p.s_ve.n
        self.np_war = [np_free[15], np_free[13], np_free[14]]
        self.prj_war = h_done[15]
        comb_done = p.s_ve.n
        pro_next = None
        if nxt is not None:
            pro_next = self.pro_b(nxt[0], nxt[1])
        for ns in range(NCH):
            bqd = p.pool_next()
            p.pool_acquire(bqd)
            p.wait("t", p.s_ve, comb_done)
            p.t(lambda e, b=bqd, ns=ns: e.transpose(
                p.pool[b][0:32, 0:128], p.qd3t[:, ns * 32:(ns + 1) * 32],
                p.identf[:, :]))
            rv = p.inc("t", p.s_pe, 1)
            p.wait("v", p.s_pe, rv)
            p.v(lambda e, b=bqd, ns=ns: e.tensor_copy(
                p.qd_fm[0:32, ns * 128:(ns + 1) * 128], p.pool[b][0:32, 0:128]))
            qdf = p.inc("v", p.s_ve, 1)
            p.pool_release(bqd, p.s_ve, qdf)
            af = p.attn_fm0
            tw = [(p.s_ve, comb_done)]
            if self.afm_war > 0:
                tw.append((p.s_pe, self.afm_war))
            _, tpe = self.emit_transposes_bf16(p.attn_tm, ns * D, af,
                                               extra_t_waits=tw)
            self.attn_war = tpe
            afm_done = p.s_sc.n
            li = p.lnin0
            lo = p.lnout0
            for half in range(2):
                b = p.pool_next()
                p.pool_acquire(b)
                p.wait("t", p.s_sc, afm_done)
                if half == 0:
                    p.wait("t", p.s_ve, qdf)
                for ic in range(OC):
                    p.t(lambda e, b=b, ic=ic, half=half, af=af: e.matmul(
                        p.pool[b][:, :], af[:, ic * 128:(ic + 1) * 128],
                        p.w_o[:, ic * D + half * 512: ic * D + (half + 1) * 512],
                        start=(ic == 0), stop=False))
                p.t(lambda e, b=b, half=half, ns=ns, s=s: e.matmul(
                    p.pool[b][:, :], p.qd_fm[0:32, ns * 128:(ns + 1) * 128],
                    p.v12t[s][0:32, half * 512:(half + 1) * 512],
                    start=False, stop=True))
                rv = p.inc("t", p.s_pe, 1)
                p.wait("v", p.s_pe, rv)
                if half == 0 and self.lnin_war is not None:
                    p.wait("v", *self.lnin_war)
                if firstb and ns == 0 and half == 0:
                    p.wait("v", *self.bo_done)
                p.v(lambda e, b=b, half=half, li=li: e.tensor_tensor(
                    out=li[:, half * 512:(half + 1) * 512], in0=p.pool[b][:, :],
                    in1=p.bo_rep[:, half * 512:(half + 1) * 512], op=ALU.add))
                wv = p.inc("v", p.s_ve, 1)
                p.pool_release(b, p.s_ve, wv)
            wo_done = p.s_ve.n
            self.afm_war = p.s_pe.n
            ap1 = self.emit_ln_chunk(tb, 4 + ns, li[:, :], lo[:, :],
                                     waits_v=[(p.s_ve, wo_done),
                                              (p.s_gp, self.lnout_war)],
                                     waits_s=[(p.s_ve, wo_done)])
            self.lnin_war = ap1
            p.wait("v", *ap1)
            p.v(lambda e, ns=ns, lo=lo, tb=tb: e.tensor_tensor(
                out=p.x_tm[tb][:, ns * D:(ns + 1) * D],
                in0=p.x_tm[tb][:, ns * D:(ns + 1) * D], in1=lo[:, :], op=ALU.add))
            x1_done = p.inc("v", p.s_ve, 1)
            x2t = p.xln2_tm[ns % 2]
            ap2 = self.emit_ln_chunk(tb, 8 + ns, p.x_tm[tb][:, ns * D:(ns + 1) * D],
                                     x2t[:, :],
                                     waits_v=[(p.s_ve, x1_done)],
                                     waits_s=[(p.s_ve, x1_done)])
            x2f = p.xln2_fm0
            tw = [ap2]
            if self.x2f_war > 0:
                tw.append((p.s_pe, self.x2f_war))
            self.emit_transposes(x2t, 0, x2f, 0, 128, extra_t_waits=tw)
            x2f_done = p.s_sc.n
            for half in range(2):
                b = p.pool_next()
                p.pool_acquire(b)
                p.wait("t", p.s_sc, x2f_done)
                if firstb and ns == 0 and half == 0:
                    p.wait("t", p.s_ve, self.wp_done)
                for ic in range(OC):
                    p.t(lambda e, b=b, ic=ic, half=half, x2f=x2f: e.matmul(
                        p.pool[b][:, :], x2f[:, ic * 128:(ic + 1) * 128],
                        p.w_p[:, ic * D + half * 512: ic * D + (half + 1) * 512],
                        start=(ic == 0), stop=(ic == OC - 1)))
                rv = p.inc("t", p.s_pe, 1)
                p.wait("v", p.s_pe, rv)
                if half == 0 and self.lnin_war is not None:
                    p.wait("v", *self.lnin_war)
                if firstb and ns == 0 and half == 0:
                    p.wait("v", *self.brep_done)
                p.v(lambda e, b=b, half=half, li=li: e.tensor_tensor(
                    out=li[:, half * 512:(half + 1) * 512], in0=p.pool[b][:, :],
                    in1=p.b_rep[:, half * 512:(half + 1) * 512], op=ALU.add))
                fv = p.inc("v", p.s_ve, 1)
                p.pool_release(b, p.s_ve, fv)
            ffn_done = p.s_ve.n
            self.x2f_war = p.s_pe.n
            ap3 = self.emit_ln_chunk(tb, 12 + ns, li[:, :], lo[:, :],
                                     waits_v=[(p.s_ve, ffn_done),
                                              (p.s_ve, x1_done)],
                                     waits_s=[(p.s_ve, ffn_done)])
            self.lnin_war = ap3
            p.wait("g", *ap3)
            p.wait("g", p.s_ve, x1_done)
            p.g(lambda e, ns=ns, lo=lo, tb=tb: e.tensor_tensor(
                out=p.x_tm[tb][:, ns * D:(ns + 1) * D],
                in0=p.x_tm[tb][:, ns * D:(ns + 1) * D], in1=lo[:, :], op=ALU.add))
            self.lnout_war = p.inc("g", p.s_gp, 1)
        out_ready = p.s_gp.n
        p.wait("s", p.s_gp, out_ready)
        src = p.x_tm[tb][:, :].rearrange("p (c f) -> p c f", f=D)
        dst = p.out_h.ap()[s, t * TIL:(t + 1) * TIL, :].rearrange("(c p) f -> p c f", p=128)
        p.s(lambda e, d=dst, s2=src: e.dma_start(out=d, in_=s2))
        stv = p.inc("s", p.s_o[tb], 16)
        p.x_rel[tb] = [(p.s_o[tb], stv)]
        return pro_next

    def schedule(self):
        p = self
        self.sched_loads()
        p.wait("s", *self.consts_done)
        if self.phases == "a0":
            xfm = self.prologue_tile(0, 0, first=True)
            self.phase_a_tile(0, 0, xfm, None)
            self._finish()
            return
        xfm = None
        for s in range(BPC):
            xfm = self.phase_a_seq(s, xfm)
        if self.phases == "a":
            self._finish()
            return
        self.sched_collective()
        self.sched_wprime()
        a_end = p.s_pe.n
        self.prj_war = a_end
        self.xfm_war = a_end
        if self.phases == "ac":
            self._finish()
            return
        p.wait("g", p.s_pe, a_end)
        for dst, nm, sem in ((self.wb0, "wqT", p.s_wq2), (self.wb1, "woT", p.s_wo2)):
            srcw = self.dt[nm].ap().rearrange("(c p) o -> p c o", p=128)
            dstw = dst[:, :].rearrange("p (c o) -> p c o", o=D)
            p.g(lambda e, d=dstw, s2=srcw: e.dma_start(out=d, in_=s2))
            p.inc("g", sem, 16)
        self.wq_done = (p.s_wq2, p.s_wq2.n)
        self.wo_done = (p.s_wo2, p.s_wo2.n)
        for s in range(BPC):
            self.sched_bsetup1(s)
        pro = self.pro_b(0, 0, firstb=True)
        for s in range(BPC):
            self.sched_bsetup2(s)
        self.np_war = [0, 0, self.psetup_read]
        if self.phases == "w":
            self._finish()
            return
        tiles = [(s, t) for s in range(BPC) for t in range(NT)]
        for i, (s, t) in enumerate(tiles):
            if i + 1 < len(tiles):
                nxt = (*tiles[i + 1], tiles[i + 2] if i + 2 < len(tiles) else None)
            else:
                nxt = None
            pro = self.phase_b_tile(s, t, pro, nxt)
        self._finish()

    def _finish(self):
        p = self
        if self.debug:
            p.wait("g", p.s_ve, p.s_ve.n)
            p.wait("g", p.s_sc, p.s_sc.n)
            p.wait("g", p.s_pe, p.s_pe.n)
            p.wait("g", p.s_gp, p.s_gp.n)
            order = ["a0", "a", "ac", "w", "b0", "b1", "full"]
            lvl = order.index(self.phases)
            taps = [("tap_prj", self.prj_fm), ("tap_vs0", self.vs_row[0])]
            if lvl >= 1:
                taps += [("tap_c1e0", self.c1e[0][0]), ("tap_c1e1", self.c1e[0][1])]
            if lvl >= 2:
                taps += [("tap_stab", self.stab_in), ("tap_eps", self.epsp)]
            if lvl >= 3:
                taps += [("tap_k2rep", self.k2rep[0]), ("tap_v12", self.v12t[0])]
            for nm, src in taps:
                p.g(lambda e, nm=nm, src=src: e.dma_start(
                    out=self.tap_h[nm].ap(), in_=src[:, :]))
                p.inc("g", p.s_tap, 16)
        p.wait("g", p.s_o[0], p.s_o[0].n)
        p.wait("g", p.s_o[1], p.s_o[1].n)
        p.wait("g", p.s_tap, p.s_tap.n)


def _projz(proj):
    z = np.zeros((128, 512), dtype=np.float32)
    z[0:64, 0:256] = proj.T
    z[64:128, 256:512] = proj.T
    return z


def _projzq(proj):
    z = np.zeros((128, 512), dtype=np.float32)
    z[0:64, 0:256] = proj.T
    z[64:128, 256:512] = proj.T
    return z


def make_inputs(inputs):
    f32 = np.float32
    x = np.ascontiguousarray(np.asarray(inputs["L0_attn_hidden_states"], dtype=f32))
    wq = np.asarray(inputs["L0_attn_Wq"], dtype=f32)
    wk = np.asarray(inputs["L0_attn_Wk"], dtype=f32)
    wv = np.asarray(inputs["L0_attn_Wv"], dtype=f32)
    wo = np.asarray(inputs["L0_attn_Wo"], dtype=f32)
    bq = np.asarray(inputs["L0_attn_bq"], dtype=f32)
    bk = np.asarray(inputs["L0_attn_bk"], dtype=f32)
    bv = np.asarray(inputs["L0_attn_bv"], dtype=f32)
    bo = np.asarray(inputs["L0_attn_bo"], dtype=f32)
    proj = np.asarray(inputs["L0_attn_proj"], dtype=f32)
    W = np.asarray(inputs["L0_ffn_W"], dtype=f32)
    bffn = np.asarray(inputs["L0_ffn_b"], dtype=f32)
    Vr = np.asarray(inputs["L0_ffn_V_r"], dtype=f32)
    C = np.asarray(inputs["L0_ffn_C"], dtype=f32)

    common = {
        "wqT": np.ascontiguousarray(wq.T), "wkT": np.ascontiguousarray(wk.T),
        "wvT": np.ascontiguousarray(wv.T), "woT": np.ascontiguousarray(wo.T),
        "wT": np.ascontiguousarray(W.T),
        "vrT": np.ascontiguousarray(Vr.T), "cT": np.ascontiguousarray(C.T),
        "projz": _projz(proj), "projzq": _projzq(proj),
        "identf": np.eye(128, dtype=f32),
        "identb": np.eye(128, dtype=f32),
        "cvals": np.ascontiguousarray(np.broadcast_to(
            np.array([EPS_LN, -LN16, LNEPSQ], dtype=f32), (128, 3))),
        "ones": np.ones((128, 128), dtype=f32),
        "bo_rep": np.ascontiguousarray(np.broadcast_to(bo, (128, D))),
        "b_rep": np.ascontiguousarray(np.broadcast_to(bffn, (128, D))),
        "bv_rep": np.ascontiguousarray(np.broadcast_to(bv, (128, D))),
    }
    bqkv = np.zeros((128, 3 * OC), dtype=f32)
    for i, bb in enumerate((bq, bk, bv)):
        bqkv[:, i * OC:(i + 1) * OC] = bb.reshape(OC, 128).T
    common["bqkv"] = bqkv
    ediag = np.zeros((128, 128), dtype=f32)
    for c in range(OC):
        ediag[0:64, c * 16 + 2 * c] = -0.5
        ediag[64:128, c * 16 + 2 * c + 1] = -0.5
    common["ediag"] = ediag

    in_maps = []
    for core in range(NCORES):
        m = dict(common)
        m["x"] = np.ascontiguousarray(x[core * BPC:(core + 1) * BPC])
        in_maps.append(m)
    return in_maps


_CACHE = {}


def get_nc(debug=False):
    key = ("nc", debug)
    if key not in _CACHE:
        b = Builder(debug=debug)
        b.emit()
        _CACHE[key] = b.nc
    return _CACHE[key]


def kernel(**inputs):
    in_maps = make_inputs(inputs)
    nc = get_nc()
    res = run_bass_kernel_spmd(nc, in_maps, core_ids=list(range(NCORES)))
    return np.concatenate([np.asarray(res.results[c]["out"]) for c in range(NCORES)],
                          axis=0)


# version 12
# speedup vs baseline: 1.3976x; 1.0305x over previous
import contextlib
import numpy as np
import ml_dtypes
import concourse.bass as bass
import concourse.mybir as mybir
from concourse.bass_utils import run_bass_kernel_spmd

F32 = mybir.dt.float32
BF16 = mybir.dt.bfloat16
AF = mybir.ActivationFunctionType
ALU = mybir.AluOpType
AXL = mybir.AxisListType

B, N, D = 16, 4096, 1024
H, DH, M = 16, 64, 256
NCORES = 8
BPC = B // NCORES
TIL = 512
NT = N // TIL
NCH = TIL // 128
OC = D // 128

NORMZ = float(DH ** -0.25)
LN16 = float(np.log(16.0))
EPSQ = float((M ** -0.5) * 1e-3)
LNEPSQ = float(np.log(EPSQ))
EPS_LN = 1e-5
INV_D = 1.0 / D


class Sem:
    def __init__(self, h):
        self.h = h
        self.n = 0


class Builder:
    def __init__(self, debug=False, phases="full", stop=99):
        self.debug = debug
        self.phases = phases
        self.stop = stop
        self.nc = bass.Bass("TRN2", target_bir_lowering=False)
        self.q = {k: [] for k in "tvsgy"}
        self._can_inc = {k: False for k in "tvsgy"}
        self.ctx = contextlib.ExitStack()
        self.dt = {}
        self.build()

    def _ins(self, eng, fn):
        self.q[eng].append(fn)
        self._can_inc[eng] = True

    def t(self, fn): self._ins("t", fn)
    def v(self, fn): self._ins("v", fn)
    def s(self, fn): self._ins("s", fn)
    def g(self, fn): self._ins("g", fn)
    def y(self, fn): self._ins("y", fn)

    def wait(self, eng, sem, val):
        if val > sem.n:
            raise RuntimeError(f"wait {val} > scheduled {sem.n}")
        if val > 0:
            self.q[eng].append(lambda e, s=sem.h, v=val: e.wait_ge(s, v))
            self._can_inc[eng] = False

    def inc(self, eng, sem, k=1):
        if not self._can_inc[eng]:
            raise RuntimeError("inc must ride on a fresh datapath instruction")
        fn = self.q[eng].pop()
        self.q[eng].append(lambda e, f=fn, s=sem.h, k=k: f(e).then_inc(s, k))
        self._can_inc[eng] = False
        sem.n += k
        return sem.n

    def emit(self):
        with self.nc.Block() as blk:
            for key, attr in (("y", "sync"), ("g", "gpsimd"), ("s", "scalar"),
                              ("v", "vector"), ("t", "tensor")):
                fns = self.q[key]
                if fns:
                    getattr(blk, attr)(lambda e, fns=fns: [f(e) for f in fns])

    def build(self):
        nc, ctx = self.nc, self.ctx
        sb = lambda name, shape, dtype: ctx.enter_context(nc.sbuf_tensor(name, shape, dtype))
        psr = lambda name, shape: ctx.enter_context(nc.psum_tensor(name, shape, F32))
        mks = lambda name: Sem(ctx.enter_context(nc.semaphore(name)))

        def din(name, shape):
            self.dt[name] = nc.dram_tensor(name, shape, F32, kind="ExternalInput")
        din("x", [BPC, N, D])
        for nm in ("wqT", "wkT", "wvT", "woT", "wT"):
            din(nm, [D, D])
        din("vrT", [32, D]); din("cT", [32, D])
        din("bqkv", [128, 3 * OC])
        din("bo_rep", [128, D]); din("b_rep", [128, D]); din("bv_rep", [128, D])
        din("projz", [128, 512]); din("projzq", [128, 512])
        din("identf", [128, 128])
        din("ediag", [128, 128]); din("ones", [128, 128])
        din("identb", [128, 128]); din("cvals", [128, 3])
        self.out_h = nc.dram_tensor("out", [BPC, N, D], F32, kind="ExternalOutput")
        self.ccin = nc.dram_tensor("ccin", [1, 16], F32)
        self.ccout = nc.dram_tensor("ccout", [1, 16], F32)
        if self.debug:
            self.tap_h = {nm: nc.dram_tensor(nm, shape, F32, kind="ExternalOutput")
                          for nm, shape in [
                              ("tap_c1e0", [128, H * 66]), ("tap_c1e1", [128, H * 66]),
                              ("tap_vs0", [1, D]), ("tap_stab", [1, 16]),
                              ("tap_prj", [128, OC * TIL]),
                              ("tap_k2rep", [128, H]), ("tap_v12", [32, D]),
                              ("tap_eps", [128, 1])]}

        self.wb0 = sb("wb0", [128, OC * D], BF16)
        self.wb1 = sb("wb1", [128, OC * D], BF16)
        self.wb2 = sb("wb2", [128, OC * D], BF16)
        self.w_k = self.w_q = self.wb0
        self.w_v = self.w_o = self.wb1
        self.w_p = self.wb2
        self.projz = sb("projzS", [128, 512], BF16)
        self.projzq = sb("projzqS", [128, 512], BF16)
        self.identf = sb("identfS", [128, 128], F32)
        self.ediag = sb("ediagS", [128, 128], BF16)
        self.onesb = sb("onesbS", [128, 128], BF16)
        self.bqkv = sb("bqkvS", [128, 3 * OC], F32)
        self.cvalsb = sb("cvalsbS", [128, 3], F32)
        nc.const_aps.register(F32, EPS_LN, self.cvalsb[:, 0:1])
        nc.const_aps.register(F32, -LN16, self.cvalsb[:, 1:2])
        nc.const_aps.register(F32, LNEPSQ, self.cvalsb[:, 2:3])
        self.bo_rep = sb("bo_repS", [128, D], F32)
        self.b_rep = sb("b_repS", [128, D], F32)
        self.bv_rep = sb("bv_repS", [128, D], F32)
        self.onesf = sb("onesfS", [1, 128], F32)
        self.vr_sb = sb("vr_sb", [32, D], BF16)
        self.c_sb = sb("c_sb", [32, D], BF16)

        self.c1a = sb("c1a", [128, 32 * 65], F32)
        self.c1e = [[sb(f"c1e_{s}_{mc}", [128, H * 66], BF16) for mc in range(2)]
                    for s in range(BPC)]
        self.vs_row = [sb(f"vs_{s}", [1, D], F32) for s in range(BPC)]
        self.vst_fm = [sb(f"vst_{s}", [128, OC], F32) for s in range(BPC)]
        self.usb = [sb(f"usb_{s}", [128, 256], BF16) for s in range(BPC)]
        self.v12t = [sb(f"v12t_{s}", [32, D], BF16) for s in range(BPC)]
        self.k2rep = [sb(f"k2rep_{s}", [128, H], F32) for s in range(BPC)]
        self.stabv = sb("stabvS", [1, 16], F32)
        self.stab_in = sb("stab_inS", [1, 16], F32)
        self.stab_rep = sb("stab_repS", [128, 16], F32)
        self.epsp = sb("epspS", [128, 1], F32)
        self.nep = sb("nepS", [128, 1], F32)
        self.epnk = sb("epnkS", [128, 1], F32)
        self.evst = sb("evstS", [128, OC], F32)
        self.evs256 = sb("evs256S", [128, OC], F32)
        self.k1_fm = sb("k1_fmS", [128, OC], F32)
        self.cs_sb = sb("cs_sbS", [1, 1024], BF16)
        self.csfm2 = [sb(f"csfmS{i}", [128, OC], F32) for i in range(BPC)]
        self.csb_w = 0
        self.cs_marks = {}
        self.k2s_marks = {}
        self.stabsc = sb("stabscS", [1, 2], F32)

        self.x_tm = [sb(f"x_tm{i}", [128, NCH * D], F32) for i in range(2)]
        self.xln_tm = [sb(f"xln_tm{i}", [128, D], F32) for i in range(2)]
        self.xln_fm0 = sb("xln_fm0", [128, OC * TIL], BF16)
        self.prj_fm = sb("prj_fmS", [128, OC * TIL], BF16)
        self.k2b = [sb(f"k2b{i}", [128, TIL], BF16) for i in range(2)]
        self.e1b0 = [sb(f"e1b0_{i}", [128, TIL], BF16) for i in range(4)]
        self.vxb = sb("vxbS", [128, NCH * H * 65], BF16)
        self.edrep0 = sb("edrep0", [128, H * 65], BF16)
        self.ed_tm = sb("ed_tmS", [128, NCH * 16], F32)
        self.ed_rows = sb("ed_rowsS", [16, TIL], F32)
        self.dmaxb = sb("dmaxbS", [16, NT], F32)
        self.dmax1 = sb("dmax1S", [16, 1], F32)
        self.stat = [sb(f"stat{i}", [128, 4 * 16], F32) for i in range(2)]
        self.bst = sb("bst", [128, 12], F32)
        self.bst_war = 0
        self.junk = sb("junkS", [128, 4], F32)
        c1a_addr = nc.lookup_mloc(self.c1a).addr
        self.gq = [nc.alloc_sbuf_tensor_at(f"gq{i}", [128, 2 * TIL], BF16,
                                           offset=c1a_addr + i * 2048)
                   for i in range(4)]
        self.qd3t = sb("qd3S", [128, NCH * 32], F32)
        self.qsden = sb("qsdenS", [128, H * 8], F32)
        self.tden = sb("tdenS", [128, H * 4], F32)
        self.attn_tm = sb("attn_tmS", [128, NCH * D], BF16)
        at_addr = nc.lookup_mloc(self.attn_tm).addr
        self.e1b = [self.e1b0,
                    [nc.alloc_sbuf_tensor_at(f"e1b1_{i}", [128, TIL], BF16,
                                             offset=at_addr + i * 1024)
                     for i in range(4)]]
        self.identb = sb("identbS", [128, 128], BF16)
        self.attn_fm = [sb(f"attn_fm{i}", [128, OC * 128], BF16)
                        for i in range(2)]
        self.qd_fm = sb("qd_fmS", [32, NCH * 128], BF16)
        self.lnin0 = sb("lnin0", [128, D], F32)
        self.lnout0 = sb("lnout0", [128, D], F32)
        self.xln2b = sb("xln2b", [128, NCH * D], BF16)
        self.xln2_fm0 = sb("xln2_fm0", [128, OC * 128], BF16)
        xb_addr = nc.lookup_mloc(self.xln2b).addr
        self.wtstage = [nc.alloc_sbuf_tensor_at(f"wtst{i}", [128, D], F32,
                                                offset=xb_addr + i * 4096)
                        for i in range(2)]

        self.pool = [psr(f"poolP{i}", [128, 512]) for i in range(3)]
        self.gen = [psr(f"genP{i}", [128, 512]) for i in range(3)]
        self.pbf = ctx.enter_context(nc.psum_tensor("pbfP", [128, 1024], BF16))

        self.s_x = [mks("s_x0"), mks("s_x1")]
        self.s_cst = mks("s_cst"); self.s_wk = mks("s_wk"); self.s_wv = mks("s_wv")
        self.s_vrc = mks("s_vrc"); self.s_bo = mks("s_bo"); self.s_br = mks("s_br")
        self.s_wq2 = mks("s_wq2"); self.s_wo2 = mks("s_wo2")
        self.s_wt = [mks("s_wt0"), mks("s_wt1")]
        self.s_o = [mks("s_o0"), mks("s_o1")]
        self.s_pe = mks("s_pe"); self.s_sc = mks("s_sc"); self.s_ve = mks("s_ve")
        self.s_gp = mks("s_gp"); self.s_cc = mks("s_cc"); self.s_tap = mks("s_tap")
        self.s_ccr = mks("s_ccr")
        self.pool_mark = [None, None, None]
        self.pbf_mark = None
        self._rr = 0

        self.vx3 = self.vxb[:, :].rearrange("p (c h j) -> p c h j", c=NCH, j=65)
        self.bv3 = self.bv_rep[:, :].rearrange("p (h j) -> p h j", j=64)
        self.qd3 = self.qd3t[:, :].rearrange("p (c k) -> p c k", k=32)

        self.xpref = {}
        self.x_rel = [None, None]
        self.xlnT_mark = [0, 0]
        self.xfm_war = 0
        self.prj_war = 0
        self.pdiag_war = 0
        self.edr_war = 0
        self.vx_war = 0
        self.k2_war = 0
        self.c1mini_free = [0, 0]
        self.c1aw = [0] * 8
        self.vs_war = [0, 0]
        self.afm_war = [0, 0]
        self.xln2b_war = [None] * 4
        self.x2f_done = 0
        self.carry_q = []
        self.attn_war = 0
        self.x2f_war = 0
        self.lnin_war = None
        self.lnout_war = 0
        self.np_war = [0, 0, 0]
        self.last_c1pair = {}
        self.ln_mark = [dict(), dict()]

        self.schedule()

    def pool_next(self):
        b = self._rr % 3
        self._rr += 1
        return b

    def pool_acquire(self, bank):
        m = self.pool_mark[bank]
        if m is not None:
            self.wait("t", m[0], m[1])

    def pool_release(self, bank, sem, val):
        self.pool_mark[bank] = (sem, val)

    def _run_carry(self, n):
        while n > 0 and self.carry_q:
            self.carry_q.pop(0)()
            n -= 1

    def emit_ln_chunk(self, tb, slot, src_ap, dst_ap, waits_v=(), waits_s=(),
                      apply_eng="v"):
        p = self
        st = self.stat[tb]
        base = slot * 4
        cMV, cLNV, cRS = base, base + 2, base + 3
        bst = self.bst
        war = self.ln_mark[tb].get(slot, 0)
        for sem, val in waits_v:
            p.wait("v", sem, val)
        if war:
            p.wait("v", p.s_sc, war)
        if self.bst_war:
            p.wait("v", p.s_ve, self.bst_war)
        p.v(lambda e, a=src_ap: e.bn_stats(bst[:, 0:6], a[:, 0:512]))
        p.v(lambda e, a=src_ap: e.bn_stats(bst[:, 6:12], a[:, 512:1024]))
        bsv = p.inc("v", p.s_ve, 1)
        p.wait("v", p.s_ve, bsv)
        p.v(lambda e: e.bn_aggr(st[:, cMV:cMV + 2], bst[:, 0:12]))
        agv = p.inc("v", p.s_ve, 1)
        self.bst_war = agv
        for sem, val in waits_s:
            p.wait("s", sem, val)
        p.wait("s", p.s_ve, agv)
        p.s(lambda e: e.activation(out=st[:, cLNV:cLNV + 1], in_=st[:, cMV + 1:cMV + 2],
                                   func=AF.Ln, bias=EPS_LN))
        lnv = p.inc("s", p.s_sc, 1)
        p.wait("s", p.s_sc, lnv)
        p.s(lambda e: e.activation(out=st[:, cRS:cRS + 1], in_=st[:, cLNV:cLNV + 1],
                                   func=AF.Exp, scale=-0.5))
        rstd = p.inc("s", p.s_sc, 1)
        cM = cMV
        self.ln_mark[tb][slot] = rstd
        sem = {"v": p.s_ve, "g": p.s_gp}[apply_eng]
        p.wait(apply_eng, p.s_sc, rstd)
        for w in waits_v:
            if apply_eng != "v":
                p.wait(apply_eng, *w)
        p.q[apply_eng].append(lambda e, a=src_ap, o=dst_ap: e.tensor_scalar(
            out=o, in0=a, scalar1=st[:, cM:cM + 1], scalar2=st[:, cRS:cRS + 1],
            op0=ALU.subtract, op1=ALU.mult))
        p._can_inc[apply_eng] = True
        return (sem, p.inc(apply_eng, sem, 1))

    def emit_transposes(self, src_t, src_off, dst_fm, dst_base, dst_stride,
                        extra_t_waits=()):
        p = self
        last_sc = last_pe = None
        d3 = dst_fm[:, :].rearrange("p (o k) -> p o k", k=dst_stride)
        b0 = dst_base // dst_stride
        co = dst_base % dst_stride
        for g2 in range(2):
            b = p.pool_next()
            p.pool_acquire(b)
            for sem, val in extra_t_waits:
                p.wait("t", sem, val)
            for j in range(4):
                oc = g2 * 4 + j
                p.t(lambda e, b=b, j=j, off=src_off + oc * 128, s=src_t:
                    e.transpose(p.pool[b][:, j * 128:(j + 1) * 128],
                                s[:, off:off + 128], p.identf[:, :]))
            last_pe = p.inc("t", p.s_pe, 1)
            p.wait("s", p.s_pe, last_pe)
            oap = d3[:, b0 + g2 * 4: b0 + (g2 + 1) * 4, co:co + 128]
            iap = p.pool[b][:, :].rearrange("p (o k) -> p o k", k=128)
            p.s(lambda e, oap=oap, iap=iap: e.copy(oap, iap))
            last_sc = p.inc("s", p.s_sc, 1)
            p.pool_release(b, p.s_sc, last_sc)
        return last_sc, last_pe

    def emit_transposes_bf16(self, src_t, src_off, dst_fm, extra_t_waits=(),
                             extra_s_waits=()):
        p = self
        if self.pbf_mark is not None:
            p.wait("t", *self.pbf_mark)
        for sem, val in extra_t_waits:
            p.wait("t", sem, val)
        for oc in range(OC):
            p.t(lambda e, off=src_off + oc * 128, oc=oc, s=src_t:
                e.transpose(p.pbf[:, oc * 128:(oc + 1) * 128],
                            s[:, off:off + 128], p.identb[:, :]))
        pe = p.inc("t", p.s_pe, 1)
        p.wait("s", p.s_pe, pe)
        for sem, val in extra_s_waits:
            p.wait("s", sem, val)
        p.s(lambda e, d=dst_fm: e.copy(d[:, 0:OC * 128], p.pbf[:, :]))
        sc = p.inc("s", p.s_sc, 1)
        self.pbf_mark = (p.s_sc, sc)
        return sc, pe

    def sched_loads(self):
        p = self

        def gload(dst, name, sem, rearr=False):
            src = self.dt[name].ap()
            if rearr:
                src = src.rearrange("(c p) o -> p c o", p=128)
                dstap = dst[:, :].rearrange("p (c o) -> p c o", o=D)
            else:
                dstap = dst[:, :]
            p.g(lambda e, d=dstap, s=src: e.dma_start(out=d, in_=s))
            return (sem, p.inc("g", sem, 16))

        for dst, nm in ((self.identf, "identf"), (self.projz, "projz"),
                        (self.projzq, "projzq"),
                        (self.ediag, "ediag"), (self.onesb, "ones"),
                        (self.bqkv, "bqkv"), (self.bv_rep, "bv_rep"),
                        (self.identb, "identb"), (self.cvalsb, "cvals")):
            gload(dst, nm, p.s_cst)
        p.g(lambda e: e.dma_start(out=self.onesf[0:1, :], in_=self.dt["ones"].ap()[0:1, :]))
        p.inc("g", p.s_cst, 16)
        self.consts_done = (p.s_cst, p.s_cst.n)
        self.wk_done = gload(self.w_k, "wkT", p.s_wk, True)
        self.wv_done = gload(self.w_v, "wvT", p.s_wv, True)
        gload(self.vr_sb, "vrT", p.s_vrc)
        self.vrc_done = gload(self.c_sb, "cT", p.s_vrc)
        self.bo_done = gload(self.bo_rep, "bo_rep", p.s_bo)
        self.brep_done = gload(self.b_rep, "b_rep", p.s_br)

    def load_x_tile(self, s, t):
        p = self
        key = (s, t)
        if key in p.xpref:
            return p.xpref.pop(key)
        tb = t % 2
        if p.x_rel[tb] is not None:
            for sem, val in p.x_rel[tb]:
                p.wait("y", sem, val)
        src = p.dt["x"].ap()[s, t * TIL:(t + 1) * TIL, :].rearrange("(c p) f -> p c f", p=128)
        dst = p.x_tm[tb][:, :].rearrange("p (c f) -> p c f", f=D)
        p.y(lambda e, d=dst, s2=src: e.dma_start(out=d, in_=s2))
        mark = (p.s_x[tb], p.inc("y", p.s_x[tb], 16))
        return mark

    def prefetch_x(self, s, t):
        if (s, t) not in self.xpref:
            self.xpref[(s, t)] = self.load_x_tile(s, t)

    def prologue_tile(self, s, t, first=False):
        p = self
        tb = t % 2
        x_done = self.load_x_tile(s, t)
        xc = p.x_tm[tb][:, :].rearrange("p (c f) -> p c f", f=D)
        for c in range(NCH):
            waits_v = [x_done]
            if p.xlnT_mark[c % 2] > 0:
                waits_v.append((p.s_pe, p.xlnT_mark[c % 2]))
            ap_done = self.emit_ln_chunk(tb, c, xc[:, c, :], p.xln_tm[c % 2][:, :],
                                         waits_v=waits_v, waits_s=[x_done])
            tw = [ap_done]
            if first and c == 0:
                tw.append(self.consts_done)
            if self.xfm_war > 0:
                tw.append((p.s_pe, self.xfm_war))
            _, tpe = self.emit_transposes(p.xln_tm[c % 2], 0, p.xln_fm0,
                                          c * 128, TIL, extra_t_waits=tw)
            p.xlnT_mark[c % 2] = tpe
        p.x_rel[tb] = [(p.s_ve, p.s_ve.n), (p.s_sc, p.s_sc.n)]
        return p.s_sc.n

    def phase_a_tile(self, s, t, xfm_done, nxt):
        p = self
        tb = t % 2
        first = (s == 0 and t == 0)
        if self.stop <= 1:
            return None
        diag_mm = {}
        k2dv = {}
        for oc in range(OC):
            b = p.pool_next()
            p.pool_acquire(b)
            p.wait("t", p.s_sc, xfm_done)
            if first and oc == 0:
                p.wait("t", *self.wk_done)
            for ic in range(OC):
                p.t(lambda e, b=b, ic=ic, oc=oc: e.matmul(
                    p.pool[b][:, :], p.w_k[:, ic * D + oc * 128: ic * D + (oc + 1) * 128],
                    p.xln_fm0[:, ic * TIL:(ic + 1) * TIL],
                    start=(ic == 0), stop=(ic == OC - 1)))
            rv = p.inc("t", p.s_pe, 1)
            p.wait("s", p.s_pe, rv)
            if oc == 0 and self.prj_war > 0:
                p.wait("s", p.s_pe, self.prj_war)
            p.s(lambda e, b=b, oc=oc: e.activation(
                out=p.prj_fm[:, oc * TIL:(oc + 1) * TIL], in_=p.pool[b][:, :],
                func=AF.Identity, bias=p.bqkv[:, OC + oc:OC + oc + 1]))
            p.inc("s", p.s_sc, 1)
            if oc >= 2:
                p.wait("s", p.s_pe, diag_mm[oc - 2])
            elif self.k2_war > 0:
                p.wait("s", p.s_pe, self.k2_war)
            p.s(lambda e, b=b, oc=oc: e.activation(
                out=p.k2b[oc % 2][:, :], in_=p.pool[b][:, :],
                func=AF.Square, bias=p.bqkv[:, OC + oc:OC + oc + 1]))
            k2d = p.inc("s", p.s_sc, 1)
            p.pool_release(b, p.s_sc, k2d)
            k2dv[oc] = k2d

            def emit_diag(oc2):
                p.wait("t", p.s_sc, k2dv[oc2])
                if oc2 == 0:
                    if self.pdiag_war > 0:
                        p.wait("t", p.s_ve, self.pdiag_war)
                    if self.edr_war > 0:
                        p.wait("t", p.s_sc, self.edr_war)
                p.t(lambda e, oc2=oc2: e.matmul(
                    p.gen[2][0:16, :], p.ediag[:, oc2 * 16:(oc2 + 1) * 16],
                    p.k2b[oc2 % 2][:, :],
                    start=(oc2 == 0), stop=(oc2 == OC - 1)))
                diag_mm[oc2] = p.inc("t", p.s_pe, 1)

            if oc >= 1:
                emit_diag(oc - 1)
            if oc == OC - 1:
                emit_diag(oc)
        self.k2_war = diag_mm[6]
        kevac_done = p.s_sc.n
        if self.stop <= 2:
            return None
        for c in range(NCH):
            for half in range(2):
                b = p.pool_next()
                p.pool_acquire(b)
                if first and c == 0 and half == 0:
                    p.wait("t", *self.wv_done)
                for ic in range(OC):
                    p.t(lambda e, b=b, ic=ic, c=c, half=half: e.matmul(
                        p.pool[b][:, :],
                        p.xln_fm0[:, ic * TIL + c * 128: ic * TIL + (c + 1) * 128],
                        p.w_v[:, ic * D + half * 512: ic * D + (half + 1) * 512],
                        start=(ic == 0), stop=(ic == OC - 1)))
                rv = p.inc("t", p.s_pe, 1)
                p.wait("v", p.s_pe, rv)
                if c == 0 and half == 0:
                    if first:
                        p.wait("v", *self.consts_done)
                    if self.vx_war > 0:
                        p.wait("v", p.s_pe, self.vx_war)
                iap = p.pool[b][:, :].rearrange("p (h j) -> p h j", j=64)
                p.v(lambda e, iap=iap, c=c, half=half: e.tensor_tensor(
                    out=p.vx3[:, c, half * 8:(half + 1) * 8, 0:64], in0=iap,
                    in1=p.bv3[:, half * 8:(half + 1) * 8, :], op=ALU.add))
                vv = p.inc("v", p.s_ve, 1)
                p.pool_release(b, p.s_ve, vv)
        self.xfm_war = p.s_pe.n
        p.v(lambda e: e.memset(p.vx3[:, :, :, 64:65], 1.0))
        vx_done = p.inc("v", p.s_ve, 1)
        if self.stop <= 3:
            return None
        p.wait("s", p.s_pe, diag_mm[OC - 1])
        p.s(lambda e: e.activation(out=p.ed_rows[:, :], in_=p.gen[2][0:16, :], func=AF.Exp))
        edr_done = p.inc("s", p.s_sc, 1)
        self.edr_war = edr_done
        p.wait("v", p.s_pe, diag_mm[OC - 1])
        p.v(lambda e, t=t: e.reduce_max(p.dmaxb[0:16, t:t + 1], p.gen[2][0:16, :],
                                        axis=AXL.X))
        self.pdiag_war = p.inc("v", p.s_ve, 1)
        b = p.pool_next()
        p.pool_acquire(b)
        p.wait("t", p.s_sc, edr_done)
        for c in range(NCH):
            p.t(lambda e, b=b, c=c: e.transpose(
                p.pool[b][:, c * 16:(c + 1) * 16],
                p.ed_rows[0:16, c * 128:(c + 1) * 128], p.identf[0:16, 0:16]))
        rv = p.inc("t", p.s_pe, 1)
        p.wait("v", p.s_pe, rv)
        p.v(lambda e, b=b: e.tensor_copy(p.ed_tm[:, :], p.pool[b][:, 0:NCH * 16]))
        edt = p.inc("v", p.s_ve, 1)
        p.pool_release(b, p.s_ve, edt)
        for half in range(2):
            b = p.pool_next()
            p.pool_acquire(b)
            p.wait("t", p.s_ve, vx_done)
            for c in range(NCH):
                p.t(lambda e, b=b, c=c, half=half: e.matmul(
                    p.pool[b][0:1, :], p.onesb[:, 0:1],
                    p.vx3[:, c, half * 8:(half + 1) * 8, 0:64],
                    start=(c == 0), stop=(c == NCH - 1)))
            rv = p.inc("t", p.s_pe, 1)
            p.wait("v", p.s_pe, rv)
            if self.vs_war[half] > 0:
                p.wait("v", p.s_ve, self.vs_war[half])
            if t == 0:
                p.v(lambda e, b=b, half=half, s=s: e.tensor_copy(
                    p.vs_row[s][:, half * 512:(half + 1) * 512], p.pool[b][0:1, :]))
            else:
                p.v(lambda e, b=b, half=half, s=s: e.tensor_tensor(
                    out=p.vs_row[s][:, half * 512:(half + 1) * 512],
                    in0=p.vs_row[s][:, half * 512:(half + 1) * 512],
                    in1=p.pool[b][0:1, :], op=ALU.add))
            self.vs_war[half] = p.inc("v", p.s_ve, 1)
            p.pool_release(b, p.s_ve, self.vs_war[half])
        vs_mm_done = p.s_pe.n
        if self.stop <= 4:
            return None
        p.wait("v", p.s_pe, vs_mm_done)
        p.wait("v", p.s_ve, edt)
        mul_prev = 0
        for c in range(NCH):
            er = p.edrep0
            src = p.ed_tm[:, c * 16:(c + 1) * 16].broadcast_to([128, 16, 65])
            if mul_prev > 0:
                p.wait("v", p.s_ve, mul_prev)
            p.v(lambda e, er=er, src=src: e.tensor_copy(
                er[:, :].rearrange("p (h j) -> p h j", j=65), src))
            erv = p.inc("v", p.s_ve, 1)
            p.wait("v", p.s_ve, erv)
            p.v(lambda e, er=er, c=c: e.tensor_tensor(
                out=p.vx3[:, c, :, :], in0=p.vx3[:, c, :, :],
                in1=er[:, :].rearrange("p (h j) -> p h j", j=65), op=ALU.mult))
            mul_prev = p.inc("v", p.s_ve, 1)
        vp_done = p.s_ve.n
        if self.stop <= 5:
            return None
        nxt_mark = None
        if nxt is not None:
            nxt_mark = self.prologue_tile(nxt[0], nxt[1])
        c1grp = dict(self.last_c1pair)
        exp_done = {}

        def emit_c1(pr):
            mb = pr % 2
            p.wait("t", p.s_sc, exp_done[pr])
            if pr == 0:
                p.wait("t", p.s_ve, vp_done)
            if self.c1mini_free[mb] > 0:
                p.wait("t", p.s_ve, self.c1mini_free[mb])
            for hh in range(2):
                for mc in range(2):
                    h = 2 * pr + hh
                    soff = (hh * 2 + mc) * 65
                    for c in range(NCH):
                        p.t(lambda e, mb=mb, soff=soff, c=c, hh=hh, h=h, pr=pr:
                            e.matmul(p.gen[mb][:, soff:soff + 65],
                                     p.e1b[pr % 2][c][:, hh * 256 + (soff // 65 % 2) * 128: hh * 256 + (soff // 65 % 2) * 128 + 128],
                                     p.vx3[:, c, h, :],
                                     start=(c == 0), stop=(c == NCH - 1)))
            c1grp[pr] = p.inc("t", p.s_pe, 1)
            p.wait("v", p.s_pe, c1grp[pr])
            if self.c1aw[pr] > 0:
                p.wait("v", p.s_ve, self.c1aw[pr])
            if t == 0:
                p.v(lambda e, mb=mb, pr=pr: e.tensor_copy(
                    p.c1a[:, pr * 260:(pr + 1) * 260], p.gen[mb][:, 0:260]))
            else:
                p.v(lambda e, mb=mb, pr=pr: e.tensor_tensor(
                    out=p.c1a[:, pr * 260:(pr + 1) * 260],
                    in0=p.c1a[:, pr * 260:(pr + 1) * 260],
                    in1=p.gen[mb][:, 0:260], op=ALU.add))
            self.c1aw[pr] = p.inc("v", p.s_ve, 1)
            self.c1mini_free[mb] = self.c1aw[pr]

        for pr in range(8):
            for c in range(NCH):
                b = p.pool_next()
                p.pool_acquire(b)
                col = pr * TIL + c * 128
                p.wait("t", p.s_sc, kevac_done)
                p.t(lambda e, b=b, col=col: e.matmul(
                    p.pool[b][:, :], p.prj_fm[:, col:col + 128], p.projz[:, :]))
                rv = p.inc("t", p.s_pe, 1)
                eb = p.e1b[pr % 2][c]
                p.wait("s", p.s_pe, rv)
                war = c1grp.get(pr - 2)
                if war:
                    p.wait("s", p.s_pe, war)
                p.s(lambda e, b=b, eb=eb: e.activation(
                    out=eb[:, :], in_=p.pool[b][:, :], func=AF.Exp,
                    scale=NORMZ, bias=-LN16))
                ev = p.inc("s", p.s_sc, 1)
                p.pool_release(b, p.s_sc, ev)
            exp_done[pr] = p.s_sc.n
            if pr >= 1:
                emit_c1(pr - 1)
        emit_c1(7)
        self.vx_war = c1grp[7]
        self.last_c1pair = {-2: c1grp[6], -1: c1grp[7]}
        self.prj_war = c1grp[7]
        self.last_c1evac = max(self.c1aw)
        return nxt_mark

    def phase_a_seq(self, s, xfm_first=None):
        p = self
        xfm = xfm_first
        for t in range(NT):
            if xfm is None:
                xfm = self.prologue_tile(s, t, first=(s == 0 and t == 0))
            if t + 1 < NT:
                nxt = (s, t + 1, (s, t + 2) if t + 2 < NT else
                       ((s + 1, 0) if s + 1 < BPC else None))
            elif s + 1 < BPC:
                nxt = (s + 1, 0, (s + 1, 1))
            else:
                nxt = None
            xfm = self.phase_a_tile(s, t, xfm, nxt)
        p.wait("v", p.s_ve, self.last_c1evac)
        for mc in range(2):
            iap = p.c1a[:, :].rearrange("p (h j) -> p h j", j=130)[:, :, mc * 65:(mc + 1) * 65]
            oap = p.c1e[s][mc][:, :].rearrange("p (h j) -> p h j", j=66)[:, :, 0:65]
            p.v(lambda e, iap=iap, oap=oap: e.tensor_copy(oap, iap))
            p.inc("v", p.s_ve, 1)
            ap1 = p.c1e[s][mc][:, :].rearrange("p (h j) -> p h j", j=66)[:, :, 65:66]
            p.v(lambda e, ap1=ap1: e.memset(ap1, 1.0))
            p.inc("v", p.s_ve, 1)
        c1e_done = p.s_ve.n
        self.c1a_free = c1e_done
        self.c1aw = [0] * 8
        b = p.pool_next()
        p.pool_acquire(b)
        p.wait("t", p.s_ve, c1e_done)
        for oc in range(OC):
            p.t(lambda e, b=b, oc=oc, s=s: e.transpose(
                p.pool[b][:, oc:oc + 1], p.vs_row[s][0:1, oc * 128:(oc + 1) * 128],
                p.identf[0:1, 0:1]))
        rv = p.inc("t", p.s_pe, 1)
        p.wait("v", p.s_pe, rv)
        p.v(lambda e, b=b, s=s: e.tensor_copy(p.vst_fm[s][:, :], p.pool[b][:, 0:OC]))
        vst = p.inc("v", p.s_ve, 1)
        p.pool_release(b, p.s_ve, vst)
        self.vs_war = [0, 0]
        self.vst_done = vst
        p.v(lambda e: e.reduce_max(p.dmax1[0:16, 0:1], p.dmaxb[0:16, 0:NT], axis=AXL.X))
        dm = p.inc("v", p.s_ve, 1)
        b = p.pool_next()
        p.pool_acquire(b)
        p.wait("t", p.s_ve, dm)
        p.t(lambda e, b=b: e.transpose(p.pool[b][0:1, 0:16], p.dmax1[0:16, 0:1],
                                       p.identf[0:16, 0:16]))
        rv = p.inc("t", p.s_pe, 1)
        p.wait("v", p.s_pe, rv)
        p.v(lambda e, b=b, s=s: e.reduce_max(p.stabsc[0:1, s:s + 1],
                                             p.pool[b][0:1, 0:16], axis=AXL.X))
        sv = p.inc("v", p.s_ve, 1)
        p.pool_release(b, p.s_ve, sv)
        return xfm

    def sched_collective(self):
        p = self
        p.wait("v", p.s_ve, p.s_ve.n)
        p.v(lambda e: e.reduce_max(p.stabv[0:1, 0:1], p.stabsc[0:1, 0:2], axis=AXL.X))
        smx = p.inc("v", p.s_ve, 1)
        p.wait("v", p.s_ve, smx)
        src = p.stabv[0:1, 0:1].broadcast_to([1, 1, 15])
        dst = p.stabv[0:1, 1:16].rearrange("p (a j) -> p a j", a=1)
        p.v(lambda e, src=src, dst=dst: e.tensor_copy(dst, src))
        st_done = p.inc("v", p.s_ve, 1)
        p.wait("g", p.s_ve, st_done)
        p.g(lambda e: e.dma_start(out=p.ccin.ap(), in_=p.stabv[0:1, :]))
        p.inc("g", p.s_cc, 16)
        p.wait("g", p.s_cc, 16)
        p.g(lambda e: e.collective_compute(
            "AllReduce", ALU.max, replica_groups=[list(range(NCORES))],
            ins=[p.ccin.ap().opt()], outs=[p.ccout.ap().opt()]))
        p.inc("g", p.s_ccr, 1)
        p.wait("g", p.s_ccr, 1)
        p.g(lambda e: e.dma_start(out=p.stab_in[0:1, :], in_=p.ccout.ap()))
        p.inc("g", p.s_cc, 16)
        p.wait("g", p.s_cc, 32)
        b = p.pool_next()
        p.pool_acquire(b)
        p.wait("t", p.s_cc, 32)
        p.t(lambda e, b=b: e.matmul(p.pool[b][:, 0:16], p.onesf[0:1, :],
                                    p.stab_in[0:1, :]))
        rv = p.inc("t", p.s_pe, 1)
        p.wait("v", p.s_pe, rv)
        p.v(lambda e, b=b: e.tensor_copy(p.stab_rep[:, :], p.pool[b][:, 0:16]))
        bc = p.inc("v", p.s_ve, 1)
        p.pool_release(b, p.s_ve, bc)
        p.wait("s", p.s_ve, bc)
        p.s(lambda e: e.activation(out=p.epsp[:, :], in_=p.stab_rep[:, 0:1],
                                   func=AF.Exp, bias=LNEPSQ))
        ep = p.inc("s", p.s_sc, 1)
        p.wait("v", p.s_sc, ep)
        p.v(lambda e: e.tensor_scalar_mul(p.nep[:, :], p.epsp[:, :], float(N)))
        p.inc("v", p.s_ve, 1)
        p.v(lambda e: e.tensor_scalar_mul(p.epnk[:, :], p.epsp[:, :],
                                          float(256.0 * EPSQ * N)))
        self.eps_done = p.inc("v", p.s_ve, 1)

    def sched_wprime(self):
        p = self
        wst_free = {}
        for ic in range(OC):
            src = p.dt["wT"].ap()[ic * 128:(ic + 1) * 128, :]
            if ic >= 2:
                p.wait("y", p.s_ve, wst_free[ic - 2])
            p.y(lambda e, s2=src, ic=ic: e.dma_start(out=p.wtstage[ic % 2][:, :], in_=s2))
            wst = (p.s_wt[ic % 2], p.inc("y", p.s_wt[ic % 2], 16))
            for half in range(2):
                b = p.pool_next()
                p.pool_acquire(b)
                if ic == 0 and half == 0:
                    p.wait("t", *self.vrc_done)
                p.t(lambda e, b=b, ic=ic, half=half: e.matmul(
                    p.pool[b][:, :], p.vr_sb[0:32, ic * 128:(ic + 1) * 128],
                    p.c_sb[0:32, half * 512:(half + 1) * 512]))
                rv = p.inc("t", p.s_pe, 1)
                p.wait("v", p.s_pe, rv)
                p.wait("v", *wst)
                p.v(lambda e, b=b, ic=ic, half=half: e.tensor_tensor(
                    out=p.w_p[:, ic * D + half * 512: ic * D + (half + 1) * 512],
                    in0=p.pool[b][:, :],
                    in1=p.wtstage[ic % 2][:, half * 512:(half + 1) * 512],
                    op=ALU.add))
                wst_free[ic] = p.inc("v", p.s_ve, 1)
                p.pool_release(b, p.s_ve, wst_free[ic])
        self.wp_done = p.s_ve.n

    def sched_bsetup1(self, s):
        p = self
        pse = p.gen[2]
        if s == 0:
            if self.pdiag_war > 0:
                p.wait("t", p.s_ve, self.pdiag_war)
            if self.edr_war > 0:
                p.wait("t", p.s_sc, self.edr_war)
        csb = self.cs_sb
        for grp in range(2):
            for hh in range(8):
                h = grp * 8 + hh
                for mc in range(2):
                    p.t(lambda e, h=h, hh=hh, mc=mc, s=s: e.matmul(
                        pse[0:1, hh * 64:(hh + 1) * 64], p.onesb[:, 0:1],
                        p.c1e[s][mc][:, h * 66:h * 66 + 64],
                        start=(mc == 0), stop=(mc == 1)))
            gv = p.inc("t", p.s_pe, 1)
            p.wait("v", p.s_pe, gv)
            if grp == 1:
                p.wait("v", p.s_ve, self.csb_w)
            p.v(lambda e, grp=grp: e.tensor_copy(csb[0:1, grp * 512:(grp + 1) * 512],
                                                 pse[0:1, 0:512]))
            self.csb_w = p.inc("v", p.s_ve, 1)
            if grp == 0:
                p.wait("t", p.s_ve, self.csb_w)
        p.wait("t", p.s_ve, self.csb_w)
        if self.pbf_mark is not None:
            p.wait("t", *self.pbf_mark)
        for h in range(H):
            p.t(lambda e, h=h: e.transpose(
                p.pbf[(h % 2) * 64:(h % 2) * 64 + 64, 2 * (h // 2):2 * (h // 2) + 1],
                csb[0:1, h * 64:(h + 1) * 64], p.identb[0:1, 0:1]))
        cst = p.inc("t", p.s_pe, 1)
        p.wait("v", p.s_pe, cst)
        csrc = p.pbf[:, 0:2 * OC].rearrange("p (c k) -> p c k", k=2)[:, :, 0]
        p.v(lambda e, s=s, csrc=csrc: e.tensor_copy(p.csfm2[s][:, :], csrc))
        csf = p.inc("v", p.s_ve, 1)
        self.pbf_mark = (p.s_ve, csf)
        self.cs_marks[s] = csf
        for mc in range(2):
            gsap = p.c1e[s][mc][:, :].rearrange("p (h j) -> p h j", j=66)[:, :, 64]
            p.t(lambda e, gsap=gsap, mc=mc, s=s: e.matmul(
                pse[:, 16 + s * 16:32 + s * 16], p.onesb[:, :], gsap,
                start=(mc == 0), stop=(mc == 1)))
        self.k2s_marks[s] = p.inc("t", p.s_pe, 1)

    def sched_bsetup2(self, s):
        p = self
        pse = p.gen[2]
        p.wait("v", p.s_ve, self.eps_done)
        p.v(lambda e, s=s: e.tensor_scalar_mul(p.evst[:, :], p.vst_fm[s][:, :],
                                               p.epsp[:, 0:1]))
        p.inc("v", p.s_ve, 1)
        p.v(lambda e, s=s: e.tensor_scalar(
            out=p.evs256[:, :], in0=p.vst_fm[s][:, :], scalar1=p.epsp[:, 0:1],
            scalar2=float(256.0 * EPSQ), op0=ALU.mult, op1=ALU.mult))
        ev2 = p.inc("v", p.s_ve, 1)
        p.wait("v", p.s_ve, ev2)
        p.v(lambda e, s=s: e.scalar_tensor_tensor(
            out=p.k1_fm[:, :], in0=p.csfm2[s][:, :], scalar=EPSQ, in1=p.evs256[:, :],
            op0=ALU.mult, op1=ALU.add))
        k1f = p.inc("v", p.s_ve, 1)
        p.wait("v", p.s_pe, self.k2s_marks[s])
        p.v(lambda e, s=s: e.tensor_scalar(
            out=p.k2rep[s][:, :], in0=pse[:, 16 + s * 16:32 + s * 16], scalar1=EPSQ,
            scalar2=p.epnk[:, 0:1], op0=ALU.mult, op1=ALU.add))
        self.psetup_read = p.inc("v", p.s_ve, 1)
        p.v(lambda e, s=s: e.memset(p.usb[s][:, :], 0.0))
        um = p.inc("v", p.s_ve, 1)
        p.wait("v", p.s_ve, um)
        for c in range(OC):
            for colofs, srcT in ((2 * c, p.evst), (16 + 2 * c, p.k1_fm)):
                p.v(lambda e, c=c, colofs=colofs, srcT=srcT, s=s: e.tensor_copy(
                    p.usb[s][0:64, c * 32 + colofs:c * 32 + colofs + 1],
                    srcT[0:64, c:c + 1]))
                p.inc("v", p.s_ve, 1)
                p.v(lambda e, c=c, colofs=colofs, srcT=srcT, s=s: e.tensor_copy(
                    p.usb[s][64:128, c * 32 + colofs + 1:c * 32 + colofs + 2],
                    srcT[64:128, c:c + 1]))
                p.inc("v", p.s_ve, 1)
        u_done = p.s_ve.n
        for half in range(2):
            b = p.pool_next()
            p.pool_acquire(b)
            p.wait("t", p.s_ve, u_done)
            if s == 0 and half == 0:
                p.wait("t", *self.wo_done)
            for c in range(OC):
                p.t(lambda e, b=b, c=c, half=half, s=s: e.matmul(
                    p.pool[b][0:32, :], p.usb[s][:, c * 32:(c + 1) * 32],
                    p.w_o[:, c * D + half * 512: c * D + (half + 1) * 512],
                    start=(c == 0), stop=(c == OC - 1)))
            rv = p.inc("t", p.s_pe, 1)
            p.wait("v", p.s_pe, rv)
            p.v(lambda e, b=b, half=half, s=s: e.tensor_copy(
                p.v12t[s][0:32, half * 512:(half + 1) * 512], p.pool[b][0:32, :]))
            vv = p.inc("v", p.s_ve, 1)
            p.pool_release(b, p.s_ve, vv)
        self.bset_done = p.s_ve.n

    def _emit_head_tail(self, h, recv):
        p = self
        p.wait("v", p.s_ve, recv)
        qodd = p.qsden[:, h * 8:(h + 1) * 8].rearrange("p (n j) -> p n j", j=2)[:, :, 1]
        p.v(lambda e, h=h, qodd=qodd: e.tensor_tensor(
            out=p.qd3[:, :, h], in0=qodd, in1=p.qd3[:, :, 16 + h], op=ALU.mult))
        p.inc("v", p.s_ve, 1)
        npb = p.gen[h % 3]
        for ns in range(NCH):
            p.v(lambda e, h=h, ns=ns, npb=npb: e.tensor_scalar_mul(
                p.attn_tm[:, ns * D + h * 64: ns * D + (h + 1) * 64],
                npb[:, ns * 66:ns * 66 + 64],
                p.qd3t[:, ns * 32 + 16 + h: ns * 32 + 16 + h + 1]))
            p.inc("v", p.s_ve, 1)

    def pro_b(self, s, t, firstb=False):
        p = self
        xfm_done = self.prologue_tile(s, t)
        for oc in range(OC):
            b = p.pool_next()
            p.pool_acquire(b)
            p.wait("t", p.s_sc, xfm_done)
            if firstb and oc == 0:
                p.wait("t", *self.wq_done)
            for ic in range(OC):
                p.t(lambda e, b=b, ic=ic, oc=oc: e.matmul(
                    p.pool[b][:, :], p.w_q[:, ic * D + oc * 128: ic * D + (oc + 1) * 128],
                    p.xln_fm0[:, ic * TIL:(ic + 1) * TIL],
                    start=(ic == 0), stop=(ic == OC - 1)))
            rv = p.inc("t", p.s_pe, 1)
            p.wait("s", p.s_pe, rv)
            if oc == 0 and self.prj_war > 0:
                p.wait("s", p.s_pe, self.prj_war)
            p.s(lambda e, b=b, oc=oc: e.activation(
                out=p.prj_fm[:, oc * TIL:(oc + 1) * TIL], in_=p.pool[b][:, :],
                func=AF.Identity, bias=p.bqkv[:, oc:oc + 1]))
            qe = p.inc("s", p.s_sc, 1)
            p.pool_release(b, p.s_sc, qe)
        self.xfm_war = p.s_pe.n
        return p.s_sc.n

    def phase_b_tile(self, s, t, qevac_done, nxt):
        p = self
        tb = t % 2
        firstb = (s == 0 and t == 0)
        xc = p.x_tm[tb][:, :].rearrange("p (c f) -> p c f", f=D)
        h_done = {}
        np_free = {}
        rec_val = {}
        prev = None
        p.wait("v", p.s_sc, qevac_done)
        if self.attn_war > 0:
            p.wait("v", p.s_pe, self.attn_war)
        exp_done = {}

        def emit_dash(h):
            hh = h % 2
            pr = h // 2
            r = h % 4
            gq3 = p.gq[r][:, :].rearrange("p (m n) -> p m n", n=TIL)
            for nh in range(2):
                b = p.pool_next()
                p.pool_acquire(b)
                p.wait("t", p.s_sc, qevac_done)
                for mc in range(2):
                    p.t(lambda e, b=b, mc=mc, hh=hh, pr=pr, nh=nh: e.matmul(
                        p.pool[b][:, mc * 256:(mc + 1) * 256],
                        p.projzq[:, hh * 256 + mc * 128: hh * 256 + (mc + 1) * 128],
                        p.prj_fm[:, pr * TIL + nh * 256: pr * TIL + (nh + 1) * 256],
                        start=True, stop=True))
                rv = p.inc("t", p.s_pe, 1)
                p.wait("s", p.s_pe, rv)
                if h >= 4 and nh == 0:
                    p.wait("s", p.s_pe, h_done[h - 4])
                elif firstb and h < 4 and nh == 0:
                    p.wait("s", p.s_ve, self.c1a_free)
                oap = gq3[:, :, nh * 256:(nh + 1) * 256]
                iap = p.pool[b][:, :].rearrange("p (m n) -> p m n", n=256)
                p.s(lambda e, oap=oap, iap=iap: e.activation(
                    out=oap, in_=iap, func=AF.Exp, scale=NORMZ, bias=-LN16))
                ev = p.inc("s", p.s_sc, 1)
                p.pool_release(b, p.s_sc, ev)
            exp_done[h] = p.s_sc.n

        emit_dash(0)
        for h in range(H):
            if h + 1 < H:
                emit_dash(h + 1)
            gq3 = p.gq[h % 4][:, :].rearrange("p (m n) -> p m n", n=TIL)
            npb = p.gen[h % 3]
            p.wait("t", p.s_sc, exp_done[h])
            if firstb and h == 0:
                p.wait("t", p.s_ve, self.bset_done)
            if h >= 3:
                p.wait("t", p.s_ve, np_free[h - 3])
            elif self.np_war[h % 3] > 0:
                p.wait("t", p.s_ve, self.np_war[h % 3])
            for ns in range(NCH):
                for mc in range(2):
                    p.t(lambda e, npb=npb, ns=ns, mc=mc, gq3=gq3, h=h, s=s: e.matmul(
                        npb[:, ns * 66:(ns + 1) * 66],
                        gq3[:, mc, ns * 128:(ns + 1) * 128],
                        p.c1e[s][mc][:, h * 66:(h + 1) * 66],
                        start=(mc == 0), stop=(mc == 1)))
            h_done[h] = p.inc("t", p.s_pe, 1)
            if True:
                p.wait("v", p.s_pe, h_done[h])
                srcA = npb[:, 0:264].rearrange("p (n j) -> p n j", j=66)[:, :, 64:66]
                dstA = p.qsden[:, h * 8:(h + 1) * 8].rearrange("p (n j) -> p n j", j=2)
                p.v(lambda e, srcA=srcA, dstA=dstA: e.tensor_copy(dstA, srcA))
                gat = p.inc("v", p.s_ve, 1)
                p.wait("s", p.s_ve, gat)
                qodd = p.qsden[:, h * 8:(h + 1) * 8].rearrange("p (n j) -> p n j", j=2)[:, :, 1]
                p.s(lambda e, h=h, qodd=qodd, s=s: e.activation(
                    out=p.tden[:, h * 4:(h + 1) * 4], in_=qodd, func=AF.Identity,
                    scale=p.nep[:, 0:1], bias=p.k2rep[s][:, h:h + 1]))
                td = p.inc("s", p.s_sc, 1)
                p.wait("v", p.s_sc, td)
                qev = p.qsden[:, h * 8:(h + 1) * 8].rearrange("p (n j) -> p n j", j=2)[:, :, 0]
                p.v(lambda e, h=h, qev=qev: e.tensor_tensor(
                    out=p.tden[:, h * 4:(h + 1) * 4], in0=p.tden[:, h * 4:(h + 1) * 4],
                    in1=qev, op=ALU.add))
                tav = p.inc("v", p.s_ve, 1)
                p.wait("v", p.s_ve, tav)
                p.v(lambda e, h=h: e.reciprocal(p.qd3[:, :, 16 + h],
                                                p.tden[:, h * 4:(h + 1) * 4]))
                rec_val[h] = p.inc("v", p.s_ve, 1)
                if prev is not None:
                    self._emit_head_tail(prev, rec_val[prev])
                    np_free[prev] = p.s_ve.n
                prev = h
            if h % 2 == 1 and h < 15:
                self._run_carry(1 if h < 13 else 2)
        self._emit_head_tail(prev, rec_val[prev])
        np_free[prev] = p.s_ve.n
        self.np_war = [np_free[15], np_free[13], np_free[14]]
        self.prj_war = h_done[15]
        comb_done = p.s_ve.n
        self._run_carry(99)
        ap2s = {}
        for ns in range(NCH):
            k = ns % 2
            af = p.attn_fm[k]
            bqd = p.pool_next()
            p.pool_acquire(bqd)
            p.wait("t", p.s_ve, comb_done)
            p.t(lambda e, b=bqd, ns=ns: e.transpose(
                p.pool[b][0:32, 0:128], p.qd3t[:, ns * 32:(ns + 1) * 32],
                p.identf[:, :]))
            rv = p.inc("t", p.s_pe, 1)
            p.wait("v", p.s_pe, rv)
            p.v(lambda e, b=bqd, ns=ns: e.tensor_copy(
                p.qd_fm[0:32, ns * 128:(ns + 1) * 128], p.pool[b][0:32, 0:128]))
            qdf = p.inc("v", p.s_ve, 1)
            p.pool_release(bqd, p.s_ve, qdf)
            esw = [(p.s_pe, self.afm_war[k])] if self.afm_war[k] else []
            self.emit_transposes_bf16(p.attn_tm, ns * D, af,
                                      extra_t_waits=[(p.s_ve, comb_done)],
                                      extra_s_waits=esw)
            afm_done = p.s_sc.n
            li = p.lnin0
            lo = p.lnout0
            for half in range(2):
                b = p.pool_next()
                p.pool_acquire(b)
                p.wait("t", p.s_sc, afm_done)
                if half == 0:
                    p.wait("t", p.s_ve, qdf)
                for ic in range(OC):
                    p.t(lambda e, b=b, ic=ic, half=half, af=af: e.matmul(
                        p.pool[b][:, :], af[:, ic * 128:(ic + 1) * 128],
                        p.w_o[:, ic * D + half * 512: ic * D + (half + 1) * 512],
                        start=(ic == 0), stop=False))
                p.t(lambda e, b=b, half=half, ns=ns, s=s: e.matmul(
                    p.pool[b][:, :], p.qd_fm[0:32, ns * 128:(ns + 1) * 128],
                    p.v12t[s][0:32, half * 512:(half + 1) * 512],
                    start=False, stop=True))
                rv = p.inc("t", p.s_pe, 1)
                p.wait("v", p.s_pe, rv)
                if half == 0 and self.lnin_war is not None:
                    p.wait("v", *self.lnin_war)
                if firstb and ns == 0 and half == 0:
                    p.wait("v", *self.bo_done)
                p.v(lambda e, b=b, half=half, li=li: e.tensor_tensor(
                    out=li[:, half * 512:(half + 1) * 512], in0=p.pool[b][:, :],
                    in1=p.bo_rep[:, half * 512:(half + 1) * 512], op=ALU.add))
                wv = p.inc("v", p.s_ve, 1)
                p.pool_release(b, p.s_ve, wv)
            wo_done = p.s_ve.n
            self.afm_war[k] = p.s_pe.n
            ap1 = self.emit_ln_chunk(tb, 4 + ns, li[:, :], lo[:, :],
                                     waits_v=[(p.s_ve, wo_done),
                                              (p.s_gp, self.lnout_war)],
                                     waits_s=[(p.s_ve, wo_done)])
            self.lnin_war = ap1
            p.wait("v", *ap1)
            p.v(lambda e, ns=ns, lo=lo, tb=tb: e.tensor_tensor(
                out=p.x_tm[tb][:, ns * D:(ns + 1) * D],
                in0=p.x_tm[tb][:, ns * D:(ns + 1) * D], in1=lo[:, :], op=ALU.add))
            x1_done = p.inc("v", p.s_ve, 1)
            wv2 = [(p.s_ve, x1_done)]
            if self.xln2b_war[ns] is not None:
                wv2.append(self.xln2b_war[ns])
            ap2s[ns] = self.emit_ln_chunk(
                tb, 8 + ns, p.x_tm[tb][:, ns * D:(ns + 1) * D],
                p.xln2b[:, ns * D:(ns + 1) * D],
                waits_v=wv2, waits_s=[(p.s_ve, x1_done)])
        self.attn_war = p.s_pe.n
        pro_next = None
        if nxt is not None:
            pro_next = self.pro_b(nxt[0], nxt[1])

        def mk_e2(ns, ap2):
            def run():
                esw = [(p.s_pe, self.x2f_war)] if self.x2f_war else []
                _, tpe = self.emit_transposes_bf16(p.xln2b, ns * D, p.xln2_fm0,
                                                   extra_t_waits=[ap2],
                                                   extra_s_waits=esw)
                self.x2f_done = p.s_sc.n
                self.xln2b_war[ns] = (p.s_pe, tpe)
            return run

        def mk_e3(ns, tb, firstb):
            def run():
                li = p.lnin0
                lo = p.lnout0
                x2f = p.xln2_fm0
                for half in range(2):
                    b = p.pool_next()
                    p.pool_acquire(b)
                    p.wait("t", p.s_sc, self.x2f_done)
                    if firstb and ns == 0 and half == 0:
                        p.wait("t", p.s_ve, self.wp_done)
                    for ic in range(OC):
                        p.t(lambda e, b=b, ic=ic, half=half, x2f=x2f: e.matmul(
                            p.pool[b][:, :], x2f[:, ic * 128:(ic + 1) * 128],
                            p.w_p[:, ic * D + half * 512: ic * D + (half + 1) * 512],
                            start=(ic == 0), stop=(ic == OC - 1)))
                    rv = p.inc("t", p.s_pe, 1)
                    p.wait("v", p.s_pe, rv)
                    if half == 0 and self.lnin_war is not None:
                        p.wait("v", *self.lnin_war)
                    if firstb and ns == 0 and half == 0:
                        p.wait("v", *self.brep_done)
                    p.v(lambda e, b=b, half=half, li=li: e.tensor_tensor(
                        out=li[:, half * 512:(half + 1) * 512], in0=p.pool[b][:, :],
                        in1=p.b_rep[:, half * 512:(half + 1) * 512], op=ALU.add))
                    fv = p.inc("v", p.s_ve, 1)
                    p.pool_release(b, p.s_ve, fv)
                ffn_done = p.s_ve.n
                self.x2f_war = p.s_pe.n
                ap3 = self.emit_ln_chunk(tb, 12 + ns, li[:, :], lo[:, :],
                                         waits_v=[(p.s_ve, ffn_done),
                                                  (p.s_gp, self.lnout_war)],
                                         waits_s=[(p.s_ve, ffn_done)])
                self.lnin_war = ap3
                p.wait("g", *ap3)
                p.g(lambda e, ns=ns, lo=lo, tb=tb: e.tensor_tensor(
                    out=p.x_tm[tb][:, ns * D:(ns + 1) * D],
                    in0=p.x_tm[tb][:, ns * D:(ns + 1) * D], in1=lo[:, :],
                    op=ALU.add))
                self.lnout_war = p.inc("g", p.s_gp, 1)
            return run

        def mk_out(s2, t2, tb):
            def run():
                out_ready = p.s_gp.n
                p.wait("s", p.s_gp, out_ready)
                srcv = p.x_tm[tb][:, :].rearrange("p (c f) -> p c f", f=D)
                dst = p.out_h.ap()[s2, t2 * TIL:(t2 + 1) * TIL, :].rearrange(
                    "(c p) f -> p c f", p=128)
                p.s(lambda e, d=dst, s3=srcv: e.dma_start(out=d, in_=s3))
                stv = p.inc("s", p.s_o[tb], 16)
                p.x_rel[tb] = [(p.s_o[tb], stv)]
            return run

        for ns in range(NCH):
            self.carry_q.append(mk_e2(ns, ap2s[ns]))
            self.carry_q.append(mk_e3(ns, tb, firstb))
        self.carry_q.append(mk_out(s, t, tb))
        return pro_next

    def schedule(self):
        p = self
        self.sched_loads()
        p.wait("s", *self.consts_done)
        if self.phases == "a0":
            xfm = self.prologue_tile(0, 0, first=True)
            self.phase_a_tile(0, 0, xfm, None)
            self._finish()
            return
        xfm = None
        for s in range(BPC):
            xfm = self.phase_a_seq(s, xfm)
        if self.phases == "a":
            self._finish()
            return
        self.sched_collective()
        self.sched_wprime()
        a_end = p.s_pe.n
        self.prj_war = a_end
        self.xfm_war = a_end
        if self.phases == "ac":
            self._finish()
            return
        p.wait("g", p.s_pe, a_end)
        for dst, nm, sem in ((self.wb0, "wqT", p.s_wq2), (self.wb1, "woT", p.s_wo2)):
            srcw = self.dt[nm].ap().rearrange("(c p) o -> p c o", p=128)
            dstw = dst[:, :].rearrange("p (c o) -> p c o", o=D)
            p.g(lambda e, d=dstw, s2=srcw: e.dma_start(out=d, in_=s2))
            p.inc("g", sem, 16)
        self.wq_done = (p.s_wq2, p.s_wq2.n)
        self.wo_done = (p.s_wo2, p.s_wo2.n)
        for s in range(BPC):
            self.sched_bsetup1(s)
        pro = self.pro_b(0, 0, firstb=True)
        for s in range(BPC):
            self.sched_bsetup2(s)
        self.np_war = [0, 0, self.psetup_read]
        self.xln2b_war = [(p.s_ve, self.wp_done)] * NCH
        if self.phases == "w":
            self._finish()
            return
        tiles = [(s, t) for s in range(BPC) for t in range(NT)]
        for i, (s, t) in enumerate(tiles):
            if i + 1 < len(tiles):
                nxt = (*tiles[i + 1], tiles[i + 2] if i + 2 < len(tiles) else None)
            else:
                nxt = None
            pro = self.phase_b_tile(s, t, pro, nxt)
        self._run_carry(99)
        self._finish()

    def _finish(self):
        p = self
        if self.debug:
            p.wait("g", p.s_ve, p.s_ve.n)
            p.wait("g", p.s_sc, p.s_sc.n)
            p.wait("g", p.s_pe, p.s_pe.n)
            p.wait("g", p.s_gp, p.s_gp.n)
            order = ["a0", "a", "ac", "w", "b0", "b1", "full"]
            lvl = order.index(self.phases)
            taps = [("tap_prj", self.prj_fm), ("tap_vs0", self.vs_row[0])]
            if lvl >= 1:
                taps += [("tap_c1e0", self.c1e[0][0]), ("tap_c1e1", self.c1e[0][1])]
            if lvl >= 2:
                taps += [("tap_stab", self.stab_in), ("tap_eps", self.epsp)]
            if lvl >= 3:
                taps += [("tap_k2rep", self.k2rep[0]), ("tap_v12", self.v12t[0])]
            for nm, src in taps:
                p.g(lambda e, nm=nm, src=src: e.dma_start(
                    out=self.tap_h[nm].ap(), in_=src[:, :]))
                p.inc("g", p.s_tap, 16)
        p.wait("g", p.s_o[0], p.s_o[0].n)
        p.wait("g", p.s_o[1], p.s_o[1].n)
        p.wait("g", p.s_tap, p.s_tap.n)


def _projz(proj):
    z = np.zeros((128, 512), dtype=np.float32)
    z[0:64, 0:256] = proj.T
    z[64:128, 256:512] = proj.T
    return z


def _projzq(proj):
    z = np.zeros((128, 512), dtype=np.float32)
    z[0:64, 0:256] = proj.T
    z[64:128, 256:512] = proj.T
    return z


def make_inputs(inputs):
    f32 = np.float32
    x = np.ascontiguousarray(np.asarray(inputs["L0_attn_hidden_states"], dtype=f32))
    wq = np.asarray(inputs["L0_attn_Wq"], dtype=f32)
    wk = np.asarray(inputs["L0_attn_Wk"], dtype=f32)
    wv = np.asarray(inputs["L0_attn_Wv"], dtype=f32)
    wo = np.asarray(inputs["L0_attn_Wo"], dtype=f32)
    bq = np.asarray(inputs["L0_attn_bq"], dtype=f32)
    bk = np.asarray(inputs["L0_attn_bk"], dtype=f32)
    bv = np.asarray(inputs["L0_attn_bv"], dtype=f32)
    bo = np.asarray(inputs["L0_attn_bo"], dtype=f32)
    proj = np.asarray(inputs["L0_attn_proj"], dtype=f32)
    W = np.asarray(inputs["L0_ffn_W"], dtype=f32)
    bffn = np.asarray(inputs["L0_ffn_b"], dtype=f32)
    Vr = np.asarray(inputs["L0_ffn_V_r"], dtype=f32)
    C = np.asarray(inputs["L0_ffn_C"], dtype=f32)

    common = {
        "wqT": np.ascontiguousarray(wq.T), "wkT": np.ascontiguousarray(wk.T),
        "wvT": np.ascontiguousarray(wv.T), "woT": np.ascontiguousarray(wo.T),
        "wT": np.ascontiguousarray(W.T),
        "vrT": np.ascontiguousarray(Vr.T), "cT": np.ascontiguousarray(C.T),
        "projz": _projz(proj), "projzq": _projzq(proj),
        "identf": np.eye(128, dtype=f32),
        "identb": np.eye(128, dtype=f32),
        "cvals": np.ascontiguousarray(np.broadcast_to(
            np.array([EPS_LN, -LN16, LNEPSQ], dtype=f32), (128, 3))),
        "ones": np.ones((128, 128), dtype=f32),
        "bo_rep": np.ascontiguousarray(np.broadcast_to(bo, (128, D))),
        "b_rep": np.ascontiguousarray(np.broadcast_to(bffn, (128, D))),
        "bv_rep": np.ascontiguousarray(np.broadcast_to(bv, (128, D))),
    }
    bqkv = np.zeros((128, 3 * OC), dtype=f32)
    for i, bb in enumerate((bq, bk, bv)):
        bqkv[:, i * OC:(i + 1) * OC] = bb.reshape(OC, 128).T
    common["bqkv"] = bqkv
    ediag = np.zeros((128, 128), dtype=f32)
    for c in range(OC):
        ediag[0:64, c * 16 + 2 * c] = -0.5
        ediag[64:128, c * 16 + 2 * c + 1] = -0.5
    common["ediag"] = ediag

    in_maps = []
    for core in range(NCORES):
        m = dict(common)
        m["x"] = np.ascontiguousarray(x[core * BPC:(core + 1) * BPC])
        in_maps.append(m)
    return in_maps


_CACHE = {}


def get_nc(debug=False):
    key = ("nc", debug)
    if key not in _CACHE:
        b = Builder(debug=debug)
        b.emit()
        _CACHE[key] = b.nc
    return _CACHE[key]


def kernel(**inputs):
    in_maps = make_inputs(inputs)
    nc = get_nc()
    res = run_bass_kernel_spmd(nc, in_maps, core_ids=list(range(NCORES)))
    return np.concatenate([np.asarray(res.results[c]["out"]) for c in range(NCORES)],
                          axis=0)


# version 13 (final)
# speedup vs baseline: 1.4708x; 1.0524x over previous
import contextlib
import numpy as np
import ml_dtypes
import concourse.bass as bass
import concourse.mybir as mybir
from concourse.bass_utils import run_bass_kernel_spmd

F32 = mybir.dt.float32
BF16 = mybir.dt.bfloat16
AF = mybir.ActivationFunctionType
ALU = mybir.AluOpType
AXL = mybir.AxisListType

B, N, D = 16, 4096, 1024
H, DH, M = 16, 64, 256
NCORES = 8
BPC = B // NCORES
TIL = 512
NT = N // TIL
NCH = TIL // 128
OC = D // 128

NORMZ = float(DH ** -0.25)
LN16 = float(np.log(16.0))
EPSQ = float((M ** -0.5) * 1e-3)
LNEPSQ = float(np.log(EPSQ))
EPS_LN = 1e-5
INV_D = 1.0 / D


class Sem:
    def __init__(self, h):
        self.h = h
        self.n = 0


class Builder:
    def __init__(self, debug=False, phases="full", stop=99):
        self.debug = debug
        self.phases = phases
        self.stop = stop
        self.nc = bass.Bass("TRN2", target_bir_lowering=False)
        self.q = {k: [] for k in "tvsgy"}
        self._can_inc = {k: False for k in "tvsgy"}
        self.ctx = contextlib.ExitStack()
        self.dt = {}
        self.build()

    def _ins(self, eng, fn):
        self.q[eng].append(fn)
        self._can_inc[eng] = True

    def t(self, fn): self._ins("t", fn)
    def v(self, fn): self._ins("v", fn)
    def s(self, fn): self._ins("s", fn)
    def g(self, fn): self._ins("g", fn)
    def y(self, fn): self._ins("y", fn)

    def wait(self, eng, sem, val):
        if val > sem.n:
            raise RuntimeError(f"wait {val} > scheduled {sem.n}")
        if val > 0:
            self.q[eng].append(lambda e, s=sem.h, v=val: e.wait_ge(s, v))
            self._can_inc[eng] = False

    def inc(self, eng, sem, k=1):
        if not self._can_inc[eng]:
            raise RuntimeError("inc must ride on a fresh datapath instruction")
        fn = self.q[eng].pop()
        self.q[eng].append(lambda e, f=fn, s=sem.h, k=k: f(e).then_inc(s, k))
        self._can_inc[eng] = False
        sem.n += k
        return sem.n

    def emit(self):
        with self.nc.Block() as blk:
            for key, attr in (("y", "sync"), ("g", "gpsimd"), ("s", "scalar"),
                              ("v", "vector"), ("t", "tensor")):
                fns = self.q[key]
                if fns:
                    getattr(blk, attr)(lambda e, fns=fns: [f(e) for f in fns])

    def build(self):
        nc, ctx = self.nc, self.ctx
        sb = lambda name, shape, dtype: ctx.enter_context(nc.sbuf_tensor(name, shape, dtype))
        psr = lambda name, shape: ctx.enter_context(nc.psum_tensor(name, shape, F32))
        mks = lambda name: Sem(ctx.enter_context(nc.semaphore(name)))

        def din(name, shape):
            self.dt[name] = nc.dram_tensor(name, shape, F32, kind="ExternalInput")
        din("x", [BPC, N, D])
        for nm in ("wqT", "wkT", "wvT", "woT", "wT"):
            din(nm, [D, D])
        din("vrT", [32, D]); din("cT", [32, D])
        din("bqkv", [128, 3 * OC])
        din("bo_rep", [128, D]); din("b_rep", [128, D]); din("bv_rep", [128, D])
        din("projz", [128, 512]); din("projzq", [128, 512])
        din("identf", [128, 128])
        din("ediag", [128, 128]); din("ones", [128, 128])
        din("identb", [128, 128]); din("cvals", [128, 3])
        self.out_h = nc.dram_tensor("out", [BPC, N, D], F32, kind="ExternalOutput")
        self.ccin = nc.dram_tensor("ccin", [1, 16], F32)
        self.ccout = nc.dram_tensor("ccout", [1, 16], F32)
        if self.debug:
            self.tap_h = {nm: nc.dram_tensor(nm, shape, F32, kind="ExternalOutput")
                          for nm, shape in [
                              ("tap_c1e0", [128, H * 66]), ("tap_c1e1", [128, H * 66]),
                              ("tap_vs0", [1, D]), ("tap_stab", [1, 16]),
                              ("tap_prj", [128, OC * TIL]),
                              ("tap_k2rep", [128, H]), ("tap_v12", [32, D]),
                              ("tap_eps", [128, 1])]}

        self.wb0 = sb("wb0", [128, OC * D], BF16)
        self.wb1 = sb("wb1", [128, OC * D], BF16)
        self.wb2 = sb("wb2", [128, OC * D], BF16)
        self.w_k = self.w_q = self.wb0
        self.w_v = self.w_o = self.wb1
        self.w_p = self.wb2
        self.projz = sb("projzS", [128, 512], BF16)
        self.projzq = sb("projzqS", [128, 512], BF16)
        self.identf = sb("identfS", [128, 128], F32)
        self.ediag = sb("ediagS", [128, 128], BF16)
        self.onesb = sb("onesbS", [128, 128], BF16)
        self.bqkv = sb("bqkvS", [128, 3 * OC], F32)
        self.cvalsb = sb("cvalsbS", [128, 3], F32)
        nc.const_aps.register(F32, EPS_LN, self.cvalsb[:, 0:1])
        nc.const_aps.register(F32, -LN16, self.cvalsb[:, 1:2])
        nc.const_aps.register(F32, LNEPSQ, self.cvalsb[:, 2:3])
        self.bo_rep = sb("bo_repS", [128, D], F32)
        self.b_rep = sb("b_repS", [128, D], F32)
        self.bv_rep = sb("bv_repS", [128, D], F32)
        self.onesf = sb("onesfS", [1, 128], F32)
        self.vr_sb = sb("vr_sb", [32, D], BF16)
        self.c_sb = sb("c_sb", [32, D], BF16)

        self.c1a = sb("c1a", [128, 32 * 65], F32)
        self.c1e = [[sb(f"c1e_{s}_{mc}", [128, H * 66], BF16) for mc in range(2)]
                    for s in range(BPC)]
        self.vs_row = [sb(f"vs_{s}", [1, D], F32) for s in range(BPC)]
        self.vst_fm = [sb(f"vst_{s}", [128, OC], F32) for s in range(BPC)]
        self.usb = [sb(f"usb_{s}", [128, 256], BF16) for s in range(BPC)]
        self.v12t = [sb(f"v12t_{s}", [32, D], BF16) for s in range(BPC)]
        self.k2rep = [sb(f"k2rep_{s}", [128, H], F32) for s in range(BPC)]
        self.stabv = sb("stabvS", [1, 16], F32)
        self.stab_in = sb("stab_inS", [1, 16], F32)
        self.stab_rep = sb("stab_repS", [128, 16], F32)
        self.epsp = sb("epspS", [128, 1], F32)
        self.nep = sb("nepS", [128, 1], F32)
        self.epnk = sb("epnkS", [128, 1], F32)
        self.evst = sb("evstS", [128, OC], F32)
        self.evs256 = sb("evs256S", [128, OC], F32)
        self.k1_fm = sb("k1_fmS", [128, OC], F32)
        self.cs_sb = sb("cs_sbS", [1, 1024], BF16)
        self.csfm2 = [sb(f"csfmS{i}", [128, OC], F32) for i in range(BPC)]
        self.csb_w = 0
        self.cs_marks = {}
        self.k2s_marks = {}
        self.stabsc = sb("stabscS", [1, 2], F32)

        self.x_tm = [sb(f"x_tm{i}", [128, NCH * D], F32) for i in range(2)]
        self.xlnb = sb("xlnb", [128, NCH * D], BF16)
        self.xln_fm0 = sb("xln_fm0", [128, OC * TIL], BF16)
        self.prj_fm = sb("prj_fmS", [128, OC * TIL], BF16)
        self.k2b = [sb(f"k2b{i}", [128, TIL], BF16) for i in range(2)]
        self.e1b0 = [sb(f"e1b0_{i}", [128, TIL], BF16) for i in range(4)]
        self.vxb = sb("vxbS", [128, NCH * H * 65], BF16)
        self.edrep0 = sb("edrep0", [128, H * 65], BF16)
        self.ed_tm = sb("ed_tmS", [128, NCH * 16], F32)
        self.ed_rows = sb("ed_rowsS", [16, TIL], F32)
        self.dmaxb = sb("dmaxbS", [16, NT], F32)
        self.dmax1 = sb("dmax1S", [16, 1], F32)
        self.stat = [sb(f"stat{i}", [128, 4 * 16], F32) for i in range(2)]
        self.bst = sb("bst", [128, 12], F32)
        self.bst_war = 0
        self.junk = sb("junkS", [128, 4], F32)
        c1a_addr = nc.lookup_mloc(self.c1a).addr
        self.gq = [nc.alloc_sbuf_tensor_at(f"gq{i}", [128, 2 * TIL], BF16,
                                           offset=c1a_addr + i * 2048)
                   for i in range(4)]
        self.qd3t = sb("qd3S", [128, NCH * 32], F32)
        self.qsden = sb("qsdenS", [128, H * 8], F32)
        self.tden = sb("tdenS", [128, H * 4], F32)
        self.attn_tm = sb("attn_tmS", [128, NCH * D], BF16)
        at_addr = nc.lookup_mloc(self.attn_tm).addr
        self.e1b = [self.e1b0,
                    [nc.alloc_sbuf_tensor_at(f"e1b1_{i}", [128, TIL], BF16,
                                             offset=at_addr + i * 1024)
                     for i in range(4)]]
        self.identb = sb("identbS", [128, 128], BF16)
        self.attn_fm = [sb(f"attn_fm{i}", [128, OC * 128], BF16)
                        for i in range(2)]
        self.qd_fm = sb("qd_fmS", [32, NCH * 128], BF16)
        self.lnin0 = sb("lnin0", [128, D], F32)
        self.lnout0 = sb("lnout0", [128, D], F32)
        self.xln2b = sb("xln2b", [128, NCH * D], BF16)
        self.xln2_fm0 = sb("xln2_fm0", [128, OC * 128], BF16)
        xb_addr = nc.lookup_mloc(self.xln2b).addr
        self.wtstage = [nc.alloc_sbuf_tensor_at(f"wtst{i}", [128, D], F32,
                                                offset=xb_addr + i * 4096)
                        for i in range(2)]

        self.pool = [psr(f"poolP{i}", [128, 512]) for i in range(3)]
        self.gen = [psr(f"genP{i}", [128, 512]) for i in range(3)]
        self.pbf = ctx.enter_context(nc.psum_tensor("pbfP", [128, 1024], BF16))
        self.pbf2 = ctx.enter_context(nc.psum_tensor("pbf2P", [128, 1024], BF16))

        self.s_x = [mks("s_x0"), mks("s_x1")]
        self.s_cst = mks("s_cst"); self.s_wk = mks("s_wk"); self.s_wv = mks("s_wv")
        self.s_vrc = mks("s_vrc"); self.s_bo = mks("s_bo"); self.s_br = mks("s_br")
        self.s_wq2 = mks("s_wq2"); self.s_wo2 = mks("s_wo2")
        self.s_wt = [mks("s_wt0"), mks("s_wt1")]
        self.s_o = [mks("s_o0"), mks("s_o1")]
        self.s_pe = mks("s_pe"); self.s_sc = mks("s_sc"); self.s_ve = mks("s_ve")
        self.s_gp = mks("s_gp"); self.s_cc = mks("s_cc"); self.s_tap = mks("s_tap")
        self.s_ccr = mks("s_ccr")
        self.pool_mark = [None, None, None]
        self.pbf_mark = None
        self.pbf2_mark = None
        self._rr = 0

        self.vx3 = self.vxb[:, :].rearrange("p (c h j) -> p c h j", c=NCH, j=65)
        self.bv3 = self.bv_rep[:, :].rearrange("p (h j) -> p h j", j=64)
        self.qd3 = self.qd3t[:, :].rearrange("p (c k) -> p c k", k=32)

        self.xpref = {}
        self.x_rel = [None, None]
        self.xlnT_mark = [0] * NCH
        self.xfm_war = 0
        self.prj_war = 0
        self.pdiag_war = 0
        self.edr_war = 0
        self.vx_war = 0
        self.k2_war = 0
        self.c1mini_free = [0, 0]
        self.c1aw = [0] * 8
        self.vs_war = [0, 0]
        self.afm_war = [0, 0]
        self.xln2b_war = [None] * 4
        self.x2f_done = 0
        self.carry_q = []
        self.attn_war = 0
        self.x2f_war = 0
        self.lnin_war = None
        self.lnout_war = 0
        self.np_war = [0, 0, 0]
        self.last_c1pair = {}
        self.ln_mark = [dict(), dict()]

        self.schedule()

    def pool_next(self):
        b = self._rr % 3
        self._rr += 1
        return b

    def pool_acquire(self, bank):
        m = self.pool_mark[bank]
        if m is not None:
            self.wait("t", m[0], m[1])

    def pool_release(self, bank, sem, val):
        self.pool_mark[bank] = (sem, val)

    def _run_carry(self, n):
        while n > 0 and self.carry_q:
            self.carry_q.pop(0)()
            n -= 1

    def emit_ln_chunk(self, tb, slot, src_ap, dst_ap, waits_v=(), waits_s=(),
                      apply_eng="v"):
        p = self
        st = self.stat[tb]
        base = slot * 4
        cMV, cLNV, cRS = base, base + 2, base + 3
        bst = self.bst
        war = self.ln_mark[tb].get(slot, 0)
        for sem, val in waits_v:
            p.wait("v", sem, val)
        if war:
            p.wait("v", p.s_sc, war)
        if self.bst_war:
            p.wait("v", p.s_ve, self.bst_war)
        p.v(lambda e, a=src_ap: e.bn_stats(bst[:, 0:6], a[:, 0:512]))
        p.v(lambda e, a=src_ap: e.bn_stats(bst[:, 6:12], a[:, 512:1024]))
        bsv = p.inc("v", p.s_ve, 1)
        p.wait("v", p.s_ve, bsv)
        p.v(lambda e: e.bn_aggr(st[:, cMV:cMV + 2], bst[:, 0:12]))
        agv = p.inc("v", p.s_ve, 1)
        self.bst_war = agv
        for sem, val in waits_s:
            p.wait("s", sem, val)
        p.wait("s", p.s_ve, agv)
        p.s(lambda e: e.activation(out=st[:, cLNV:cLNV + 1], in_=st[:, cMV + 1:cMV + 2],
                                   func=AF.Ln, bias=EPS_LN))
        lnv = p.inc("s", p.s_sc, 1)
        p.wait("s", p.s_sc, lnv)
        p.s(lambda e: e.activation(out=st[:, cRS:cRS + 1], in_=st[:, cLNV:cLNV + 1],
                                   func=AF.Exp, scale=-0.5))
        rstd = p.inc("s", p.s_sc, 1)
        cM = cMV
        self.ln_mark[tb][slot] = rstd
        sem = {"v": p.s_ve, "g": p.s_gp}[apply_eng]
        p.wait(apply_eng, p.s_sc, rstd)
        for w in waits_v:
            if apply_eng != "v":
                p.wait(apply_eng, *w)
        p.q[apply_eng].append(lambda e, a=src_ap, o=dst_ap: e.tensor_scalar(
            out=o, in0=a, scalar1=st[:, cM:cM + 1], scalar2=st[:, cRS:cRS + 1],
            op0=ALU.subtract, op1=ALU.mult))
        p._can_inc[apply_eng] = True
        return (sem, p.inc(apply_eng, sem, 1))

    def emit_transposes(self, src_t, src_off, dst_fm, dst_base, dst_stride,
                        extra_t_waits=()):
        p = self
        last_sc = last_pe = None
        d3 = dst_fm[:, :].rearrange("p (o k) -> p o k", k=dst_stride)
        b0 = dst_base // dst_stride
        co = dst_base % dst_stride
        for g2 in range(2):
            b = p.pool_next()
            p.pool_acquire(b)
            for sem, val in extra_t_waits:
                p.wait("t", sem, val)
            for j in range(4):
                oc = g2 * 4 + j
                p.t(lambda e, b=b, j=j, off=src_off + oc * 128, s=src_t:
                    e.transpose(p.pool[b][:, j * 128:(j + 1) * 128],
                                s[:, off:off + 128], p.identf[:, :]))
            last_pe = p.inc("t", p.s_pe, 1)
            p.wait("s", p.s_pe, last_pe)
            oap = d3[:, b0 + g2 * 4: b0 + (g2 + 1) * 4, co:co + 128]
            iap = p.pool[b][:, :].rearrange("p (o k) -> p o k", k=128)
            p.s(lambda e, oap=oap, iap=iap: e.copy(oap, iap))
            last_sc = p.inc("s", p.s_sc, 1)
            p.pool_release(b, p.s_sc, last_sc)
        return last_sc, last_pe

    def emit_transposes_bf16(self, src_t, src_off, dst_fm, extra_t_waits=(),
                             extra_s_waits=(), dst_base=0, dst_stride=128,
                             bank=0):
        p = self
        pb = p.pbf if bank == 0 else p.pbf2
        mark = self.pbf_mark if bank == 0 else self.pbf2_mark
        if mark is not None:
            p.wait("t", *mark)
        for sem, val in extra_t_waits:
            p.wait("t", sem, val)
        for oc in range(OC):
            p.t(lambda e, off=src_off + oc * 128, oc=oc, s=src_t, pb=pb:
                e.transpose(pb[:, oc * 128:(oc + 1) * 128],
                            s[:, off:off + 128], p.identb[:, :]))
        pe = p.inc("t", p.s_pe, 1)
        p.wait("s", p.s_pe, pe)
        for sem, val in extra_s_waits:
            p.wait("s", sem, val)
        if dst_stride == 128:
            oap = dst_fm[:, dst_base:dst_base + OC * 128]
            iap = pb[:, :]
        else:
            oap = dst_fm[:, :].rearrange("p (o k) -> p o k", k=dst_stride)[
                :, 0:OC, dst_base:dst_base + 128]
            iap = pb[:, :].rearrange("p (o k) -> p o k", k=128)
        p.s(lambda e, oap=oap, iap=iap: e.copy(oap, iap))
        sc = p.inc("s", p.s_sc, 1)
        if bank == 0:
            self.pbf_mark = (p.s_sc, sc)
        else:
            self.pbf2_mark = (p.s_sc, sc)
        return sc, pe

    def sched_loads(self):
        p = self

        def gload(dst, name, sem, rearr=False):
            src = self.dt[name].ap()
            if rearr:
                src = src.rearrange("(c p) o -> p c o", p=128)
                dstap = dst[:, :].rearrange("p (c o) -> p c o", o=D)
            else:
                dstap = dst[:, :]
            p.g(lambda e, d=dstap, s=src: e.dma_start(out=d, in_=s))
            return (sem, p.inc("g", sem, 16))

        for dst, nm in ((self.identf, "identf"), (self.projz, "projz"),
                        (self.projzq, "projzq"),
                        (self.ediag, "ediag"), (self.onesb, "ones"),
                        (self.bqkv, "bqkv"), (self.bv_rep, "bv_rep"),
                        (self.identb, "identb"), (self.cvalsb, "cvals")):
            gload(dst, nm, p.s_cst)
        p.g(lambda e: e.dma_start(out=self.onesf[0:1, :], in_=self.dt["ones"].ap()[0:1, :]))
        p.inc("g", p.s_cst, 16)
        self.consts_done = (p.s_cst, p.s_cst.n)
        self.wk_done = gload(self.w_k, "wkT", p.s_wk, True)
        self.wv_done = gload(self.w_v, "wvT", p.s_wv, True)
        gload(self.vr_sb, "vrT", p.s_vrc)
        self.vrc_done = gload(self.c_sb, "cT", p.s_vrc)
        self.bo_done = gload(self.bo_rep, "bo_rep", p.s_bo)
        self.brep_done = gload(self.b_rep, "b_rep", p.s_br)

    def load_x_tile(self, s, t):
        p = self
        key = (s, t)
        if key in p.xpref:
            return p.xpref.pop(key)
        tb = t % 2
        if p.x_rel[tb] is not None:
            for sem, val in p.x_rel[tb]:
                p.wait("y", sem, val)
        src = p.dt["x"].ap()[s, t * TIL:(t + 1) * TIL, :].rearrange("(c p) f -> p c f", p=128)
        dst = p.x_tm[tb][:, :].rearrange("p (c f) -> p c f", f=D)
        p.y(lambda e, d=dst, s2=src: e.dma_start(out=d, in_=s2))
        mark = (p.s_x[tb], p.inc("y", p.s_x[tb], 16))
        return mark

    def prefetch_x(self, s, t):
        if (s, t) not in self.xpref:
            self.xpref[(s, t)] = self.load_x_tile(s, t)

    def prologue_front(self, s, t, first=False):
        p = self
        tb = t % 2
        x_done = self.load_x_tile(s, t)
        xc = p.x_tm[tb][:, :].rearrange("p (c f) -> p c f", f=D)
        aps = []
        for c in range(NCH):
            waits_v = [x_done]
            if p.xlnT_mark[c] > 0:
                waits_v.append((p.s_pe, p.xlnT_mark[c]))
            aps.append(self.emit_ln_chunk(tb, c, xc[:, c, :],
                                          p.xlnb[:, c * D:(c + 1) * D],
                                          waits_v=waits_v, waits_s=[x_done]))
        p.x_rel[tb] = [(p.s_ve, p.s_ve.n)]
        return aps

    def prologue_back(self, aps, first=False):
        p = self
        for c in range(NCH):
            tw = [aps[c]]
            if first and c == 0:
                tw.append(self.consts_done)
            if self.xfm_war > 0:
                tw.append((p.s_pe, self.xfm_war))
            _, tpe = self.emit_transposes_bf16(p.xlnb, c * D, p.xln_fm0,
                                               extra_t_waits=tw,
                                               dst_base=c * 128,
                                               dst_stride=TIL, bank=1)
            p.xlnT_mark[c] = tpe
        return p.s_sc.n

    def prologue_tile(self, s, t, first=False):
        return self.prologue_back(self.prologue_front(s, t, first), first)

    def phase_a_tile(self, s, t, xfm_done, nxt):
        p = self
        tb = t % 2
        first = (s == 0 and t == 0)
        if self.stop <= 1:
            return None
        diag_mm = {}
        k2dv = {}
        for oc in range(OC):
            b = p.pool_next()
            p.pool_acquire(b)
            p.wait("t", p.s_sc, xfm_done)
            if first and oc == 0:
                p.wait("t", *self.wk_done)
            for ic in range(OC):
                p.t(lambda e, b=b, ic=ic, oc=oc: e.matmul(
                    p.pool[b][:, :], p.w_k[:, ic * D + oc * 128: ic * D + (oc + 1) * 128],
                    p.xln_fm0[:, ic * TIL:(ic + 1) * TIL],
                    start=(ic == 0), stop=(ic == OC - 1)))
            rv = p.inc("t", p.s_pe, 1)
            p.wait("s", p.s_pe, rv)
            if oc == 0 and self.prj_war > 0:
                p.wait("s", p.s_pe, self.prj_war)
            p.s(lambda e, b=b, oc=oc: e.activation(
                out=p.prj_fm[:, oc * TIL:(oc + 1) * TIL], in_=p.pool[b][:, :],
                func=AF.Identity, bias=p.bqkv[:, OC + oc:OC + oc + 1]))
            p.inc("s", p.s_sc, 1)
            if oc >= 2:
                p.wait("s", p.s_pe, diag_mm[oc - 2])
            elif self.k2_war > 0:
                p.wait("s", p.s_pe, self.k2_war)
            p.s(lambda e, b=b, oc=oc: e.activation(
                out=p.k2b[oc % 2][:, :], in_=p.pool[b][:, :],
                func=AF.Square, bias=p.bqkv[:, OC + oc:OC + oc + 1]))
            k2d = p.inc("s", p.s_sc, 1)
            p.pool_release(b, p.s_sc, k2d)
            k2dv[oc] = k2d

            def emit_diag(oc2):
                p.wait("t", p.s_sc, k2dv[oc2])
                if oc2 == 0:
                    if self.pdiag_war > 0:
                        p.wait("t", p.s_ve, self.pdiag_war)
                    if self.edr_war > 0:
                        p.wait("t", p.s_sc, self.edr_war)
                p.t(lambda e, oc2=oc2: e.matmul(
                    p.gen[2][0:16, :], p.ediag[:, oc2 * 16:(oc2 + 1) * 16],
                    p.k2b[oc2 % 2][:, :],
                    start=(oc2 == 0), stop=(oc2 == OC - 1)))
                diag_mm[oc2] = p.inc("t", p.s_pe, 1)

            if oc >= 1:
                emit_diag(oc - 1)
            if oc == OC - 1:
                emit_diag(oc)
        self.k2_war = diag_mm[6]
        kevac_done = p.s_sc.n
        if self.stop <= 2:
            return None
        for c in range(NCH):
            for half in range(2):
                b = p.pool_next()
                p.pool_acquire(b)
                if first and c == 0 and half == 0:
                    p.wait("t", *self.wv_done)
                for ic in range(OC):
                    p.t(lambda e, b=b, ic=ic, c=c, half=half: e.matmul(
                        p.pool[b][:, :],
                        p.xln_fm0[:, ic * TIL + c * 128: ic * TIL + (c + 1) * 128],
                        p.w_v[:, ic * D + half * 512: ic * D + (half + 1) * 512],
                        start=(ic == 0), stop=(ic == OC - 1)))
                rv = p.inc("t", p.s_pe, 1)
                p.wait("v", p.s_pe, rv)
                if c == 0 and half == 0:
                    if first:
                        p.wait("v", *self.consts_done)
                    if self.vx_war > 0:
                        p.wait("v", p.s_pe, self.vx_war)
                iap = p.pool[b][:, :].rearrange("p (h j) -> p h j", j=64)
                p.v(lambda e, iap=iap, c=c, half=half: e.tensor_tensor(
                    out=p.vx3[:, c, half * 8:(half + 1) * 8, 0:64], in0=iap,
                    in1=p.bv3[:, half * 8:(half + 1) * 8, :], op=ALU.add))
                vv = p.inc("v", p.s_ve, 1)
                p.pool_release(b, p.s_ve, vv)
        self.xfm_war = p.s_pe.n
        p.v(lambda e: e.memset(p.vx3[:, :, :, 64:65], 1.0))
        vx_done = p.inc("v", p.s_ve, 1)
        if self.stop <= 3:
            return None
        p.wait("s", p.s_pe, diag_mm[OC - 1])
        p.s(lambda e: e.activation(out=p.ed_rows[:, :], in_=p.gen[2][0:16, :], func=AF.Exp))
        edr_done = p.inc("s", p.s_sc, 1)
        self.edr_war = edr_done
        p.wait("v", p.s_pe, diag_mm[OC - 1])
        p.v(lambda e, t=t: e.reduce_max(p.dmaxb[0:16, t:t + 1], p.gen[2][0:16, :],
                                        axis=AXL.X))
        self.pdiag_war = p.inc("v", p.s_ve, 1)
        b = p.pool_next()
        p.pool_acquire(b)
        p.wait("t", p.s_sc, edr_done)
        for c in range(NCH):
            p.t(lambda e, b=b, c=c: e.transpose(
                p.pool[b][:, c * 16:(c + 1) * 16],
                p.ed_rows[0:16, c * 128:(c + 1) * 128], p.identf[0:16, 0:16]))
        rv = p.inc("t", p.s_pe, 1)
        p.wait("v", p.s_pe, rv)
        p.v(lambda e, b=b: e.tensor_copy(p.ed_tm[:, :], p.pool[b][:, 0:NCH * 16]))
        edt = p.inc("v", p.s_ve, 1)
        p.pool_release(b, p.s_ve, edt)
        for half in range(2):
            b = p.pool_next()
            p.pool_acquire(b)
            p.wait("t", p.s_ve, vx_done)
            for c in range(NCH):
                p.t(lambda e, b=b, c=c, half=half: e.matmul(
                    p.pool[b][0:1, :], p.onesb[:, 0:1],
                    p.vx3[:, c, half * 8:(half + 1) * 8, 0:64],
                    start=(c == 0), stop=(c == NCH - 1)))
            rv = p.inc("t", p.s_pe, 1)
            p.wait("v", p.s_pe, rv)
            if self.vs_war[half] > 0:
                p.wait("v", p.s_ve, self.vs_war[half])
            if t == 0:
                p.v(lambda e, b=b, half=half, s=s: e.tensor_copy(
                    p.vs_row[s][:, half * 512:(half + 1) * 512], p.pool[b][0:1, :]))
            else:
                p.v(lambda e, b=b, half=half, s=s: e.tensor_tensor(
                    out=p.vs_row[s][:, half * 512:(half + 1) * 512],
                    in0=p.vs_row[s][:, half * 512:(half + 1) * 512],
                    in1=p.pool[b][0:1, :], op=ALU.add))
            self.vs_war[half] = p.inc("v", p.s_ve, 1)
            p.pool_release(b, p.s_ve, self.vs_war[half])
        vs_mm_done = p.s_pe.n
        if self.stop <= 4:
            return None
        p.wait("v", p.s_pe, vs_mm_done)
        p.wait("v", p.s_ve, edt)
        mul_prev = 0
        for c in range(NCH):
            er = p.edrep0
            src = p.ed_tm[:, c * 16:(c + 1) * 16].broadcast_to([128, 16, 65])
            if mul_prev > 0:
                p.wait("v", p.s_ve, mul_prev)
            p.v(lambda e, er=er, src=src: e.tensor_copy(
                er[:, :].rearrange("p (h j) -> p h j", j=65), src))
            erv = p.inc("v", p.s_ve, 1)
            p.wait("v", p.s_ve, erv)
            p.v(lambda e, er=er, c=c: e.tensor_tensor(
                out=p.vx3[:, c, :, :], in0=p.vx3[:, c, :, :],
                in1=er[:, :].rearrange("p (h j) -> p h j", j=65), op=ALU.mult))
            mul_prev = p.inc("v", p.s_ve, 1)
        vp_done = p.s_ve.n
        if self.stop <= 5:
            return None
        nxt_mark = None
        if nxt is not None:
            nxt_mark = self.prologue_tile(nxt[0], nxt[1])
        c1grp = dict(self.last_c1pair)
        exp_done = {}

        def emit_c1(pr):
            mb = pr % 2
            p.wait("t", p.s_sc, exp_done[pr])
            if pr == 0:
                p.wait("t", p.s_ve, vp_done)
            if self.c1mini_free[mb] > 0:
                p.wait("t", p.s_ve, self.c1mini_free[mb])
            for hh in range(2):
                for mc in range(2):
                    h = 2 * pr + hh
                    soff = (hh * 2 + mc) * 65
                    for c in range(NCH):
                        p.t(lambda e, mb=mb, soff=soff, c=c, hh=hh, h=h, pr=pr:
                            e.matmul(p.gen[mb][:, soff:soff + 65],
                                     p.e1b[pr % 2][c][:, hh * 256 + (soff // 65 % 2) * 128: hh * 256 + (soff // 65 % 2) * 128 + 128],
                                     p.vx3[:, c, h, :],
                                     start=(c == 0), stop=(c == NCH - 1)))
            c1grp[pr] = p.inc("t", p.s_pe, 1)
            p.wait("v", p.s_pe, c1grp[pr])
            if self.c1aw[pr] > 0:
                p.wait("v", p.s_ve, self.c1aw[pr])
            if t == 0:
                p.v(lambda e, mb=mb, pr=pr: e.tensor_copy(
                    p.c1a[:, pr * 260:(pr + 1) * 260], p.gen[mb][:, 0:260]))
            else:
                p.v(lambda e, mb=mb, pr=pr: e.tensor_tensor(
                    out=p.c1a[:, pr * 260:(pr + 1) * 260],
                    in0=p.c1a[:, pr * 260:(pr + 1) * 260],
                    in1=p.gen[mb][:, 0:260], op=ALU.add))
            self.c1aw[pr] = p.inc("v", p.s_ve, 1)
            self.c1mini_free[mb] = self.c1aw[pr]

        for pr in range(8):
            for c in range(NCH):
                b = p.pool_next()
                p.pool_acquire(b)
                col = pr * TIL + c * 128
                p.wait("t", p.s_sc, kevac_done)
                p.t(lambda e, b=b, col=col: e.matmul(
                    p.pool[b][:, :], p.prj_fm[:, col:col + 128], p.projz[:, :]))
                rv = p.inc("t", p.s_pe, 1)
                eb = p.e1b[pr % 2][c]
                p.wait("s", p.s_pe, rv)
                war = c1grp.get(pr - 2)
                if war:
                    p.wait("s", p.s_pe, war)
                p.s(lambda e, b=b, eb=eb: e.activation(
                    out=eb[:, :], in_=p.pool[b][:, :], func=AF.Exp,
                    scale=NORMZ, bias=-LN16))
                ev = p.inc("s", p.s_sc, 1)
                p.pool_release(b, p.s_sc, ev)
            exp_done[pr] = p.s_sc.n
            if pr >= 1:
                emit_c1(pr - 1)
        emit_c1(7)
        self.vx_war = c1grp[7]
        self.last_c1pair = {-2: c1grp[6], -1: c1grp[7]}
        self.prj_war = c1grp[7]
        self.last_c1evac = max(self.c1aw)
        return nxt_mark

    def phase_a_seq(self, s, xfm_first=None):
        p = self
        xfm = xfm_first
        for t in range(NT):
            if xfm is None:
                xfm = self.prologue_tile(s, t, first=(s == 0 and t == 0))
            if t + 1 < NT:
                nxt = (s, t + 1, (s, t + 2) if t + 2 < NT else
                       ((s + 1, 0) if s + 1 < BPC else None))
            elif s + 1 < BPC:
                nxt = (s + 1, 0, (s + 1, 1))
            else:
                nxt = None
            xfm = self.phase_a_tile(s, t, xfm, nxt)
        p.wait("v", p.s_ve, self.last_c1evac)
        for mc in range(2):
            iap = p.c1a[:, :].rearrange("p (h j) -> p h j", j=130)[:, :, mc * 65:(mc + 1) * 65]
            oap = p.c1e[s][mc][:, :].rearrange("p (h j) -> p h j", j=66)[:, :, 0:65]
            p.v(lambda e, iap=iap, oap=oap: e.tensor_copy(oap, iap))
            p.inc("v", p.s_ve, 1)
            ap1 = p.c1e[s][mc][:, :].rearrange("p (h j) -> p h j", j=66)[:, :, 65:66]
            p.v(lambda e, ap1=ap1: e.memset(ap1, 1.0))
            p.inc("v", p.s_ve, 1)
        c1e_done = p.s_ve.n
        self.c1a_free = c1e_done
        self.c1aw = [0] * 8
        b = p.pool_next()
        p.pool_acquire(b)
        p.wait("t", p.s_ve, c1e_done)
        for oc in range(OC):
            p.t(lambda e, b=b, oc=oc, s=s: e.transpose(
                p.pool[b][:, oc:oc + 1], p.vs_row[s][0:1, oc * 128:(oc + 1) * 128],
                p.identf[0:1, 0:1]))
        rv = p.inc("t", p.s_pe, 1)
        p.wait("v", p.s_pe, rv)
        p.v(lambda e, b=b, s=s: e.tensor_copy(p.vst_fm[s][:, :], p.pool[b][:, 0:OC]))
        vst = p.inc("v", p.s_ve, 1)
        p.pool_release(b, p.s_ve, vst)
        self.vs_war = [0, 0]
        self.vst_done = vst
        p.v(lambda e: e.reduce_max(p.dmax1[0:16, 0:1], p.dmaxb[0:16, 0:NT], axis=AXL.X))
        dm = p.inc("v", p.s_ve, 1)
        b = p.pool_next()
        p.pool_acquire(b)
        p.wait("t", p.s_ve, dm)
        p.t(lambda e, b=b: e.transpose(p.pool[b][0:1, 0:16], p.dmax1[0:16, 0:1],
                                       p.identf[0:16, 0:16]))
        rv = p.inc("t", p.s_pe, 1)
        p.wait("v", p.s_pe, rv)
        p.v(lambda e, b=b, s=s: e.reduce_max(p.stabsc[0:1, s:s + 1],
                                             p.pool[b][0:1, 0:16], axis=AXL.X))
        sv = p.inc("v", p.s_ve, 1)
        p.pool_release(b, p.s_ve, sv)
        return xfm

    def sched_collective(self):
        p = self
        p.wait("v", p.s_ve, p.s_ve.n)
        p.v(lambda e: e.reduce_max(p.stabv[0:1, 0:1], p.stabsc[0:1, 0:2], axis=AXL.X))
        smx = p.inc("v", p.s_ve, 1)
        p.wait("v", p.s_ve, smx)
        src = p.stabv[0:1, 0:1].broadcast_to([1, 1, 15])
        dst = p.stabv[0:1, 1:16].rearrange("p (a j) -> p a j", a=1)
        p.v(lambda e, src=src, dst=dst: e.tensor_copy(dst, src))
        st_done = p.inc("v", p.s_ve, 1)
        p.wait("g", p.s_ve, st_done)
        p.g(lambda e: e.dma_start(out=p.ccin.ap(), in_=p.stabv[0:1, :]))
        p.inc("g", p.s_cc, 16)
        p.wait("g", p.s_cc, 16)
        p.g(lambda e: e.collective_compute(
            "AllReduce", ALU.max, replica_groups=[list(range(NCORES))],
            ins=[p.ccin.ap().opt()], outs=[p.ccout.ap().opt()]))
        p.inc("g", p.s_ccr, 1)
        p.wait("g", p.s_ccr, 1)
        p.g(lambda e: e.dma_start(out=p.stab_in[0:1, :], in_=p.ccout.ap()))
        p.inc("g", p.s_cc, 16)
        p.wait("g", p.s_cc, 32)
        b = p.pool_next()
        p.pool_acquire(b)
        p.wait("t", p.s_cc, 32)
        p.t(lambda e, b=b: e.matmul(p.pool[b][:, 0:16], p.onesf[0:1, :],
                                    p.stab_in[0:1, :]))
        rv = p.inc("t", p.s_pe, 1)
        p.wait("v", p.s_pe, rv)
        p.v(lambda e, b=b: e.tensor_copy(p.stab_rep[:, :], p.pool[b][:, 0:16]))
        bc = p.inc("v", p.s_ve, 1)
        p.pool_release(b, p.s_ve, bc)
        p.wait("s", p.s_ve, bc)
        p.s(lambda e: e.activation(out=p.epsp[:, :], in_=p.stab_rep[:, 0:1],
                                   func=AF.Exp, bias=LNEPSQ))
        ep = p.inc("s", p.s_sc, 1)
        p.wait("v", p.s_sc, ep)
        p.v(lambda e: e.tensor_scalar_mul(p.nep[:, :], p.epsp[:, :], float(N)))
        p.inc("v", p.s_ve, 1)
        p.v(lambda e: e.tensor_scalar_mul(p.epnk[:, :], p.epsp[:, :],
                                          float(256.0 * EPSQ * N)))
        self.eps_done = p.inc("v", p.s_ve, 1)

    def sched_wprime(self):
        p = self
        wst_free = {}
        for ic in range(OC):
            src = p.dt["wT"].ap()[ic * 128:(ic + 1) * 128, :]
            if ic >= 2:
                p.wait("y", p.s_ve, wst_free[ic - 2])
            p.y(lambda e, s2=src, ic=ic: e.dma_start(out=p.wtstage[ic % 2][:, :], in_=s2))
            wst = (p.s_wt[ic % 2], p.inc("y", p.s_wt[ic % 2], 16))
            for half in range(2):
                b = p.pool_next()
                p.pool_acquire(b)
                if ic == 0 and half == 0:
                    p.wait("t", *self.vrc_done)
                p.t(lambda e, b=b, ic=ic, half=half: e.matmul(
                    p.pool[b][:, :], p.vr_sb[0:32, ic * 128:(ic + 1) * 128],
                    p.c_sb[0:32, half * 512:(half + 1) * 512]))
                rv = p.inc("t", p.s_pe, 1)
                p.wait("v", p.s_pe, rv)
                p.wait("v", *wst)
                p.v(lambda e, b=b, ic=ic, half=half: e.tensor_tensor(
                    out=p.w_p[:, ic * D + half * 512: ic * D + (half + 1) * 512],
                    in0=p.pool[b][:, :],
                    in1=p.wtstage[ic % 2][:, half * 512:(half + 1) * 512],
                    op=ALU.add))
                wst_free[ic] = p.inc("v", p.s_ve, 1)
                p.pool_release(b, p.s_ve, wst_free[ic])
        self.wp_done = p.s_ve.n

    def sched_bsetup1(self, s):
        p = self
        pse = p.gen[2]
        if s == 0:
            if self.pdiag_war > 0:
                p.wait("t", p.s_ve, self.pdiag_war)
            if self.edr_war > 0:
                p.wait("t", p.s_sc, self.edr_war)
        csb = self.cs_sb
        for grp in range(2):
            for hh in range(8):
                h = grp * 8 + hh
                for mc in range(2):
                    p.t(lambda e, h=h, hh=hh, mc=mc, s=s: e.matmul(
                        pse[0:1, hh * 64:(hh + 1) * 64], p.onesb[:, 0:1],
                        p.c1e[s][mc][:, h * 66:h * 66 + 64],
                        start=(mc == 0), stop=(mc == 1)))
            gv = p.inc("t", p.s_pe, 1)
            p.wait("v", p.s_pe, gv)
            if grp == 1:
                p.wait("v", p.s_ve, self.csb_w)
            p.v(lambda e, grp=grp: e.tensor_copy(csb[0:1, grp * 512:(grp + 1) * 512],
                                                 pse[0:1, 0:512]))
            self.csb_w = p.inc("v", p.s_ve, 1)
            if grp == 0:
                p.wait("t", p.s_ve, self.csb_w)
        p.wait("t", p.s_ve, self.csb_w)
        if self.pbf_mark is not None:
            p.wait("t", *self.pbf_mark)
        for h in range(H):
            p.t(lambda e, h=h: e.transpose(
                p.pbf[(h % 2) * 64:(h % 2) * 64 + 64, 2 * (h // 2):2 * (h // 2) + 1],
                csb[0:1, h * 64:(h + 1) * 64], p.identb[0:1, 0:1]))
        cst = p.inc("t", p.s_pe, 1)
        p.wait("v", p.s_pe, cst)
        csrc = p.pbf[:, 0:2 * OC].rearrange("p (c k) -> p c k", k=2)[:, :, 0]
        p.v(lambda e, s=s, csrc=csrc: e.tensor_copy(p.csfm2[s][:, :], csrc))
        csf = p.inc("v", p.s_ve, 1)
        self.pbf_mark = (p.s_ve, csf)
        self.cs_marks[s] = csf
        for mc in range(2):
            gsap = p.c1e[s][mc][:, :].rearrange("p (h j) -> p h j", j=66)[:, :, 64]
            p.t(lambda e, gsap=gsap, mc=mc, s=s: e.matmul(
                pse[:, 16 + s * 16:32 + s * 16], p.onesb[:, :], gsap,
                start=(mc == 0), stop=(mc == 1)))
        self.k2s_marks[s] = p.inc("t", p.s_pe, 1)

    def sched_bsetup2(self, s):
        p = self
        pse = p.gen[2]
        p.wait("v", p.s_ve, self.eps_done)
        p.v(lambda e, s=s: e.tensor_scalar_mul(p.evst[:, :], p.vst_fm[s][:, :],
                                               p.epsp[:, 0:1]))
        p.inc("v", p.s_ve, 1)
        p.v(lambda e, s=s: e.tensor_scalar(
            out=p.evs256[:, :], in0=p.vst_fm[s][:, :], scalar1=p.epsp[:, 0:1],
            scalar2=float(256.0 * EPSQ), op0=ALU.mult, op1=ALU.mult))
        ev2 = p.inc("v", p.s_ve, 1)
        p.wait("v", p.s_ve, ev2)
        p.v(lambda e, s=s: e.scalar_tensor_tensor(
            out=p.k1_fm[:, :], in0=p.csfm2[s][:, :], scalar=EPSQ, in1=p.evs256[:, :],
            op0=ALU.mult, op1=ALU.add))
        k1f = p.inc("v", p.s_ve, 1)
        p.wait("v", p.s_pe, self.k2s_marks[s])
        p.v(lambda e, s=s: e.tensor_scalar(
            out=p.k2rep[s][:, :], in0=pse[:, 16 + s * 16:32 + s * 16], scalar1=EPSQ,
            scalar2=p.epnk[:, 0:1], op0=ALU.mult, op1=ALU.add))
        self.psetup_read = p.inc("v", p.s_ve, 1)
        p.v(lambda e, s=s: e.memset(p.usb[s][:, :], 0.0))
        um = p.inc("v", p.s_ve, 1)
        p.wait("v", p.s_ve, um)
        for c in range(OC):
            for colofs, srcT in ((2 * c, p.evst), (16 + 2 * c, p.k1_fm)):
                p.v(lambda e, c=c, colofs=colofs, srcT=srcT, s=s: e.tensor_copy(
                    p.usb[s][0:64, c * 32 + colofs:c * 32 + colofs + 1],
                    srcT[0:64, c:c + 1]))
                p.inc("v", p.s_ve, 1)
                p.v(lambda e, c=c, colofs=colofs, srcT=srcT, s=s: e.tensor_copy(
                    p.usb[s][64:128, c * 32 + colofs + 1:c * 32 + colofs + 2],
                    srcT[64:128, c:c + 1]))
                p.inc("v", p.s_ve, 1)
        u_done = p.s_ve.n
        for half in range(2):
            b = p.pool_next()
            p.pool_acquire(b)
            p.wait("t", p.s_ve, u_done)
            if s == 0 and half == 0:
                p.wait("t", *self.wo_done)
            for c in range(OC):
                p.t(lambda e, b=b, c=c, half=half, s=s: e.matmul(
                    p.pool[b][0:32, :], p.usb[s][:, c * 32:(c + 1) * 32],
                    p.w_o[:, c * D + half * 512: c * D + (half + 1) * 512],
                    start=(c == 0), stop=(c == OC - 1)))
            rv = p.inc("t", p.s_pe, 1)
            p.wait("v", p.s_pe, rv)
            p.v(lambda e, b=b, half=half, s=s: e.tensor_copy(
                p.v12t[s][0:32, half * 512:(half + 1) * 512], p.pool[b][0:32, :]))
            vv = p.inc("v", p.s_ve, 1)
            p.pool_release(b, p.s_ve, vv)
        self.bset_done = p.s_ve.n

    def _emit_head_tail(self, h, recv):
        p = self
        p.wait("v", p.s_ve, recv)
        qodd = p.qsden[:, h * 8:(h + 1) * 8].rearrange("p (n j) -> p n j", j=2)[:, :, 1]
        p.v(lambda e, h=h, qodd=qodd: e.tensor_tensor(
            out=p.qd3[:, :, h], in0=qodd, in1=p.qd3[:, :, 16 + h], op=ALU.mult))
        p.inc("v", p.s_ve, 1)
        npb = p.gen[h % 3]
        for ns in range(NCH):
            p.v(lambda e, h=h, ns=ns, npb=npb: e.tensor_scalar_mul(
                p.attn_tm[:, ns * D + h * 64: ns * D + (h + 1) * 64],
                npb[:, ns * 66:ns * 66 + 64],
                p.qd3t[:, ns * 32 + 16 + h: ns * 32 + 16 + h + 1]))
            p.inc("v", p.s_ve, 1)

    def pro_b_back(self, aps, firstb=False):
        p = self
        xfm_done = self.prologue_back(aps, firstb)
        for oc in range(OC):
            b = p.pool_next()
            p.pool_acquire(b)
            p.wait("t", p.s_sc, xfm_done)
            if firstb and oc == 0:
                p.wait("t", *self.wq_done)
            for ic in range(OC):
                p.t(lambda e, b=b, ic=ic, oc=oc: e.matmul(
                    p.pool[b][:, :], p.w_q[:, ic * D + oc * 128: ic * D + (oc + 1) * 128],
                    p.xln_fm0[:, ic * TIL:(ic + 1) * TIL],
                    start=(ic == 0), stop=(ic == OC - 1)))
            rv = p.inc("t", p.s_pe, 1)
            p.wait("s", p.s_pe, rv)
            if oc == 0 and self.prj_war > 0:
                p.wait("s", p.s_pe, self.prj_war)
            p.s(lambda e, b=b, oc=oc: e.activation(
                out=p.prj_fm[:, oc * TIL:(oc + 1) * TIL], in_=p.pool[b][:, :],
                func=AF.Identity, bias=p.bqkv[:, oc:oc + 1]))
            qe = p.inc("s", p.s_sc, 1)
            p.pool_release(b, p.s_sc, qe)
        self.xfm_war = p.s_pe.n
        return p.s_sc.n

    def pro_b(self, s, t, firstb=False):
        return self.pro_b_back(self.prologue_front(s, t, firstb), firstb)

    def phase_b_tile(self, s, t, qevac_done, nxt):
        p = self
        tb = t % 2
        firstb = (s == 0 and t == 0)
        xc = p.x_tm[tb][:, :].rearrange("p (c f) -> p c f", f=D)
        h_done = {}
        np_free = {}
        rec_val = {}
        prev = None
        p.wait("v", p.s_sc, qevac_done)
        if self.attn_war > 0:
            p.wait("v", p.s_pe, self.attn_war)
        exp_done = {}

        def emit_dash(h):
            hh = h % 2
            pr = h // 2
            r = h % 4
            gq3 = p.gq[r][:, :].rearrange("p (m n) -> p m n", n=TIL)
            for nh in range(2):
                b = p.pool_next()
                p.pool_acquire(b)
                p.wait("t", p.s_sc, qevac_done)
                for mc in range(2):
                    p.t(lambda e, b=b, mc=mc, hh=hh, pr=pr, nh=nh: e.matmul(
                        p.pool[b][:, mc * 256:(mc + 1) * 256],
                        p.projzq[:, hh * 256 + mc * 128: hh * 256 + (mc + 1) * 128],
                        p.prj_fm[:, pr * TIL + nh * 256: pr * TIL + (nh + 1) * 256],
                        start=True, stop=True))
                rv = p.inc("t", p.s_pe, 1)
                p.wait("s", p.s_pe, rv)
                if h >= 4 and nh == 0:
                    p.wait("s", p.s_pe, h_done[h - 4])
                elif firstb and h < 4 and nh == 0:
                    p.wait("s", p.s_ve, self.c1a_free)
                oap = gq3[:, :, nh * 256:(nh + 1) * 256]
                iap = p.pool[b][:, :].rearrange("p (m n) -> p m n", n=256)
                p.s(lambda e, oap=oap, iap=iap: e.activation(
                    out=oap, in_=iap, func=AF.Exp, scale=NORMZ, bias=-LN16))
                ev = p.inc("s", p.s_sc, 1)
                p.pool_release(b, p.s_sc, ev)
            exp_done[h] = p.s_sc.n

        emit_dash(0)
        for h in range(H):
            if h + 1 < H:
                emit_dash(h + 1)
            gq3 = p.gq[h % 4][:, :].rearrange("p (m n) -> p m n", n=TIL)
            npb = p.gen[h % 3]
            p.wait("t", p.s_sc, exp_done[h])
            if firstb and h == 0:
                p.wait("t", p.s_ve, self.bset_done)
            if h >= 3:
                p.wait("t", p.s_ve, np_free[h - 3])
            elif self.np_war[h % 3] > 0:
                p.wait("t", p.s_ve, self.np_war[h % 3])
            for ns in range(NCH):
                for mc in range(2):
                    p.t(lambda e, npb=npb, ns=ns, mc=mc, gq3=gq3, h=h, s=s: e.matmul(
                        npb[:, ns * 66:(ns + 1) * 66],
                        gq3[:, mc, ns * 128:(ns + 1) * 128],
                        p.c1e[s][mc][:, h * 66:(h + 1) * 66],
                        start=(mc == 0), stop=(mc == 1)))
            h_done[h] = p.inc("t", p.s_pe, 1)
            if True:
                p.wait("v", p.s_pe, h_done[h])
                srcA = npb[:, 0:264].rearrange("p (n j) -> p n j", j=66)[:, :, 64:66]
                dstA = p.qsden[:, h * 8:(h + 1) * 8].rearrange("p (n j) -> p n j", j=2)
                p.v(lambda e, srcA=srcA, dstA=dstA: e.tensor_copy(dstA, srcA))
                gat = p.inc("v", p.s_ve, 1)
                p.wait("s", p.s_ve, gat)
                qodd = p.qsden[:, h * 8:(h + 1) * 8].rearrange("p (n j) -> p n j", j=2)[:, :, 1]
                p.s(lambda e, h=h, qodd=qodd, s=s: e.activation(
                    out=p.tden[:, h * 4:(h + 1) * 4], in_=qodd, func=AF.Identity,
                    scale=p.nep[:, 0:1], bias=p.k2rep[s][:, h:h + 1]))
                td = p.inc("s", p.s_sc, 1)
                p.wait("v", p.s_sc, td)
                qev = p.qsden[:, h * 8:(h + 1) * 8].rearrange("p (n j) -> p n j", j=2)[:, :, 0]
                p.v(lambda e, h=h, qev=qev: e.tensor_tensor(
                    out=p.tden[:, h * 4:(h + 1) * 4], in0=p.tden[:, h * 4:(h + 1) * 4],
                    in1=qev, op=ALU.add))
                tav = p.inc("v", p.s_ve, 1)
                p.wait("v", p.s_ve, tav)
                p.v(lambda e, h=h: e.reciprocal(p.qd3[:, :, 16 + h],
                                                p.tden[:, h * 4:(h + 1) * 4]))
                rec_val[h] = p.inc("v", p.s_ve, 1)
                if prev is not None:
                    self._emit_head_tail(prev, rec_val[prev])
                    np_free[prev] = p.s_ve.n
                prev = h
            if h % 2 == 1 and h < 15:
                self._run_carry(1 if h < 13 else 2)
        self._emit_head_tail(prev, rec_val[prev])
        np_free[prev] = p.s_ve.n
        self.np_war = [np_free[15], np_free[13], np_free[14]]
        self.prj_war = h_done[15]
        comb_done = p.s_ve.n
        self._run_carry(99)
        aps_next = self.prologue_front(nxt[0], nxt[1]) if nxt is not None else None
        ap2s = {}
        bqd = p.pool_next()
        p.pool_acquire(bqd)
        p.wait("t", p.s_ve, comb_done)
        for ns in range(NCH):
            p.t(lambda e, b=bqd, ns=ns: e.transpose(
                p.pool[b][0:32, ns * 128:(ns + 1) * 128],
                p.qd3t[:, ns * 32:(ns + 1) * 32], p.identf[:, :]))
        rv = p.inc("t", p.s_pe, 1)
        p.wait("v", p.s_pe, rv)
        p.v(lambda e, b=bqd: e.tensor_copy(p.qd_fm[0:32, :], p.pool[b][0:32, :]))
        qdf = p.inc("v", p.s_ve, 1)
        p.pool_release(bqd, p.s_ve, qdf)

        def emit_attnT(ns):
            k = ns % 2
            esw = [(p.s_pe, self.afm_war[k])] if self.afm_war[k] else []
            self.emit_transposes_bf16(p.attn_tm, ns * D, p.attn_fm[k],
                                      extra_t_waits=[(p.s_ve, comb_done)],
                                      extra_s_waits=esw)
            return p.s_sc.n

        afm_done = {0: emit_attnT(0)}
        for ns in range(NCH):
            k = ns % 2
            af = p.attn_fm[k]
            li = p.lnin0
            lo = p.lnout0
            for half in range(2):
                b = p.pool_next()
                p.pool_acquire(b)
                p.wait("t", p.s_sc, afm_done[ns])
                if half == 0:
                    p.wait("t", p.s_ve, qdf)
                for ic in range(OC):
                    p.t(lambda e, b=b, ic=ic, half=half, af=af: e.matmul(
                        p.pool[b][:, :], af[:, ic * 128:(ic + 1) * 128],
                        p.w_o[:, ic * D + half * 512: ic * D + (half + 1) * 512],
                        start=(ic == 0), stop=False))
                p.t(lambda e, b=b, half=half, ns=ns, s=s: e.matmul(
                    p.pool[b][:, :], p.qd_fm[0:32, ns * 128:(ns + 1) * 128],
                    p.v12t[s][0:32, half * 512:(half + 1) * 512],
                    start=False, stop=True))
                rv = p.inc("t", p.s_pe, 1)
                p.wait("v", p.s_pe, rv)
                if half == 0 and self.lnin_war is not None:
                    p.wait("v", *self.lnin_war)
                if firstb and ns == 0 and half == 0:
                    p.wait("v", *self.bo_done)
                p.v(lambda e, b=b, half=half, li=li: e.tensor_tensor(
                    out=li[:, half * 512:(half + 1) * 512], in0=p.pool[b][:, :],
                    in1=p.bo_rep[:, half * 512:(half + 1) * 512], op=ALU.add))
                wv = p.inc("v", p.s_ve, 1)
                p.pool_release(b, p.s_ve, wv)
            wo_done = p.s_ve.n
            self.afm_war[k] = p.s_pe.n
            if ns + 1 < NCH:
                afm_done[ns + 1] = emit_attnT(ns + 1)
            ap1 = self.emit_ln_chunk(tb, 4 + ns, li[:, :], lo[:, :],
                                     waits_v=[(p.s_ve, wo_done),
                                              (p.s_gp, self.lnout_war)],
                                     waits_s=[(p.s_ve, wo_done)])
            self.lnin_war = ap1
            p.wait("v", *ap1)
            p.v(lambda e, ns=ns, lo=lo, tb=tb: e.tensor_tensor(
                out=p.x_tm[tb][:, ns * D:(ns + 1) * D],
                in0=p.x_tm[tb][:, ns * D:(ns + 1) * D], in1=lo[:, :], op=ALU.add))
            x1_done = p.inc("v", p.s_ve, 1)
            wv2 = [(p.s_ve, x1_done)]
            if self.xln2b_war[ns] is not None:
                wv2.append(self.xln2b_war[ns])
            ap2s[ns] = self.emit_ln_chunk(
                tb, 8 + ns, p.x_tm[tb][:, ns * D:(ns + 1) * D],
                p.xln2b[:, ns * D:(ns + 1) * D],
                waits_v=wv2, waits_s=[(p.s_ve, x1_done)])
        self.attn_war = p.s_pe.n
        pro_next = None
        if nxt is not None:
            pro_next = self.pro_b_back(aps_next)

        def mk_e2(ns, ap2):
            def run():
                esw = [(p.s_pe, self.x2f_war)] if self.x2f_war else []
                _, tpe = self.emit_transposes_bf16(p.xln2b, ns * D, p.xln2_fm0,
                                                   extra_t_waits=[ap2],
                                                   extra_s_waits=esw)
                self.x2f_done = p.s_sc.n
                self.xln2b_war[ns] = (p.s_pe, tpe)
            return run

        def mk_e3(ns, tb, firstb):
            def run():
                li = p.lnin0
                lo = p.lnout0
                x2f = p.xln2_fm0
                for half in range(2):
                    b = p.pool_next()
                    p.pool_acquire(b)
                    p.wait("t", p.s_sc, self.x2f_done)
                    if firstb and ns == 0 and half == 0:
                        p.wait("t", p.s_ve, self.wp_done)
                    for ic in range(OC):
                        p.t(lambda e, b=b, ic=ic, half=half, x2f=x2f: e.matmul(
                            p.pool[b][:, :], x2f[:, ic * 128:(ic + 1) * 128],
                            p.w_p[:, ic * D + half * 512: ic * D + (half + 1) * 512],
                            start=(ic == 0), stop=(ic == OC - 1)))
                    rv = p.inc("t", p.s_pe, 1)
                    p.wait("v", p.s_pe, rv)
                    if half == 0 and self.lnin_war is not None:
                        p.wait("v", *self.lnin_war)
                    if firstb and ns == 0 and half == 0:
                        p.wait("v", *self.brep_done)
                    p.v(lambda e, b=b, half=half, li=li: e.tensor_tensor(
                        out=li[:, half * 512:(half + 1) * 512], in0=p.pool[b][:, :],
                        in1=p.b_rep[:, half * 512:(half + 1) * 512], op=ALU.add))
                    fv = p.inc("v", p.s_ve, 1)
                    p.pool_release(b, p.s_ve, fv)
                ffn_done = p.s_ve.n
                self.x2f_war = p.s_pe.n
                ap3 = self.emit_ln_chunk(tb, 12 + ns, li[:, :], lo[:, :],
                                         waits_v=[(p.s_ve, ffn_done),
                                                  (p.s_gp, self.lnout_war)],
                                         waits_s=[(p.s_ve, ffn_done)])
                self.lnin_war = ap3
                p.wait("g", *ap3)
                p.g(lambda e, ns=ns, lo=lo, tb=tb: e.tensor_tensor(
                    out=p.x_tm[tb][:, ns * D:(ns + 1) * D],
                    in0=p.x_tm[tb][:, ns * D:(ns + 1) * D], in1=lo[:, :],
                    op=ALU.add))
                self.lnout_war = p.inc("g", p.s_gp, 1)
            return run

        def mk_out(s2, t2, tb):
            def run():
                out_ready = p.s_gp.n
                p.wait("s", p.s_gp, out_ready)
                srcv = p.x_tm[tb][:, :].rearrange("p (c f) -> p c f", f=D)
                dst = p.out_h.ap()[s2, t2 * TIL:(t2 + 1) * TIL, :].rearrange(
                    "(c p) f -> p c f", p=128)
                p.s(lambda e, d=dst, s3=srcv: e.dma_start(out=d, in_=s3))
                stv = p.inc("s", p.s_o[tb], 16)
                p.x_rel[tb] = [(p.s_o[tb], stv)]
            return run

        for ns in range(NCH):
            self.carry_q.append(mk_e2(ns, ap2s[ns]))
            self.carry_q.append(mk_e3(ns, tb, firstb))
        self.carry_q.append(mk_out(s, t, tb))
        return pro_next

    def schedule(self):
        p = self
        self.sched_loads()
        p.wait("s", *self.consts_done)
        if self.phases == "a0":
            xfm = self.prologue_tile(0, 0, first=True)
            self.phase_a_tile(0, 0, xfm, None)
            self._finish()
            return
        xfm = None
        for s in range(BPC):
            xfm = self.phase_a_seq(s, xfm)
        if self.phases == "a":
            self._finish()
            return
        self.sched_collective()
        self.sched_wprime()
        a_end = p.s_pe.n
        self.prj_war = a_end
        self.xfm_war = a_end
        if self.phases == "ac":
            self._finish()
            return
        p.wait("g", p.s_pe, a_end)
        for dst, nm, sem in ((self.wb0, "wqT", p.s_wq2), (self.wb1, "woT", p.s_wo2)):
            srcw = self.dt[nm].ap().rearrange("(c p) o -> p c o", p=128)
            dstw = dst[:, :].rearrange("p (c o) -> p c o", o=D)
            p.g(lambda e, d=dstw, s2=srcw: e.dma_start(out=d, in_=s2))
            p.inc("g", sem, 16)
        self.wq_done = (p.s_wq2, p.s_wq2.n)
        self.wo_done = (p.s_wo2, p.s_wo2.n)
        for s in range(BPC):
            self.sched_bsetup1(s)
        pro = self.pro_b(0, 0, firstb=True)
        for s in range(BPC):
            self.sched_bsetup2(s)
        self.np_war = [0, 0, self.psetup_read]
        self.xln2b_war = [(p.s_ve, self.wp_done)] * NCH
        if self.phases == "w":
            self._finish()
            return
        tiles = [(s, t) for s in range(BPC) for t in range(NT)]
        for i, (s, t) in enumerate(tiles):
            if i + 1 < len(tiles):
                nxt = (*tiles[i + 1], tiles[i + 2] if i + 2 < len(tiles) else None)
            else:
                nxt = None
            pro = self.phase_b_tile(s, t, pro, nxt)
        self._run_carry(99)
        self._finish()

    def _finish(self):
        p = self
        if self.debug:
            p.wait("g", p.s_ve, p.s_ve.n)
            p.wait("g", p.s_sc, p.s_sc.n)
            p.wait("g", p.s_pe, p.s_pe.n)
            p.wait("g", p.s_gp, p.s_gp.n)
            order = ["a0", "a", "ac", "w", "b0", "b1", "full"]
            lvl = order.index(self.phases)
            taps = [("tap_prj", self.prj_fm), ("tap_vs0", self.vs_row[0])]
            if lvl >= 1:
                taps += [("tap_c1e0", self.c1e[0][0]), ("tap_c1e1", self.c1e[0][1])]
            if lvl >= 2:
                taps += [("tap_stab", self.stab_in), ("tap_eps", self.epsp)]
            if lvl >= 3:
                taps += [("tap_k2rep", self.k2rep[0]), ("tap_v12", self.v12t[0])]
            for nm, src in taps:
                p.g(lambda e, nm=nm, src=src: e.dma_start(
                    out=self.tap_h[nm].ap(), in_=src[:, :]))
                p.inc("g", p.s_tap, 16)
        p.wait("g", p.s_o[0], p.s_o[0].n)
        p.wait("g", p.s_o[1], p.s_o[1].n)
        p.wait("g", p.s_tap, p.s_tap.n)


def _projz(proj):
    z = np.zeros((128, 512), dtype=np.float32)
    z[0:64, 0:256] = proj.T
    z[64:128, 256:512] = proj.T
    return z


def _projzq(proj):
    z = np.zeros((128, 512), dtype=np.float32)
    z[0:64, 0:256] = proj.T
    z[64:128, 256:512] = proj.T
    return z


def make_inputs(inputs):
    f32 = np.float32
    x = np.ascontiguousarray(np.asarray(inputs["L0_attn_hidden_states"], dtype=f32))
    wq = np.asarray(inputs["L0_attn_Wq"], dtype=f32)
    wk = np.asarray(inputs["L0_attn_Wk"], dtype=f32)
    wv = np.asarray(inputs["L0_attn_Wv"], dtype=f32)
    wo = np.asarray(inputs["L0_attn_Wo"], dtype=f32)
    bq = np.asarray(inputs["L0_attn_bq"], dtype=f32)
    bk = np.asarray(inputs["L0_attn_bk"], dtype=f32)
    bv = np.asarray(inputs["L0_attn_bv"], dtype=f32)
    bo = np.asarray(inputs["L0_attn_bo"], dtype=f32)
    proj = np.asarray(inputs["L0_attn_proj"], dtype=f32)
    W = np.asarray(inputs["L0_ffn_W"], dtype=f32)
    bffn = np.asarray(inputs["L0_ffn_b"], dtype=f32)
    Vr = np.asarray(inputs["L0_ffn_V_r"], dtype=f32)
    C = np.asarray(inputs["L0_ffn_C"], dtype=f32)

    common = {
        "wqT": np.ascontiguousarray(wq.T), "wkT": np.ascontiguousarray(wk.T),
        "wvT": np.ascontiguousarray(wv.T), "woT": np.ascontiguousarray(wo.T),
        "wT": np.ascontiguousarray(W.T),
        "vrT": np.ascontiguousarray(Vr.T), "cT": np.ascontiguousarray(C.T),
        "projz": _projz(proj), "projzq": _projzq(proj),
        "identf": np.eye(128, dtype=f32),
        "identb": np.eye(128, dtype=f32),
        "cvals": np.ascontiguousarray(np.broadcast_to(
            np.array([EPS_LN, -LN16, LNEPSQ], dtype=f32), (128, 3))),
        "ones": np.ones((128, 128), dtype=f32),
        "bo_rep": np.ascontiguousarray(np.broadcast_to(bo, (128, D))),
        "b_rep": np.ascontiguousarray(np.broadcast_to(bffn, (128, D))),
        "bv_rep": np.ascontiguousarray(np.broadcast_to(bv, (128, D))),
    }
    bqkv = np.zeros((128, 3 * OC), dtype=f32)
    for i, bb in enumerate((bq, bk, bv)):
        bqkv[:, i * OC:(i + 1) * OC] = bb.reshape(OC, 128).T
    common["bqkv"] = bqkv
    ediag = np.zeros((128, 128), dtype=f32)
    for c in range(OC):
        ediag[0:64, c * 16 + 2 * c] = -0.5
        ediag[64:128, c * 16 + 2 * c + 1] = -0.5
    common["ediag"] = ediag

    in_maps = []
    for core in range(NCORES):
        m = dict(common)
        m["x"] = np.ascontiguousarray(x[core * BPC:(core + 1) * BPC])
        in_maps.append(m)
    return in_maps


_CACHE = {}


def get_nc(debug=False):
    key = ("nc", debug)
    if key not in _CACHE:
        b = Builder(debug=debug)
        b.emit()
        _CACHE[key] = b.nc
    return _CACHE[key]


def kernel(**inputs):
    in_maps = make_inputs(inputs)
    nc = get_nc()
    res = run_bass_kernel_spmd(nc, in_maps, core_ids=list(range(NCORES)))
    return np.concatenate([np.asarray(res.results[c]["out"]) for c in range(NCORES)],
                          axis=0)
